# version 18
# baseline (speedup 1.0000x reference)
import numpy as np
from contextlib import ExitStack
import concourse.bass as bass
import concourse.mybir as mybir
from concourse.dve_op import DveOp
from concourse.dve_spec import Spec, Src0, Src1, C0, C1, C2, Latch, lower as _dve_lower
import concourse.dve_ops as _dve_ops_mod

F32 = mybir.dt.float32
BF16 = mybir.dt.bfloat16
AF = mybir.ActivationFunctionType
ALU = mybir.AluOpType
P = 128

ENGINES = ("sync", "gpsimd", "tensor", "scalar", "vector")

QTH = (0.25 / 6, 0.75 / 6, 1.25 / 6, 1.75 / 6, 2.5 / 6, 3.5 / 6, 5.0 / 6)
QW = (0.5, 0.5, 0.5, 0.5, 1.0, 1.0, 2.0)

_FP4OPS = {}


def _register_fp4_ops():
    if _FP4OPS:
        return _FP4OPS
    b1 = (Src0 > C0) + (Src0 > C1) + (Src0 > C2) + (Src0 > Latch(Src1))
    _c5 = Src0 > C0
    _c6 = Src0 > C1
    _c7 = Src0 > C2
    _e = _c5 + _c6 + _c7 + _c7
    b2 = Src1 + (_e + _e)

    def ref1(in0, in1, s0, s1, imm2):
        f = np.result_type(in0.dtype, np.float32)
        return ((in0 > s0).astype(f) + (in0 > s1).astype(f)
                + (in0 > imm2).astype(f) + (in0 > in1).astype(f))

    def ref2(in0, in1, s0, s1, imm2):
        f = np.result_type(in0.dtype, np.float32)
        return in1 + 2 * ((in0 > s0).astype(f) + (in0 > s1).astype(f)) + 4 * (in0 > imm2).astype(f)

    for nm, body, ref in (("FP4Q1", b1, ref1), ("FP4Q2", b2, ref2)):
        shas = {}
        for ver in ("v3", "v4"):
            try:
                shas[ver] = _dve_lower(Spec(body=body), ver=ver, perf_en=False, name=nm).sha(ver)
            except Exception:
                pass
        op = DveOp(nm, Spec(body=body), subdim=False, uops_sha=shas, reference=ref)
        _dve_ops_mod.OPS.append(op)
        _dve_ops_mod._SUB_OPCODE_FOR_NAME[nm] = max(_dve_ops_mod._SUB_OPCODE_FOR_NAME.values()) + 1
        _dve_ops_mod.CUSTOM_DVE_REFS[nm] = op.ref
        _FP4OPS[nm] = op
    assert max(_dve_ops_mod._SUB_OPCODE_FOR_NAME.values()) < 0x20
    return _FP4OPS


class SemC:
    def __init__(self, h):
        self.h = h
        self.n = 0


class Prog:
    def __init__(self, nc, es):
        self.nc = nc
        self.es = es
        self.ops = {e: [] for e in ENGINES}

    def sem(self, name):
        return SemC(self.es.enter_context(self.nc.semaphore(name)))

    def emit(self, eng, fn, waits=(), incs=()):
        waits = tuple((sm, v) for sm, v in waits if v > 0)
        for sm, k in incs:
            sm.n += k
        self.ops[eng].append((fn, waits, tuple(incs)))

    def barrier(self, bar):
        tgt = bar.n + len(ENGINES)
        for e in ENGINES:
            self.emit(e, lambda en: en.nop(), incs=((bar, 1),))
        for e in ENGINES:
            self.emit(e, None, waits=((bar, tgt),))


class DmaSlots:
    def __init__(self, pr, name, nslots):
        self.s = [pr.sem(f"{name}_{i}") for i in range(nslots)]
        self.u = [0] * nslots

    def issue(self, slot):
        self.u[slot] += 1
        return self.s[slot], 16 * self.u[slot]


def build(cfg, debug=False, stop_after=None, n_cores=1):
    T = cfg["T"]; D = cfg["D"]; E = cfg["E"]; N = cfg["N"]; R = cfg["R"]; I = cfg["I"]
    WQ_SHARD = n_cores > 1
    if WQ_SHARD:
        assert I % n_cores == 0 and D % n_cores == 0
        MYI = I // n_cores
        MYD = D // n_cores
    DT = D // P
    ET = E // P
    IT = I // P
    TS = T // P
    TCH = min(512, T)
    NTC = T // TCH
    GE = min(2, ET)
    NG = ET // GE
    NDH = 2 if D > 512 else 1
    DW = min(512, D)
    CW = min(256, T)
    TS2 = CW // P
    NCH = T // CW
    EPS_RMS = 1e-6
    EPS_LN = 1e-5
    ncols = []
    off = 0
    while off < I:
        w = min(512, I - off)
        ncols.append((off, w))
        off += w
    NNC = len(ncols)

    nc = bass.Bass("TRN2", target_bir_lowering=False, debug=False,
                   detect_race_conditions=False)
    es = ExitStack()
    pr = Prog(nc, es)
    io = "ExternalOutput" if debug else "Internal"

    dr = {}
    def dram(name, shape, dt, kind, addr_space="Local"):
        dr[name] = nc.dram_tensor(name, list(shape), dt, kind=kind,
                                  addr_space=addr_space).ap()
        return dr[name]

    x_in = dram("x", (T, D), F32, "ExternalInput")
    w_in = dram("W_in", (DT * 2 * ET * P, P), F32, "ExternalInput")
    convw = dram("convw", (P, 4 * (E // P)), F32, "ExternalInput")
    w_x = dram("W_x", (P, (E // P) * 96), F32, "ExternalInput")
    wdt_e = dram("Wdt_ext", (R + 1, E), F32, "ExternalInput")
    alog = dram("A_log", (P, (E // P) * N), F32, "ExternalInput")
    dpar = dram("D_param", (P, E // P), F32, "ExternalInput")
    nw_d = dram("norm_w", (1, D), F32, "ExternalInput")
    wg_d = dram("wg", (MYI if WQ_SHARD else I, D), F32, "ExternalInput")
    wu_d = dram("wu", (MYI if WQ_SHARD else I, D), F32, "ExternalInput")
    wd_d = dram("wd", (MYD if WQ_SHARD else D, I), F32, "ExternalInput")
    idn = dram("ident", (P, P), BF16, "ExternalInput")
    wout_d = dram("W_out", (E, D), F32, "ExternalInput")
    out_d = dram("out", (T, D), F32, "ExternalOutput")

    sc_rows = dram("sc_rows", (32, T), BF16, io)
    sc_u = dram("sc_u", (E, T), BF16, io)
    sc_sz = dram("sc_sz", (E, T), BF16, io)
    sc_g = dram("sc_g", (E, T), BF16, io)
    sc_x2 = dram("sc_x2", (T, D), F32, io)
    sc_xqT = dram("sc_xqT", (D, T), BF16, io)
    sc_hqT = dram("sc_hqT", (I, T), BF16, io)
    wq_space = "Shared" if WQ_SHARD else "Local"
    sc_wgq = dram("sc_wgq", (I, D), BF16, io, addr_space=wq_space)
    sc_wuq = dram("sc_wuq", (I, D), BF16, io, addr_space=wq_space)
    sc_wdq = dram("sc_wdq", (D, I), BF16, io, addr_space=wq_space)
    if WQ_SHARD:
        sc_wgq_my = dram("sc_wgq_my", (MYI, D), BF16, "Internal")
        sc_wuq_my = dram("sc_wuq_my", (MYI, D), BF16, "Internal")
        sc_wdq_my = dram("sc_wdq_my", (MYD, I), BF16, "Internal")
    if debug:
        dram("dbg_q_ab", (P, D), F32, "ExternalOutput")
        dram("dbg_q_m1", (P, D // 2), F32, "ExternalOutput")
        dram("dbg_q_m2", (P, D // 4), F32, "ExternalOutput")
        dram("dbg_q_m3", (P, D // 8), F32, "ExternalOutput")
        dram("dbg_wgq", (P, I), BF16, "ExternalOutput")
        dram("dbg_qo", (P, D), BF16, "ExternalOutput")
        dram("dbg_hq", (P, I), BF16, "ExternalOutput")
        dram("dbg_q_mx", (P, D // 16), F32, "ExternalOutput")
        dram("dbg_q_v", (P, D), F32, "ExternalOutput")
        dram("dbg_q_qa", (P, D), BF16, "ExternalOutput")
        dram("dbg_delta", (E, T), BF16, "ExternalOutput")
        dram("dbg_y", (E, T), F32, "ExternalOutput")
        dram("dbg_dtlr", (R + 1, T), BF16, "ExternalOutput")

    def sb(name, shape, dt):
        return es.enter_context(nc.sbuf_tensor(name, list(shape), dt))

    AR_BYTES = 114688
    ar = sb("ar", (P, AR_BYTES // 2), BF16)

    def arb(off, n):
        assert off % 2 == 0 and off + 2 * n <= AR_BYTES, (off, n)
        return ar[:, off // 2: off // 2 + n]

    def arf(off, n):
        assert off % 4 == 0 and off + 4 * n <= AR_BYTES, (off, n)
        return ar[:, off // 2: off // 2 + 2 * n].bitcast(F32)

    def aln(v):
        return (v + 15) // 16 * 16

    o_xnT = 0
    o_acc1 = aln(o_xnT + DT * T * 2)
    o_acc2 = aln(o_acc1 + T * 4)
    o_et = aln(o_acc2 + T * 4)
    o_xin = aln(o_et + T * 4)
    o_ubuf = aln(o_xin + (T + 4) * 4)
    o_szb1 = aln(o_ubuf + 2 * T * 2)
    o_xnw = aln(o_szb1 + 2 * T * 2)
    end_p1 = o_xnw + 2 * D * 2
    o_dl = 0
    o_du = o_dl + GE * T * 2
    o_y = o_du + GE * T * 2
    o_ug = o_y + GE * T * 4
    o_dA = o_ug + GE * T * 2
    o_b = o_dA + 2 * T * 4
    o_h = o_b + 2 * T * 2
    o_bcB = o_h + 2 * T * 2
    o_bcC = o_bcB + 2 * T * 2
    o_szb2 = o_b
    end_p2 = o_bcC + 2 * T * 2

    def qblock(base, W):
        q = {}
        for nm, wdt, esz in (("absx", W, 4), ("v", W, 4), ("qa", W, 2), ("qb", W, 2),
                             ("sgn", W, 2), ("mx", W // 16, 4), ("mxc", W // 16, 4),
                             ("rs", W // 16, 4), ("sc", W // 16, 4)):
            q[nm] = (base, wdt)
            base = aln(base + wdt * esz)
        q["end"] = base
        return q

    o_wo = 0
    o_rt = aln(o_wo + ET * D * 2)
    o_gld = aln(o_rt + D * 4)
    o_q3 = aln(o_gld + 2 * ET * CW * 2)
    q3 = qblock(o_q3, D)
    o_x2b = aln(q3["end"])
    o_x2q = aln(o_x2b + D * 4)
    end_p3 = o_x2q + D * 2
    o_wgq = 0
    o_wuq = o_wgq + DT * I * 2
    o_hq = aln(o_wuq + DT * I * 2)
    o_hqT = aln(o_hq + I * 2)
    o_qf = aln(o_hqT + IT * P * 2)
    QFW = min(512, I)
    qf = qblock(o_qf, QFW)
    end_pfa = qf["end"]
    qb = qblock(0, I)
    o_outt = 0
    o_rt2 = aln(o_outt + 2 * D * 4)
    o_x2r = aln(o_rt2 + D * 4)
    o_wdq = aln(max(qb["end"], o_x2r + 2 * D * 4))
    o_hqr = aln(o_wdq + IT * D * 2)
    end_pfb = o_hqr + 2 * I * 2
    assert max(end_p1, end_p2, end_p3, end_pfa, end_pfb) <= AR_BYTES, (
        end_p1, end_p2, end_p3, end_pfa, end_pfb)

    ident = sb("ident_s", (P, P), BF16)
    nwbc = sb("nwbc", (P, D), F32)
    cwc = sb("cwc", (P, 4 * ET), F32)
    A_t = sb("A_t", (P, ET * N), F32)
    A_s = sb("A_s", (P, ET * N), F32)
    D_s = sb("D_s", (P, ET), F32)
    Wx_s = sb("Wx_s", (P, ET * 96), BF16)
    Wdt_s = sb("Wdt_s", (R + 1, E), BF16)
    dtlr_s = sb("dtlr_s", (R + 1, T), BF16)
    bcsb = sb("bcsb", (96, T), BF16)
    cols = sb("cols", (P, 32), F32)
    cst = sb("cst", (P, 8), F32)
    WSW = max(D, I, 1024) + 512
    wst = [sb(f"wst{i}", (P, WSW), F32) for i in range(2)]
    wbf = [sb(f"wbf{i}", (P, P), BF16) for i in range(4)]
    xt = [sb(f"xt{i}", (P, D), F32) for i in range(2)]
    scr = sb("scr", (P, max(D, I)), F32)
    evt = [sb(f"evt{i}", (P, P), BF16) for i in range(2)]
    xqt = [sb(f"xqt{i}", (P, DT * P), BF16) for i in range(2)]
    qo = sb("qo", (P, 2 * max(D, I, T, 1024)), BF16)

    ps = [es.enter_context(nc.psum_tensor(f"ps{i}", (P, 512), F32)) for i in range(8)]
    def psv(i, w=512):
        return ps[i][:, 0:w]
    def psb(i):
        return ps[i][:, 0:64].bitcast(BF16)

    s = {k: pr.sem(k) for k in (
        "bar", "sW", "sWc", "sXf", "sRst", "sXnw", "sTr", "sEvD", "sWbf",
        "sWuse", "sPsX", "sEvX", "sConv", "sSig", "sU", "sWxU", "sSZp",
        "sRows", "sRowsD", "sDpre", "sSPa", "sSPr", "sDel", "sDelV", "sDu",
        "sDA", "sScan", "sYm", "sGate", "sWo", "sMMo", "sStA", "sStB",
        "sX2", "sX2sp", "sQd", "sQo", "sQs", "sTr3", "sEv3", "sWq",
        "sWqh", "sWqT", "sMMg", "sSigF", "sHq", "sTrH", "sEvH", "sHsp",
        "sMMd", "sRes", "sLs1", "sLs2", "sLs3", "sDbg", "sCol", "sBg")}

    MMKW = dict(skip_group_check=True)

    rX = DmaSlots(pr, "rX", 2)
    rW = DmaSlots(pr, "rW", 2)
    rUg = DmaSlots(pr, "rUg", GE)
    rBB = DmaSlots(pr, "rBB", 2)
    rCC = DmaSlots(pr, "rCC", 2)
    rSZ = DmaSlots(pr, "rSZ", 2)
    rGld = DmaSlots(pr, "rGld", 2)
    rXq = DmaSlots(pr, "rXq", 2)
    rHr = DmaSlots(pr, "rHr", 2)
    rX2r = DmaSlots(pr, "rX2r", 2)
    rUsp = DmaSlots(pr, "rUsp", 2)
    rSZsp = DmaSlots(pr, "rSZsp", 2)
    rGsp = DmaSlots(pr, "rGsp", GE)
    rXsp = DmaSlots(pr, "rXsp", 2)
    rOut = DmaSlots(pr, "rOut", 2)
    rWQ = DmaSlots(pr, "rWQ", 2)
    rWQsp = DmaSlots(pr, "rWQsp", 2)
    rWQ2 = DmaSlots(pr, "rWQ2", 2)

    def p0():
        loads = [
            (nwbc[:], nw_d.broadcast_to((P, D))),
            (cwc[:], convw),
            (A_t[:], alog),
            (D_s[:], dpar),
            (ident[:], idn),
            (wst[0][:, :ET * 96], w_x),
            (wst[1][:R + 1, :E], wdt_e),
        ]
        for dst, src in loads:
            pr.emit("sync", lambda e, d=dst, sr=src: e.dma_start(out=d, in_=sr),
                    incs=((s["sW"], 16),))
        w0 = s["sW"].n
        pr.emit("scalar", lambda e: e.activation(A_s[:], A_t[:], AF.Exp),
                waits=((s["sW"], w0),), incs=((s["sWc"], 1),))
        pr.emit("vector", lambda e: e.tensor_scalar_mul(A_t[:], A_s[:], -1.0),
                waits=((s["sWc"], s["sWc"].n),), incs=((s["sWc"], 1),))
        src = wst[0][:, :ET * 96]
        pr.emit("vector", lambda e, sr=src: e.tensor_copy(Wx_s[:], sr),
                waits=((s["sW"], w0),), incs=((s["sWc"], 1),))
        src = wst[1][:R + 1, :E]
        pr.emit("vector", lambda e, sr=src: e.tensor_copy(Wdt_s[:], sr),
                waits=((s["sW"], w0),), incs=((s["sWc"], 1),))
        pr.emit("vector", lambda e: e.memset(dtlr_s[R:R + 1, :], 1.0),
                incs=((s["sWc"], 1),))
        pr.emit("vector", lambda e, a=arf(o_xin, 4): e.memset(a, 0.0),
                incs=((s["sWc"], 1),))
        for ci, val in ((0, EPS_RMS), (1, EPS_LN), (2, 1.0 / D), (3, -0.5), (4, -1.0), (5, 1.0), (6, 0.5), (7, QTH[3])):
            pr.emit("vector", lambda e, a=cst[:, ci:ci + 1], v=val: e.memset(a, v),
                    incs=((s["sWc"], 1),))

    def xnT_tile(dt):
        return arb(o_xnT + dt * T * 2, T)

    def p1a():
        for ts in range(TS):
            sl = ts % 2
            src = x_in[ts * P:(ts + 1) * P, :]
            smx, vx = rX.issue(sl)
            pr.emit("sync", lambda e, d=xt[sl][:], sr=src: e.dma_start(out=d, in_=sr),
                    waits=((s["sXf"], ts - 1),), incs=((smx, 16),))
            c_ssq = cols[:, sl:sl + 1]
            pr.emit("scalar", lambda e, xi=xt[sl][:], c=c_ssq: e.activation(
                scr[:, :D], xi, AF.Square, accum_out=c),
                    waits=((smx, vx),))
            c_l = cols[:, 4 + sl:5 + sl]
            c_r = cols[:, 6 + sl:7 + sl]
            pr.emit("scalar", lambda e, a=c_l, b=c_ssq: e.activation(
                a, b, AF.Ln, bias=cst[:, 0:1], scale=cst[:, 2:3]),
                    waits=((s["sXf"], ts - 1),))
            pr.emit("scalar", lambda e, a=c_r, b=c_l: e.activation(a, b, AF.Exp, scale=cst[:, 3:4]),
                    incs=((s["sRst"], 1),))
            tmpf = arf(o_acc1, D)
            pr.emit("vector", lambda e, xi=xt[sl][:], c=c_r, t_=tmpf: e.tensor_scalar_mul(t_, xi, c),
                    waits=((s["sRst"], ts + 1),), incs=((s["sXf"], 1),))
            xnw = arb(o_xnw + sl * D * 2, D)
            pr.emit("vector", lambda e, t_=tmpf, o=xnw: e.tensor_mul(o, t_, nwbc[:]),
                    waits=((s["sEvD"], (ts - 1) * DT),), incs=((s["sXnw"], 1),))
            for dt in range(DT):
                pp = dt % 2
                src_t = arb(o_xnw + sl * D * 2 + dt * P * 2, P)
                pr.emit("tensor", lambda e, o=psb(4 + pp), i=src_t: e.transpose(o, i, ident[:]),
                        waits=((s["sXnw"], ts + 1), (s["sEvD"], s["sTr"].n - 1)),
                        incs=((s["sTr"], 1),))
                dst_t = xnT_tile(dt)[:, ts * P:(ts + 1) * P]
                pr.emit("vector", lambda e, o=dst_t, i=psb(4 + pp): e.tensor_copy(o, i),
                        waits=((s["sTr"], s["sTr"].n),), incs=((s["sEvD"], 1),))

    def p1b_half(half):
        def kloop(et):
            ib = half * ET + et
            for k in range(DT):
                li = ib * DT + k
                wsl = li % 4
                src = w_in[(k * 2 * ET + ib) * P:(k * 2 * ET + ib + 1) * P, :]
                smw, vw = rW.issue(li % 2)
                pr.emit("sync", lambda e, d=wst[li % 2][:, :P], sr=src: e.dma_start(out=d, in_=sr),
                        waits=((s["sWbf"], li - 1),), incs=((smw, 16),))
                pr.emit("vector", lambda e, o=wbf[wsl][:], i=wst[li % 2][:, :P]: e.tensor_copy(o, i),
                        waits=((smw, vw), (s["sWuse"], li - 3)),
                        incs=((s["sWbf"], 1),))
                for tc in range(NTC):
                    incs = []
                    if k == DT - 1:
                        incs.append((s["sPsX"], 1))
                    rhs = xnT_tile(k)[:, tc * TCH:(tc + 1) * TCH]
                    pr.emit("tensor", lambda e, o=psv(tc, TCH), l_=wbf[wsl][:], r_=rhs,
                            st=(k == 0), sp=(k == DT - 1): e.matmul(o, l_, r_, start=st, stop=sp, **MMKW),
                            waits=((s["sWbf"], s["sWbf"].n),
                                   (s["sEvX"], (ib - 1) * NTC + tc + 1)),
                            incs=tuple(incs))
                pr.emit("tensor", lambda e: e.nop(), incs=((s["sWuse"], 1),))

        def tail(et):
            ib = half * ET + et
            for tc in range(NTC):
                if half == 0:
                    dst = arf(o_xin + 16 + tc * TCH * 4, TCH)
                else:
                    dst = arf(o_acc1 + tc * TCH * 4, TCH)
                waits = [(s["sPsX"], ib * NTC + tc + 1)]
                if half == 1 and et == 0 and tc == 0:
                    waits.append((s["sU"], ET))
                if half == 1 and et > 0 and tc == 0:
                    waits.append((s["sSZp"], et))
                if half == 0 and et > 0 and tc == 0:
                    waits.append((s["sConv"], et))
                pr.emit("scalar", lambda e, d=dst, i=psv(tc, TCH): e.copy(d, i),
                        waits=tuple(waits), incs=((s["sEvX"], 1),))
            if half == 0:
                xin = lambda j: arf(o_xin + j * 4, T)
                wc = lambda j: cwc[:, et * 4 + j: et * 4 + j + 1]
                pr.emit("vector", lambda e, o=arf(o_acc1, T), i=xin(1), w=wc(0): e.tensor_scalar_mul(o, i, w),
                        waits=((s["sEvX"], (ib + 1) * NTC),))
                pr.emit("vector", lambda e, o=arf(o_acc2, T), i=xin(2), w=wc(1),
                        a=arf(o_acc1, T): e.scalar_tensor_tensor(o, i, w, a, ALU.mult, ALU.add))
                pr.emit("vector", lambda e, o=arf(o_acc1, T), i=xin(3), w=wc(2),
                        a=arf(o_acc2, T): e.scalar_tensor_tensor(o, i, w, a, ALU.mult, ALU.add))
                pr.emit("vector", lambda e, o=arf(o_acc2, T), i=xin(4), w=wc(3),
                        a=arf(o_acc1, T): e.scalar_tensor_tensor(o, i, w, a, ALU.mult, ALU.add),
                        incs=((s["sConv"], 1),))
                pr.emit("scalar", lambda e, o=arf(o_et, T), i=arf(o_acc2, T): e.activation(
                    o, i, AF.Tanh, scale=cst[:, 6:7]),
                        waits=((s["sConv"], s["sConv"].n),), incs=((s["sSig"], 1),))
                pr.emit("vector", lambda e, o=arf(o_acc1, T), i=arf(o_et, T): e.tensor_scalar(
                    o, i, 0.5, 0.5, ALU.mult, ALU.add),
                        waits=((s["sSig"], s["sSig"].n),))
                ub = arb(o_ubuf + (et % 2) * T * 2, T)
                pr.emit("vector", lambda e, o=ub, a=arf(o_acc2, T), b=arf(o_acc1, T): e.tensor_mul(o, a, b),
                        waits=((s["sWxU"], et - 1), (rUsp.s[et % 2], 16 * rUsp.u[et % 2])),
                        incs=((s["sU"], 1),))
                for tc in range(NTC):
                    rhs = ub[:, tc * TCH:(tc + 1) * TCH]
                    pr.emit("tensor", lambda e, o=ps[4 + tc][0:96, 0:TCH],
                            l_=Wx_s[:, et * 96:(et + 1) * 96], r_=rhs,
                            st=(et == 0), sp=(et == ET - 1): e.matmul(o, l_, r_, start=st, stop=sp, **MMKW),
                            waits=((s["sU"], et + 1),),
                            incs=((s["sWxU"], 1),) if tc == NTC - 1 else ())
                smu, _ = rUsp.issue(et % 2)
                pr.emit("gpsimd", lambda e, o=sc_u[et * P:(et + 1) * P, :], i=ub: e.dma_start(out=o, in_=i),
                        waits=((s["sU"], et + 1),), incs=((smu, 16),))
                emit_wq_tasks(2)
            elif True:
                pr.emit("scalar", lambda e, o=arf(o_et, T), i=arf(o_acc1, T): e.activation(
                    o, i, AF.Tanh, scale=cst[:, 6:7]),
                        waits=((s["sEvX"], (ib + 1) * NTC),), incs=((s["sSig"], 1),))
                pr.emit("vector", lambda e, o=arf(o_acc2, T), i=arf(o_et, T): e.tensor_scalar(
                    o, i, 0.5, 0.5, ALU.mult, ALU.add),
                        waits=((s["sSig"], s["sSig"].n),))
                szb = arb(o_szb1 + (et % 2) * T * 2, T)
                pr.emit("vector", lambda e, o=szb, a=arf(o_acc1, T), b=arf(o_acc2, T): e.tensor_mul(o, a, b),
                        waits=((rSZsp.s[et % 2], 16 * rSZsp.u[et % 2]),), incs=((s["sSZp"], 1),))
                smz, _ = rSZsp.issue(et % 2)
                pr.emit("gpsimd", lambda e, o=sc_sz[et * P:(et + 1) * P, :], i=szb: e.dma_start(out=o, in_=i),
                        waits=((s["sSZp"], s["sSZp"].n),), incs=((smz, 16),))
            emit_wq_tasks(2)

        for et in range(ET):
            kloop(et)
            if et > 0:
                tail(et - 1)
        tail(ET - 1)

    def p1c():
        for tc in range(NTC):
            pr.emit("scalar", lambda e, o=dtlr_s[0:R, tc * TCH:(tc + 1) * TCH],
                    i=ps[4 + tc][0:R, 0:TCH]: e.copy(o, i),
                    waits=((s["sWxU"], ET),))
            pr.emit("scalar", lambda e, o=bcsb[64:96, tc * TCH:(tc + 1) * TCH],
                    i=ps[4 + tc][64:96, 0:TCH]: e.copy(o, i),
                    incs=((s["sRows"], 1),))
        pr.emit("gpsimd", lambda e: e.dma_start(out=sc_rows[:], in_=bcsb[64:96, :]),
                waits=((s["sRows"], s["sRows"].n),), incs=((s["sRowsD"], 16),))
        emit_wq_collectives()
        if debug:
            pr.emit("gpsimd", lambda e: e.dma_start(out=dr["dbg_dtlr"][:], in_=dtlr_s[:]),
                    waits=((s["sRows"], NTC),), incs=((s["sDbg"], 16),))

    def p2(until=None):
        ROWS_RDY = NTC

        def dl_t(eh): return arb(o_dl + eh * T * 2, T)
        def du_t(eh): return arb(o_du + eh * T * 2, T)
        def y_e(eh): return arb(o_y + eh * T * 4, T)
        def y_o(eh): return arb(o_y + eh * T * 4 + T * 2, T)
        def ug_t(eh): return arb(o_ug + eh * T * 2, T)
        def dA_t(r2): return arf(o_dA + r2 * T * 4, T)
        def b_t(r2): return arb(o_b + r2 * T * 2, T)
        def h_t(r2): return arb(o_h + r2 * T * 2, T)
        def bcB(r2): return arb(o_bcB + r2 * T * 2, T)
        def bcC(r2): return arb(o_bcC + r2 * T * 2, T)

        for g in range(NG):
            if until == "p2a0":
                return
            for eh in range(GE):
                et = g * GE + eh
                smug, _ = rUg.issue(eh)
                pr.emit("sync", lambda e, o=ug_t(eh), i=sc_u[et * P:(et + 1) * P, :]: e.dma_start(out=o, in_=i),
                        waits=((rGsp.s[eh], 16 * g),
                               (rUsp.s[0], 16 * ((ET + 1) // 2)), (rUsp.s[1], 16 * (ET // 2))),
                        incs=((smug, 16),))
            if until == "p2a1":
                return
            if until == "p2b0":
                for tc in range(NTC):
                    rhs = dtlr_s[:, tc * TCH:(tc + 1) * TCH]
                    pr.emit("tensor", lambda e, o=psv(tc, TCH),
                            l_=Wdt_s[:, 0:P], r_=rhs: e.matmul(
                        o, l_, r_, start=True, stop=True, **MMKW),
                            waits=((s["sRows"], ROWS_RDY), (s["sRowsD"], 16)),
                            incs=((s["sDpre"], 1),))
                return
            for eh in range(GE):
                et = g * GE + eh
                j = g * GE + eh
                for tc in range(NTC):
                    rhs = dtlr_s[:, tc * TCH:(tc + 1) * TCH]
                    free_v = (j - 1) * NTC + tc + 1
                    pr.emit("tensor", lambda e, o=psv(tc, TCH),
                            l_=Wdt_s[:, et * P:(et + 1) * P], r_=rhs: e.matmul(
                        o, l_, r_, start=True, stop=True, **MMKW),
                            waits=((s["sRows"], ROWS_RDY), (s["sRowsD"], 16),
                                   (s["sSPa"], free_v), (s["sSPr"], free_v)),
                            incs=((s["sDpre"], 1),))
                for tc in range(NTC):
                    pw = j * NTC + tc + 1
                    spA = wst[0][:, tc * TCH:(tc + 1) * TCH]
                    spB = wst[1][:, tc * TCH:(tc + 1) * TCH]
                    spP = scr[:, tc * TCH:(tc + 1) * TCH]
                    rlb = qo[:, tc * TCH:(tc + 1) * TCH]
                    wq_guard = ((rWQsp.s[0], 16 * rWQsp.u[0]),
                                (rWQsp.s[1], 16 * rWQsp.u[1])) if (g == 0 and j == 0 and tc == 0) else ()
                    pr.emit("scalar", lambda e, o=spP, i=psv(tc, TCH): e.copy(o, i),
                            waits=((s["sDpre"], pw), (s["sSPr"], (j - 1) * NTC + tc + 1),
                                   (s["sDelV"], (j - 1) * NTC + tc + 1)) + wq_guard,
                            incs=((s["sSPa"], 1),))
                    pr.emit("scalar", lambda e, o=spA, i=spP: e.activation(o, i, AF.Abs))
                    pr.emit("scalar", lambda e, o=spB, i=spA: e.activation(o, i, AF.Exp, scale=cst[:, 4:5]))
                    pr.emit("scalar", lambda e, o=spA, i=spB: e.activation(o, i, AF.Ln, bias=cst[:, 5:6]),
                            incs=((s["sDel"], 1),))
                    pr.emit("vector", lambda e, o=rlb, i=spP: e.tensor_scalar_max(o, i, 0.0),
                            waits=((s["sDel"], s["sDel"].n),), incs=((s["sSPr"], 1),))
                    pr.emit("vector", lambda e, o=dl_t(eh)[:, tc * TCH:(tc + 1) * TCH],
                            a=rlb, b=spA: e.tensor_tensor(o, a, b, ALU.add),
                            incs=((s["sDelV"], 1),))
                pr.emit("vector", lambda e, o=du_t(eh), a=dl_t(eh), b=ug_t(eh): e.tensor_mul(o, a, b),
                        waits=((rUg.s[eh], 16 * (g + 1)),), incs=((s["sDu"], 1),))
                if debug:
                    pr.emit("gpsimd", lambda e, o=dr["dbg_delta"][et * P:(et + 1) * P, :],
                            i=dl_t(eh): e.dma_start(out=o, in_=i),
                            waits=((s["sDu"], s["sDu"].n),), incs=((s["sDbg"], 16),))
            if until in ("p2b", "p2b1", "p2b2", "p2b3", "p2b4", "p2b0", "p2n1", "p2n2", "p2n3", "p2n4", "p2n5"):
                return
            dgrp = (g + 1) * GE * NTC
            for n in range(N if until != "p2c" else 1):
                nsl = n % 2
                gi = g * N + n
                smb, vbB = rBB.issue(nsl)
                pr.emit("sync", lambda e, o=bcB(nsl),
                        i=sc_rows[n:n + 1, :].broadcast_to((P, T)): e.dma_start(out=o, in_=i),
                        waits=((s["sRows"], ROWS_RDY), (s["sRowsD"], 16),
                               (s["sYm"], (gi - 1) * GE)),
                        incs=((smb, 16),))
                smc, vbC = rCC.issue(nsl)
                pr.emit("sync", lambda e, o=bcC(nsl),
                        i=sc_rows[16 + n:16 + n + 1, :].broadcast_to((P, T)): e.dma_start(out=o, in_=i),
                        incs=((smc, 16),))
                for eh in range(GE):
                    et = g * GE + eh
                    it = gi * GE + eh
                    r2 = it % 2
                    pr.emit("scalar", lambda e, o=dA_t(r2), i=dl_t(eh),
                            a=A_t[:, et * N + n: et * N + n + 1]: e.activation(o, i, AF.Exp, scale=a),
                            waits=((s["sDelV"], dgrp), (s["sScan"], it - 1)),
                            incs=((s["sDA"], 1),))
                    pr.emit("gpsimd", lambda e, o=b_t(r2), a=du_t(eh), b=bcB(nsl): e.tensor_mul(o, a, b),
                            waits=((s["sDu"], (g + 1) * GE), (smb, vbB),
                                   (s["sScan"], it - 1)),
                            incs=((s["sBg"], 1),))
                    pr.emit("vector", lambda e, o=h_t(r2), a=dA_t(r2), b=b_t(r2): e.tensor_tensor_scan(
                        o, a, b, 0.0, ALU.mult, ALU.add),
                            waits=((s["sDA"], it + 1), (s["sBg"], it + 1), (s["sYm"], it - 1)),
                            incs=((s["sScan"], 1),))
                    yacc = y_e(eh) if n % 2 == 0 else y_o(eh)
                    if n < 2:
                        pr.emit("vector", lambda e, o=yacc, a=h_t(r2), b=bcC(nsl): e.tensor_mul(o, a, b),
                                waits=((smc, vbC),), incs=((s["sYm"], 1),))
                    else:
                        pr.emit("vector", lambda e, o=h_t(r2), a=h_t(r2), b=bcC(nsl): e.tensor_mul(o, a, b),
                                waits=((smc, vbC),))
                        pr.emit("vector", lambda e, o=yacc, a=yacc, b=h_t(r2): e.tensor_tensor(
                            o, a, b, ALU.add), incs=((s["sYm"], 1),))
            szv = {}
            for eh in range(GE):
                et = g * GE + eh
                szv[eh] = rSZ.issue(eh % 2)
                pr.emit("sync", lambda e, o=arb(o_szb2 + (eh % 2) * T * 2, T),
                        i=sc_sz[et * P:(et + 1) * P, :]: e.dma_start(out=o, in_=i),
                        waits=((rSZsp.s[0], 16 * ((ET + 1) // 2)), (rSZsp.s[1], 16 * (ET // 2)),
                               (s["sGate"], g * GE + eh - 1),
                               (s["sScan"], (g + 1) * N * GE)),
                        incs=((szv[eh][0], 16),))
            ym_grp = (g * N + N - 1) * GE
            for eh in range(GE):
                et = g * GE + eh
                if debug:
                    pr.emit("gpsimd", lambda e, o=dr["dbg_y"][et * P:(et + 1) * P, :],
                            i=y_e(eh): e.dma_start(out=o, in_=i),
                            waits=((s["sYm"], ym_grp + eh + 1),), incs=((s["sDbg"], 16),))
                yf = arf(o_h, T)
                pr.emit("vector", lambda e, o=yf, i=ug_t(eh), d=D_s[:, et:et + 1],
                        y=y_e(eh): e.scalar_tensor_tensor(o, i, d, y, ALU.mult, ALU.add),
                        waits=((s["sYm"], ym_grp + eh + 1),))
                pr.emit("vector", lambda e, o=yf, a=yf, b=y_o(eh): e.tensor_tensor(o, a, b, ALU.add))
                pr.emit("vector", lambda e, o=ug_t(eh), a=yf,
                        b=arb(o_szb2 + (eh % 2) * T * 2, T): e.tensor_mul(o, a, b),
                        waits=(szv[eh],),
                        incs=((s["sGate"], 1),))
                smg, _ = rGsp.issue(eh)
                pr.emit("gpsimd", lambda e, o=sc_g[et * P:(et + 1) * P, :], i=ug_t(eh): e.dma_start(out=o, in_=i),
                        waits=((s["sGate"], g * GE + eh + 1),), incs=((smg, 16),))

    import os
    QDBG = {"n": 0, "tgt": int(os.environ.get("QDBG_N", "0"))}

    def quant(src_ap, out_ap, W, q, waits_first=(), waits_dve=(), incs_last=()):
        B = W // 16
        ab = arf(*q["absx"])[:, :W]
        vv = arf(*q["v"])[:, :W]
        qa = arb(*q["qa"])[:, :W]
        qb_ = arb(*q["qb"])[:, :W]
        sg = arb(*q["sgn"])[:, :W]
        mx = arf(*q["mx"])[:, :B]
        mxc = arf(*q["mxc"])[:, :B]
        rs = arf(*q["rs"])[:, :B]
        scq = arf(*q["sc"])[:, :B]
        prev = s["sQd"].n
        dbgw = ((s["sDbg"], s["sDbg"].n),) if debug else ()
        pr.emit("scalar", lambda e, o=ab, i=src_ap: e.activation(o, i, AF.Abs),
                waits=tuple(waits_first) + ((s["sQd"], prev),) + dbgw, incs=((s["sQd"], 1),))
        pr.emit("scalar", lambda e, o=sg, i=src_ap: e.activation(o, i, AF.Sign))
        pr.emit("scalar", lambda e, o=cols[:, 24:25], i=ab[:, W - 1:W]: e.copy(o, i))
        pr.emit("scalar", lambda e, o=cols[:, 25:26], i=sg[:, W - 1:W]: e.copy(o, i),
                incs=((s["sQd"], 1),))
        w0 = s["sQd"].n
        pr.emit("vector", lambda e, o=mx.unsqueeze(2),
                i=ab.rearrange("p (b i) -> p b i", i=16): e.tensor_reduce(
            o, i, mybir.AxisListType.X, ALU.max),
                waits=((s["sQd"], w0),), incs=((s["sQs"], 1),))
        pr.emit("vector", lambda e, o=mxc, i=mx: e.tensor_scalar_max(o, i, 6e-12),
                waits=((s["sQs"], s["sQs"].n),), incs=((s["sQs"], 1),))
        pr.emit("vector", lambda e, o=rs, i=mxc: e.reciprocal(o, i),
                waits=((s["sQs"], s["sQs"].n),))
        pr.emit("vector", lambda e, o=scq, i=mxc: e.tensor_scalar_mul(o, i, 1.0 / 6.0),
                incs=((s["sQs"], 1),))
        bv = lambda t_: t_.rearrange("p (b i) -> p b i", i=16)
        rsb = rs.unsqueeze(2).broadcast_to((P, B, 16))
        scb = scq.unsqueeze(2).broadcast_to((P, B, 16))
        pr.emit("vector", lambda e, o=bv(vv), a=bv(ab), b=rsb: e.tensor_tensor(o, a, b, ALU.mult),
                waits=((s["sQs"], s["sQs"].n),))
        pr.emit("vector", lambda e, o=qa, i=vv: e.tensor_scalar(o, i, QTH[0], QW[0], ALU.is_gt, ALU.mult))
        for j in range(1, 7):
            pr.emit("vector", lambda e, o=qb_, i=vv, t_=QTH[j], w=QW[j]: e.tensor_scalar(
                o, i, t_, w, ALU.is_gt, ALU.mult))
            pr.emit("vector", lambda e, o=qa, b=qb_: e.tensor_tensor(o, o, b, ALU.add))
        take = debug and W == D and QDBG["n"] == QDBG["tgt"]
        QDBG["n"] += 1
        if take:
            pr.emit("gpsimd", lambda e, i=ab: e.dma_start(out=dr["dbg_q_ab"], in_=i),
                    waits=((s["sQd"], s["sQd"].n),), incs=((s["sDbg"], 16),))
            pr.emit("gpsimd", lambda e, i=mx: e.dma_start(out=dr["dbg_q_mx"], in_=i),
                    incs=((s["sDbg"], 16),))
            pr.emit("gpsimd", lambda e, i=vv: e.dma_start(out=dr["dbg_q_v"], in_=i),
                    incs=((s["sDbg"], 16),))
            pr.emit("gpsimd", lambda e, i=qa: e.dma_start(out=dr["dbg_q_qa"], in_=i),
                    incs=((s["sDbg"], 16),))
        pr.emit("vector", lambda e, o=qa, b=sg: e.tensor_mul(o, o, b),
                incs=((s["sQd"], 1),))
        assert len(incs_last) <= 1
        pr.emit("vector", lambda e, o=bv(out_ap), a=bv(qa), b=scb: e.tensor_tensor(o, a, b, ALU.mult),
                waits=tuple(waits_dve), incs=tuple(incs_last))

    QHW = 1408 if I % 1408 == 0 or I >= 1408 else I
    qh = qblock(aln(end_p1), min(1408, I))
    assert qh["end"] <= AR_BYTES
    QOS = max(D, I, T, 1024)

    def _row_tiles(rows):
        off, out = 0, []
        while off < rows:
            out.append((off, min(P, rows - off)))
            off += P
        return out

    wq_tasks = []
    for off, h in _row_tiles(MYI if WQ_SHARD else I):
        wq_tasks.append(("wg", off, h))
    for off, h in _row_tiles(MYI if WQ_SHARD else I):
        wq_tasks.append(("wu", off, h))
    for off, h in _row_tiles(MYD if WQ_SHARD else D):
        wq_tasks.append(("wd", off, h))
    wq_state = {"i": 0}

    def emit_wq_tasks(k):
        for _ in range(k):
            qi = wq_state["i"]
            if qi >= len(wq_tasks):
                return
            wq_state["i"] += 1
            kind, roff, h = wq_tasks[qi]
            sl = qi % 2
            if kind == "wd":
                srcd, W = wd_d[roff:roff + h, :], I
            else:
                srcd = (wg_d if kind == "wg" else wu_d)[roff:roff + h, :]
                W = D
            stg = wst[sl][:, P:P + W]
            smq, vq = rWQ.issue(sl)
            pr.emit("sync", lambda e, o=stg[0:h, :], i=srcd: e.dma_start(out=o, in_=i),
                    waits=((s["sQd"], s["sQd"].n - 4),), incs=((smq, 16),))
            oslot = qo[:, sl * QOS: sl * QOS + W]
            halves = [(0, W)] if W <= 1408 else [(0, W // 2), (W // 2, W // 2)]
            for hoff, hw in halves:
                quant(stg[:, hoff:hoff + hw], oslot[:, hoff:hoff + hw], hw, qh,
                      waits_first=((smq, vq),),
                      waits_dve=((rWQsp.s[sl], 16 * rWQsp.u[sl]),),
                      incs_last=((s["sWqh"], 1),))
            if WQ_SHARD:
                dstd = {"wg": sc_wgq_my, "wu": sc_wuq_my, "wd": sc_wdq_my}[kind]
            else:
                dstd = {"wg": sc_wgq, "wu": sc_wuq, "wd": sc_wdq}[kind]
            smsp, _ = rWQsp.issue(sl)
            pr.emit("gpsimd", lambda e, o=dstd[roff:roff + h, :], i=oslot[0:h, :]: e.dma_start(out=o, in_=i),
                    waits=((s["sWqh"], s["sWqh"].n),), incs=((smsp, 16),))

    def emit_wq_collectives():
        emit_wq_tasks(len(wq_tasks))
        if not WQ_SHARD:
            return
        rgs = [list(range(n_cores))]
        wqw = ((rWQsp.s[0], 16 * rWQsp.u[0]), (rWQsp.s[1], 16 * rWQsp.u[1]))
        for my_t, full_t in ((sc_wgq_my, sc_wgq), (sc_wuq_my, sc_wuq), (sc_wdq_my, sc_wdq)):
            pr.emit("gpsimd", lambda e, i=my_t, o=full_t: e.collective_compute(
                "AllGather", ALU.bypass, rgs, ins=[i], outs=[o]),
                    waits=wqw, incs=((s["sCol"], 1),))

    def p3():
        def wo_t(et): return arb(o_wo + et * D * 2, D)
        for et in range(ET):
            sl = et % 2
            smw, vw = rW.issue(sl)
            pr.emit("sync", lambda e, o=wst[sl][:, :D],
                    i=wout_d[et * P:(et + 1) * P, :]: e.dma_start(out=o, in_=i),
                    waits=((s["sWo"], et - 1),), incs=((smw, 16),))
            pr.emit("vector", lambda e, o=wo_t(et), i=wst[sl][:, :D]: e.tensor_copy(o, i),
                    waits=((smw, vw),), incs=((s["sWo"], 1),))
        for ch in range(NCH):
            cs = ch % 2
            def gld(et, cs=cs):
                return arb(o_gld + (cs * ET + et) * CW * 2, CW)
            gw = tuple((rGsp.s[eh], 16 * NG) for eh in range(GE))
            vgld = 0
            for et in range(ET):
                smgl, vgld = rGld.issue(cs)
                pr.emit("sync", lambda e, o=gld(et),
                        i=sc_g[et * P:(et + 1) * P, ch * CW:(ch + 1) * CW]: e.dma_start(out=o, in_=i),
                        waits=gw + ((s["sMMo"], (ch - 1) * TS2 * NDH),),
                        incs=((smgl, 16),))
            for t2 in range(TS2):
                ts = ch * TS2 + t2
                for dh in range(NDH):
                    bank = t2 * NDH + dh
                    for et in range(ET):
                        lhs = gld(et)[:, t2 * P:(t2 + 1) * P]
                        rhs = wo_t(et)[:, dh * 512:dh * 512 + DW]
                        pr.emit("tensor", lambda e, o=psv(bank, DW), l_=lhs, r_=rhs,
                                st=(et == 0), sp=(et == ET - 1): e.matmul(o, l_, r_, start=st, stop=sp, **MMKW),
                                waits=((rGld.s[cs], vgld), (s["sWo"], ET),
                                       (s["sX2"], ts - TS2 + 1)),
                                incs=((s["sMMo"], 1),) if et == ET - 1 else ())
                smx3, vx3 = rX.issue(ts % 2)
                pr.emit("sync", lambda e, o=xt[ts % 2][:],
                        i=x_in[ts * P:(ts + 1) * P, :]: e.dma_start(out=o, in_=i),
                        waits=((s["sX2"], ts - 1),), incs=((smx3, 16),))
                mmw = s["sMMo"].n
                for dh in range(NDH):
                    bank = t2 * NDH + dh
                    rt_h = arf(o_rt + dh * 512 * 4, DW)
                    xr_h = xt[ts % 2][:, dh * 512:dh * 512 + DW]
                    pr.emit("vector", lambda e, o=rt_h, i=xr_h, p_=psv(bank, DW): e.scalar_tensor_tensor(
                        o, i, 2.0, p_, ALU.mult, ALU.add),
                            waits=((s["sMMo"], mmw), (smx3, vx3)),
                            incs=((s["sStA"], 1),) if dh == NDH - 1 else ())
                rt = arf(o_rt, D)
                pr.emit("scalar", lambda e, i=rt: e.activation(scr[:, :D], i, AF.Identity,
                                                               accum_out=cols[:, 8:9]),
                        waits=((s["sStA"], s["sStA"].n),))
                pr.emit("scalar", lambda e, i=rt: e.activation(scr[:, :D], i, AF.Square,
                                                               accum_out=cols[:, 9:10]))
                pr.emit("scalar", lambda e: e.mul(cols[:, 10:11], cols[:, 8:9], cst[:, 2:3]))
                pr.emit("scalar", lambda e: e.activation(cols[:, 11:12], cols[:, 10:11], AF.Square))
                pr.emit("scalar", lambda e: e.mul(cols[:, 12:13], cols[:, 9:10], cst[:, 2:3]),
                        incs=((s["sLs1"], 1),))
                pr.emit("vector", lambda e: e.tensor_tensor(cols[:, 13:14], cols[:, 12:13],
                                                            cols[:, 11:12], ALU.subtract),
                        waits=((s["sLs1"], s["sLs1"].n),), incs=((s["sLs2"], 1),))
                pr.emit("scalar", lambda e: e.activation(cols[:, 14:15], cols[:, 13:14], AF.Ln, bias=cst[:, 1:2]),
                        waits=((s["sLs2"], s["sLs2"].n),))
                pr.emit("scalar", lambda e: e.activation(cols[:, 15:16], cols[:, 14:15], AF.Exp, scale=cst[:, 3:4]),
                        incs=((s["sLs3"], 1),))
                x2b = arf(o_x2b, D)
                x2q = arb(o_x2q, D)
                pr.emit("vector", lambda e, o=x2b, i=rt: e.tensor_scalar(
                    o, i, cols[:, 10:11], cols[:, 15:16], ALU.subtract, ALU.mult),
                        waits=((s["sLs3"], s["sLs3"].n), (s["sQd"], s["sQd"].n),
                               (s["sX2sp"], s["sX2sp"].n)), incs=((s["sStB"], 1),))
                pr.emit("gpsimd", lambda e, o=sc_x2[ts * P:(ts + 1) * P, :], i=x2b: e.dma_start(out=o, in_=i),
                        waits=((s["sStB"], s["sStB"].n),), incs=((s["sX2sp"], 16),))
                quant(x2b, x2q, D, q3,
                      waits_first=((s["sStB"], s["sStB"].n),),
                      waits_dve=((s["sEv3"], s["sEv3"].n),),
                      incs_last=((s["sQo"], 1),))
                qd0 = s["sQo"].n
                for dt in range(DT):
                    pp = dt % 2
                    pr.emit("tensor", lambda e, o=psb(4 + pp), i=x2q[:, dt * P:(dt + 1) * P]: e.transpose(
                        o, i, ident[:]),
                            waits=((s["sQo"], qd0), (s["sEv3"], s["sTr3"].n - 1)),
                            incs=((s["sTr3"], 1),))
                    pr.emit("vector", lambda e, o=evt[pp][:], i=psb(4 + pp): e.tensor_copy(o, i),
                            waits=((s["sTr3"], s["sTr3"].n), (rXsp.s[pp], 16 * rXsp.u[pp])),
                            incs=((s["sEv3"], 1),))
                    smxs, _ = rXsp.issue(pp)
                    pr.emit("gpsimd", lambda e, o=sc_xqT[dt * P:(dt + 1) * P, ts * P:(ts + 1) * P],
                            i=evt[pp][:]: e.dma_start(out=o, in_=i),
                            waits=((s["sEv3"], s["sEv3"].n),), incs=((smxs, 16),))
                pr.emit("vector", lambda e: e.nop(), incs=((s["sX2"], 1),))

    def pf():
        def wgq_t(dt): return arb(o_wgq + dt * I * 2, I)
        def wuq_t(dt): return arb(o_wuq + dt * I * 2, I)
        def wdq_t(kt): return arb(o_wdq + kt * D * 2, D)
        qpf0 = qblock(o_hq, D)
        wq_rdy = (((s["sCol"], s["sCol"].n),) if WQ_SHARD else
                  ((rWQsp.s[0], 16 * rWQsp.u[0]), (rWQsp.s[1], 16 * rWQsp.u[1])))
        for wi, (wspl, dstf) in enumerate(((sc_wgq, wgq_t), (sc_wuq, wuq_t))):
            for it in range(IT):
                li = wi * IT + it
                sl = li % 2
                qslot = qo[:, sl * QOS: sl * QOS + D]
                smw, vw = rWQ2.issue(sl)
                pr.emit("sync", lambda e, o=qslot,
                        i=wspl[it * P:(it + 1) * P, :]: e.dma_start(out=o, in_=i),
                        waits=wq_rdy + ((s["sEvH"], (li - 1) * DT),),
                        incs=((smw, 16),))
                for dt in range(DT):
                    pp = dt % 2
                    pr.emit("tensor", lambda e, o=psb(4 + pp), i=qslot[:, dt * P:(dt + 1) * P]: e.transpose(
                        o, i, ident[:]),
                            waits=((smw, vw), (s["sEvH"], s["sTrH"].n - 1)),
                            incs=((s["sTrH"], 1),))
                    pr.emit("vector", lambda e, o=dstf(dt)[:, it * P:(it + 1) * P],
                            i=psb(4 + pp): e.tensor_copy(o, i),
                            waits=((s["sTrH"], s["sTrH"].n),), incs=((s["sEvH"], 1),))
        pr.barrier(s["bar"])
        if debug:
            pr.emit("gpsimd", lambda e, i=wgq_t(0): e.dma_start(out=dr["dbg_wgq"], in_=i),
                    incs=((s["sDbg"], 16),))
        for ts in range(TS):
            sl = ts % 2
            src = sc_xqT[:, ts * P:(ts + 1) * P].rearrange("(dt p) c -> dt p c", p=P).transpose([1, 0, 2])
            dst3 = xqt[sl][:].rearrange("p (dt c) -> p dt c", c=P)
            smxq, vxq = rXq.issue(sl)
            pr.emit("sync", lambda e, o=dst3, i=src: e.dma_start(out=o, in_=i),
                    waits=((rXsp.s[0], 16 * TS * DT // 2), (rXsp.s[1], 16 * TS * DT // 2),
                           (s["sMMg"], (ts - 1) * 2 * NNC)),
                    incs=((smxq, 16),))
            for ni, (noff, nwd) in enumerate(ncols):
                nig = ts * NNC + ni
                rot3 = nig % 3
                bg = (0, 1, 6)[rot3]
                bu = (2, 3, 7)[rot3]
                for wq_t, bank in ((wgq_t, bg), (wuq_t, bu)):
                    for k in range(DT):
                        lhs = xqt[sl][:, k * P:(k + 1) * P]
                        rhs = wq_t(k)[:, noff:noff + nwd]
                        pr.emit("tensor", lambda e, o=psv(bank, nwd), l_=lhs, r_=rhs,
                                st=(k == 0), sp=(k == DT - 1): e.matmul(o, l_, r_, start=st, stop=sp, **MMKW),
                                waits=((smxq, vxq),
                                       (s["sSigF"], 3 * (nig - 2))),
                                incs=((s["sMMg"], 1),) if k == DT - 1 else ())
                mmw = s["sMMg"].n
                spE = wst[0][:, 0:nwd]
                spD_ = wst[0][:, 512:512 + nwd].bitcast(BF16)[:, 0:nwd]
                g_s = wst[1][:, 0:nwd].bitcast(BF16)[:, 0:nwd]
                u_s = wst[1][:, 512:512 + nwd].bitcast(BF16)[:, 0:nwd]
                hhb = wst[0][:, 1024:1024 + nwd]
                pr.emit("scalar", lambda e, o=g_s, i=psv(bg, nwd): e.copy(o, i),
                        waits=((s["sMMg"], mmw), (s["sSigF"], 3 * nig)))
                pr.emit("scalar", lambda e, o=u_s, i=psv(bu, nwd): e.copy(o, i),
                        incs=((s["sSigF"], 1),))
                pr.emit("scalar", lambda e, o=spE, i=g_s: e.activation(o, i, AF.Tanh, scale=cst[:, 6:7]),
                        incs=((s["sSigF"], 1),))
                pr.emit("vector", lambda e, o=spD_, i=spE: e.tensor_scalar(
                    o, i, 0.5, 0.5, ALU.mult, ALU.add),
                        waits=((s["sSigF"], 3 * nig + 2),))
                pr.emit("vector", lambda e, o=spD_, a=g_s, b=spD_: e.tensor_mul(o, a, b))
                pr.emit("vector", lambda e, o=hhb, a=spD_, b=u_s: e.tensor_mul(o, a, b),
                        incs=((s["sSigF"], 1),))
                quant(hhb, arb(o_hq + noff * 2, nwd), nwd, qf,
                      waits_first=((s["sSigF"], s["sSigF"].n),),
                      waits_dve=((s["sEvH"], s["sEvH"].n),),
                      incs_last=((s["sHq"], 1),))
            hq0 = s["sHq"].n
            if debug and ts == 0:
                pr.emit("gpsimd", lambda e, i=arb(o_hq, I): e.dma_start(out=dr["dbg_hq"], in_=i),
                        waits=((s["sHq"], hq0),), incs=((s["sDbg"], 16),))
            for kt in range(IT):
                pp = kt % 2
                pr.emit("tensor", lambda e, o=psb(4 + pp), i=arb(o_hq + kt * P * 2, P): e.transpose(
                    o, i, ident[:]),
                        waits=((s["sHq"], hq0), (s["sEvH"], s["sTrH"].n - 1)),
                        incs=((s["sTrH"], 1),))
                pr.emit("vector", lambda e, o=arb(o_hqT + kt * P * 2, P), i=psb(4 + pp): e.tensor_copy(o, i),
                        waits=((s["sTrH"], s["sTrH"].n), (s["sHsp"], 16 * ts)),
                        incs=((s["sEvH"], 1),))
            dst = sc_hqT[:, ts * P:(ts + 1) * P].rearrange("(kt p) c -> kt p c", p=P).transpose([1, 0, 2])
            src_h = arb(o_hqT, IT * P).rearrange("p (kt c) -> p kt c", c=P)
            pr.emit("gpsimd", lambda e, o=dst, i=src_h: e.dma_start(out=o, in_=i),
                    waits=((s["sEvH"], s["sEvH"].n),), incs=((s["sHsp"], 16),))
        pr.barrier(s["bar"])
        for dt in range(DT):
            sl = dt % 2
            qslot = qo[:, sl * QOS: sl * QOS + I]
            smw, vw = rWQ2.issue(sl)
            evb = s["sEvH"].n - IT if dt >= 2 else 0
            pr.emit("sync", lambda e, o=qslot,
                    i=sc_wdq[dt * P:(dt + 1) * P, :]: e.dma_start(out=o, in_=i),
                    waits=((s["sEvH"], evb),), incs=((smw, 16),))
            for kt in range(IT):
                pp = kt % 2
                pr.emit("tensor", lambda e, o=psb(4 + pp), i=qslot[:, kt * P:(kt + 1) * P]: e.transpose(
                    o, i, ident[:]),
                        waits=((smw, vw), (s["sEvH"], s["sTrH"].n - 1)),
                        incs=((s["sTrH"], 1),))
                pr.emit("vector", lambda e, o=wdq_t(kt)[:, dt * P:(dt + 1) * P], i=psb(4 + pp): e.tensor_copy(o, i),
                        waits=((s["sTrH"], s["sTrH"].n),), incs=((s["sEvH"], 1),))
        pr.emit("vector", lambda e: e.nop(), incs=((s["sWqT"], 1),))
        wq_all = s["sWq"].n
        trh_all = s["sTrH"].n
        for ts in range(TS):
            sl = ts % 2
            src = sc_hqT[:, ts * P:(ts + 1) * P].rearrange("(kt p) c -> kt p c", p=P).transpose([1, 0, 2])
            dsth = arb(o_hqr + sl * I * 2, I).rearrange("p (kt c) -> p kt c", c=P)
            smhr, vhr = rHr.issue(sl)
            pr.emit("sync", lambda e, o=dsth, i=src: e.dma_start(out=o, in_=i),
                    waits=((s["sHsp"], 16 * TS), (s["sMMd"], (ts - 1) * NDH)),
                    incs=((smhr, 16),))
            smx2, vx2 = rX2r.issue(sl)
            pr.emit("sync", lambda e, o=arf(o_x2r + sl * D * 4, D),
                    i=sc_x2[ts * P:(ts + 1) * P, :]: e.dma_start(out=o, in_=i),
                    waits=((s["sX2sp"], 16 * TS), (s["sRes"], (ts - 1) * NDH),
                           (s["sWq"], wq_all), (s["sTrH"], trh_all)),
                    incs=((smx2, 16),))
            for dh in range(NDH):
                bank = (ts % 2) * 2 + dh
                for kt in range(IT):
                    lhs = arb(o_hqr + sl * I * 2 + kt * P * 2, P)
                    rhs = wdq_t(kt)[:, dh * 512:dh * 512 + DW]
                    pr.emit("tensor", lambda e, o=psv(bank, DW), l_=lhs, r_=rhs,
                            st=(kt == 0), sp=(kt == IT - 1): e.matmul(o, l_, r_, start=st, stop=sp, **MMKW),
                            waits=((smhr, vhr), (s["sWqT"], 1),
                                   (s["sRes"], (ts - 2) * NDH + dh + 1)),
                            incs=((s["sMMd"], 1),) if kt == IT - 1 else ())
            mmw = s["sMMd"].n
            rt2 = arf(o_rt2, D)
            for dh in range(NDH):
                bank = (ts % 2) * 2 + dh
                rt2h = arf(o_rt2 + dh * 512 * 4, DW)
                x2h = arf(o_x2r + sl * D * 4 + dh * 512 * 4, DW)
                pr.emit("vector", lambda e, o=rt2h, a=x2h, p_=psv(bank, DW): e.tensor_tensor(
                    o, a, p_, ALU.add),
                        waits=((s["sMMd"], mmw), (smx2, vx2)),
                        incs=((s["sRes"], 1),))
            pr.emit("scalar", lambda e, i=rt2: e.activation(scr[:, :D], i, AF.Identity,
                                                            accum_out=cols[:, 16:17]),
                    waits=((s["sRes"], s["sRes"].n),))
            pr.emit("scalar", lambda e, i=rt2: e.activation(scr[:, :D], i, AF.Square,
                                                            accum_out=cols[:, 17:18]))
            pr.emit("scalar", lambda e: e.mul(cols[:, 18:19], cols[:, 16:17], cst[:, 2:3]))
            pr.emit("scalar", lambda e: e.activation(cols[:, 19:20], cols[:, 18:19], AF.Square))
            pr.emit("scalar", lambda e: e.mul(cols[:, 20:21], cols[:, 17:18], cst[:, 2:3]),
                    incs=((s["sLs1"], 1),))
            pr.emit("vector", lambda e: e.tensor_tensor(cols[:, 21:22], cols[:, 20:21],
                                                        cols[:, 19:20], ALU.subtract),
                    waits=((s["sLs1"], s["sLs1"].n),), incs=((s["sLs2"], 1),))
            pr.emit("scalar", lambda e: e.activation(cols[:, 22:23], cols[:, 21:22], AF.Ln, bias=cst[:, 1:2]),
                    waits=((s["sLs2"], s["sLs2"].n),))
            pr.emit("scalar", lambda e: e.activation(cols[:, 23:24], cols[:, 22:23], AF.Exp, scale=cst[:, 3:4]),
                    incs=((s["sLs3"], 1),))
            ot = arf(o_outt + (ts % 2) * D * 4, D)
            pr.emit("vector", lambda e, o=ot, i=rt2: e.tensor_scalar(
                o, i, cols[:, 18:19], cols[:, 23:24], ALU.subtract, ALU.mult),
                    waits=((s["sLs3"], s["sLs3"].n), (rOut.s[ts % 2], 16 * rOut.u[ts % 2])),
                    incs=((s["sStA"], 1),))
            smo, _ = rOut.issue(ts % 2)
            pr.emit("gpsimd", lambda e, o=out_d[ts * P:(ts + 1) * P, :], i=ot: e.dma_start(out=o, in_=i),
                    waits=((s["sStA"], s["sStA"].n),), incs=((smo, 16),))
        pr.emit("gpsimd", None, waits=((rOut.s[0], 16 * ((TS + 1) // 2)), (rOut.s[1], 16 * (TS // 2))))

    def _terminate():
        zt = arf(0, D)
        pr.emit("vector", lambda e, o=zt: e.memset(o, 0.0), incs=((s["sStB"], 1),))
        for ts in range(TS):
            smo, _ = rOut.issue(ts % 2)
            pr.emit("gpsimd", lambda e, o=out_d[ts * P:(ts + 1) * P, :], i=zt: e.dma_start(out=o, in_=i),
                    waits=((s["sStB"], s["sStB"].n),), incs=((smo, 16),))
        pr.emit("gpsimd", None, waits=((rOut.s[0], 16 * rOut.u[0]), (rOut.s[1], 16 * rOut.u[1])))

    phases = [("p0", p0), ("p1a", p1a), ("p1bx", lambda: p1b_half(0)),
              ("p1bz", lambda: p1b_half(1)), ("p1c", p1c)]
    if stop_after in ("p2b", "p2c", "p2a0", "p2a1", "p2b1", "p2b2", "p2b3", "p2b4", "p2b0", "p2n1", "p2n2", "p2n3", "p2n4", "p2n5"):
        phases.append((stop_after, lambda: p2(until=stop_after)))
    else:
        phases += [("p2", p2), ("p3", p3), ("pf", pf)]
    for nm, fn in phases:
        fn()
        pr.barrier(s["bar"])
        if stop_after == nm:
            _terminate()
            break

    with nc.Block() as block:
        def runner(lst):
            def f(e):
                for fn, waits, incs in lst:
                    for sm, v in waits:
                        e.wait_ge(sm.h, v)
                    if fn is None:
                        continue
                    ins = fn(e)
                    for sm, k in incs:
                        if k > 0:
                            ins.then_inc(sm.h, k)
            return f
        block.sync(runner(pr.ops["sync"]))
        block.gpsimd(runner(pr.ops["gpsimd"]))
        block.tensor(runner(pr.ops["tensor"]))
        block.scalar(runner(pr.ops["scalar"]))
        block.vector(runner(pr.ops["vector"]))
    es.close()
    return nc




CFG = dict(T=2048, D=1024, E=2048, N=16, R=64, I=2816)
N_CORES = 8
_NC_CACHE = {}


def _get_nc():
    if "nc" not in _NC_CACHE:
        _NC_CACHE["nc"] = build(CFG, debug=False, n_cores=N_CORES)
    return _NC_CACHE["nc"]


def kernel(L0_attn_hidden_states, L0_attn_norm_w, L0_attn_W_in, L0_attn_conv_w,
           L0_attn_W_x, L0_attn_W_dt, L0_attn_b_dt, L0_attn_A_log,
           L0_attn_D_param, L0_attn_W_out, L0_ffn_w_gate, L0_ffn_w_up,
           L0_ffn_w_down):
    import ml_dtypes
    from concourse.bass_utils import run_bass_kernel_spmd
    E, Pn = CFG["E"], P
    ET = E // Pn
    c = np.ascontiguousarray

    def featmajor(a, w):
        return c(np.asarray(a, np.float32).reshape(ET, Pn, w).transpose(1, 0, 2)
                 .reshape(Pn, ET * w))

    DT_, ET2 = CFG["D"] // Pn, 2 * CFG["E"] // Pn
    common = {
        "W_in": c(np.asarray(L0_attn_W_in, np.float32)
                  .reshape(DT_, Pn, ET2, Pn).transpose(0, 2, 1, 3)
                  .reshape(DT_ * ET2 * Pn, Pn)),
        "convw": featmajor(np.asarray(L0_attn_conv_w, np.float32).reshape(E, 4), 4),
        "W_x": featmajor(L0_attn_W_x, 96),
        "Wdt_ext": c(np.concatenate(
            [np.asarray(L0_attn_W_dt, np.float32),
             np.asarray(L0_attn_b_dt, np.float32)[None, :]], axis=0)),
        "A_log": featmajor(L0_attn_A_log, CFG["N"]),
        "D_param": c(np.asarray(L0_attn_D_param, np.float32).reshape(ET, Pn).T),
        "norm_w": c(np.asarray(L0_attn_norm_w, np.float32).reshape(1, CFG["D"])),
        "ident": np.eye(Pn, dtype=ml_dtypes.bfloat16),
        "W_out": c(np.asarray(L0_attn_W_out, np.float32)),
    }
    wg_full = np.asarray(L0_ffn_w_gate, np.float32)
    wu_full = np.asarray(L0_ffn_w_up, np.float32)
    wd_full = np.asarray(L0_ffn_w_down, np.float32)
    myi = CFG["I"] // N_CORES
    myd = CFG["D"] // N_CORES
    xs = np.asarray(L0_attn_hidden_states, np.float32)
    in_maps = [dict(common, x=c(xs[b]),
                    wg=c(wg_full[b * myi:(b + 1) * myi]),
                    wu=c(wu_full[b * myi:(b + 1) * myi]),
                    wd=c(wd_full[b * myd:(b + 1) * myd]))
               for b in range(xs.shape[0])]
    nc = _get_nc()
    res = run_bass_kernel_spmd(nc, in_maps, core_ids=list(range(len(in_maps))))
    return np.stack([r["out"] for r in res.results], axis=0)


# version 19
# speedup vs baseline: 1.1259x; 1.1259x over previous
import numpy as np
from contextlib import ExitStack
import concourse.bass as bass
import concourse.mybir as mybir
from concourse.dve_op import DveOp
from concourse.dve_spec import Spec, Src0, Src1, C0, C1, C2, Latch, lower as _dve_lower
import concourse.dve_ops as _dve_ops_mod

F32 = mybir.dt.float32
BF16 = mybir.dt.bfloat16
AF = mybir.ActivationFunctionType
ALU = mybir.AluOpType
P = 128

ENGINES = ("sync", "gpsimd", "tensor", "scalar", "vector")

QTH = (0.25 / 6, 0.75 / 6, 1.25 / 6, 1.75 / 6, 2.5 / 6, 3.5 / 6, 5.0 / 6)
QW = (0.5, 0.5, 0.5, 0.5, 1.0, 1.0, 2.0)

_FP4OPS = {}


def _register_fp4_ops():
    if _FP4OPS:
        return _FP4OPS
    b1 = (Src0 > C0) + (Src0 > C1) + (Src0 > C2) + (Src0 > Latch(Src1))
    _c5 = Src0 > C0
    _c6 = Src0 > C1
    _c7 = Src0 > C2
    _e = _c5 + _c6 + _c7 + _c7
    b2 = Src1 + (_e + _e)

    def ref1(in0, in1, s0, s1, imm2):
        f = np.result_type(in0.dtype, np.float32)
        return ((in0 > s0).astype(f) + (in0 > s1).astype(f)
                + (in0 > imm2).astype(f) + (in0 > in1).astype(f))

    def ref2(in0, in1, s0, s1, imm2):
        f = np.result_type(in0.dtype, np.float32)
        return in1 + 2 * ((in0 > s0).astype(f) + (in0 > s1).astype(f)) + 4 * (in0 > imm2).astype(f)

    for nm, body, ref in (("FP4Q1", b1, ref1), ("FP4Q2", b2, ref2)):
        shas = {}
        for ver in ("v3", "v4"):
            try:
                shas[ver] = _dve_lower(Spec(body=body), ver=ver, perf_en=False, name=nm).sha(ver)
            except Exception:
                pass
        op = DveOp(nm, Spec(body=body), subdim=False, uops_sha=shas, reference=ref)
        _dve_ops_mod.OPS.append(op)
        _dve_ops_mod._SUB_OPCODE_FOR_NAME[nm] = max(_dve_ops_mod._SUB_OPCODE_FOR_NAME.values()) + 1
        _dve_ops_mod.CUSTOM_DVE_REFS[nm] = op.ref
        _FP4OPS[nm] = op
    assert max(_dve_ops_mod._SUB_OPCODE_FOR_NAME.values()) < 0x20
    return _FP4OPS


class SemC:
    def __init__(self, h):
        self.h = h
        self.n = 0


class Prog:
    def __init__(self, nc, es):
        self.nc = nc
        self.es = es
        self.ops = {e: [] for e in ENGINES}

    def sem(self, name):
        return SemC(self.es.enter_context(self.nc.semaphore(name)))

    def emit(self, eng, fn, waits=(), incs=()):
        waits = tuple((sm, v) for sm, v in waits if v > 0)
        for sm, k in incs:
            sm.n += k
        self.ops[eng].append((fn, waits, tuple(incs)))

    def barrier(self, bar):
        tgt = bar.n + len(ENGINES)
        for e in ENGINES:
            self.emit(e, lambda en: en.nop(), incs=((bar, 1),))
        for e in ENGINES:
            self.emit(e, None, waits=((bar, tgt),))


class DmaSlots:
    def __init__(self, pr, name, nslots):
        self.s = [pr.sem(f"{name}_{i}") for i in range(nslots)]
        self.u = [0] * nslots

    def issue(self, slot):
        self.u[slot] += 1
        return self.s[slot], 16 * self.u[slot]


def build(cfg, debug=False, stop_after=None, n_cores=1):
    T = cfg["T"]; D = cfg["D"]; E = cfg["E"]; N = cfg["N"]; R = cfg["R"]; I = cfg["I"]
    WQ_SHARD = n_cores > 1
    if WQ_SHARD:
        assert I % n_cores == 0 and D % n_cores == 0
        MYI = I // n_cores
        MYD = D // n_cores
    DT = D // P
    ET = E // P
    IT = I // P
    TS = T // P
    TCH = min(512, T)
    NTC = T // TCH
    GE = min(2, ET)
    NG = ET // GE
    NDH = 2 if D > 512 else 1
    DW = min(512, D)
    CW = min(256, T)
    TS2 = CW // P
    NCH = T // CW
    EPS_RMS = 1e-6
    EPS_LN = 1e-5
    ncols = []
    off = 0
    while off < I:
        w = min(512, I - off)
        ncols.append((off, w))
        off += w
    NNC = len(ncols)

    nc = bass.Bass("TRN2", target_bir_lowering=False, debug=False,
                   detect_race_conditions=False)
    es = ExitStack()
    pr = Prog(nc, es)
    io = "ExternalOutput" if debug else "Internal"

    dr = {}
    def dram(name, shape, dt, kind, addr_space="Local"):
        dr[name] = nc.dram_tensor(name, list(shape), dt, kind=kind,
                                  addr_space=addr_space).ap()
        return dr[name]

    x_in = dram("x", (T, D), F32, "ExternalInput")
    w_in = dram("W_in", (DT * 2 * ET * P, P), F32, "ExternalInput")
    convw = dram("convw", (P, 4 * (E // P)), F32, "ExternalInput")
    w_x = dram("W_x", (P, (E // P) * 96), F32, "ExternalInput")
    wdt_e = dram("Wdt_ext", (R + 1, E), F32, "ExternalInput")
    alog = dram("A_log", (P, (E // P) * N), F32, "ExternalInput")
    dpar = dram("D_param", (P, E // P), F32, "ExternalInput")
    nw_d = dram("norm_w", (1, D), F32, "ExternalInput")
    wg_d = dram("wg", (MYI if WQ_SHARD else I, D), F32, "ExternalInput")
    wu_d = dram("wu", (MYI if WQ_SHARD else I, D), F32, "ExternalInput")
    wd_d = dram("wd", (MYD if WQ_SHARD else D, I), F32, "ExternalInput")
    idn = dram("ident", (P, P), BF16, "ExternalInput")
    wout_d = dram("W_out", (E, D), F32, "ExternalInput")
    out_d = dram("out", (T, D), F32, "ExternalOutput")

    sc_rows = dram("sc_rows", (32, T), BF16, io)
    sc_u = dram("sc_u", (E, T), BF16, io)
    sc_sz = dram("sc_sz", (E, T), BF16, io)
    sc_g = dram("sc_g", (E, T), BF16, io)
    sc_x2 = dram("sc_x2", (T, D), F32, io)
    sc_xqT = dram("sc_xqT", (D, T), BF16, io)
    sc_hqT = dram("sc_hqT", (I, T), BF16, io)
    wq_space = "Shared" if WQ_SHARD else "Local"
    sc_wgq = dram("sc_wgq", (I, D), BF16, io, addr_space=wq_space)
    sc_wuq = dram("sc_wuq", (I, D), BF16, io, addr_space=wq_space)
    sc_wdq = dram("sc_wdq", (D, I), BF16, io, addr_space=wq_space)
    if WQ_SHARD:
        sc_wgq_my = dram("sc_wgq_my", (MYI, D), BF16, "Internal")
        sc_wuq_my = dram("sc_wuq_my", (MYI, D), BF16, "Internal")
        sc_wdq_my = dram("sc_wdq_my", (MYD, I), BF16, "Internal")
    if debug:
        dram("dbg_q_ab", (P, D), F32, "ExternalOutput")
        dram("dbg_q_m1", (P, D // 2), F32, "ExternalOutput")
        dram("dbg_q_m2", (P, D // 4), F32, "ExternalOutput")
        dram("dbg_q_m3", (P, D // 8), F32, "ExternalOutput")
        dram("dbg_wgq", (P, I), BF16, "ExternalOutput")
        dram("dbg_qo", (P, D), BF16, "ExternalOutput")
        dram("dbg_hq", (P, I), BF16, "ExternalOutput")
        dram("dbg_q_mx", (P, D // 16), F32, "ExternalOutput")
        dram("dbg_q_v", (P, D), F32, "ExternalOutput")
        dram("dbg_q_qa", (P, D), BF16, "ExternalOutput")
        dram("dbg_delta", (E, T), BF16, "ExternalOutput")
        dram("dbg_y", (E, T), F32, "ExternalOutput")
        dram("dbg_dtlr", (R + 1, T), BF16, "ExternalOutput")

    def sb(name, shape, dt):
        return es.enter_context(nc.sbuf_tensor(name, list(shape), dt))

    AR_BYTES = 114688
    ar = sb("ar", (P, AR_BYTES // 2), BF16)

    def arb(off, n):
        assert off % 2 == 0 and off + 2 * n <= AR_BYTES, (off, n)
        return ar[:, off // 2: off // 2 + n]

    def arf(off, n):
        assert off % 4 == 0 and off + 4 * n <= AR_BYTES, (off, n)
        return ar[:, off // 2: off // 2 + 2 * n].bitcast(F32)

    def aln(v):
        return (v + 15) // 16 * 16

    o_xnT = 0
    o_acc1 = aln(o_xnT + DT * T * 2)
    o_acc2 = aln(o_acc1 + T * 4)
    o_et = aln(o_acc2 + T * 4)
    o_xin = aln(o_et + T * 4)
    o_ubuf = aln(o_xin + (T + 4) * 4)
    o_szb1 = aln(o_ubuf + 2 * T * 2)
    o_xnw = aln(o_szb1 + 2 * T * 2)
    end_p1 = o_xnw + 2 * D * 2
    o_dl = 0
    o_du = o_dl + GE * T * 2
    o_y = o_du + GE * T * 2
    o_ug = o_y + GE * T * 4
    o_dA = o_ug + GE * T * 2
    o_b = o_dA + 2 * T * 4
    o_h = o_b + 2 * T * 2
    o_bcB = o_h + 2 * T * 2
    o_bcC = o_bcB + 2 * T * 2
    o_szb2 = o_b
    end_p2 = o_bcC + 2 * T * 2

    def qblock(base, W):
        q = {}
        for nm, wdt, esz in (("absx", W, 4), ("v", W, 4), ("qa", W, 2), ("qb", W, 2),
                             ("sgn", W, 2), ("mx", W // 16, 4), ("mxc", W // 16, 4),
                             ("rs", W // 16, 4), ("sc", W // 16, 4)):
            q[nm] = (base, wdt)
            base = aln(base + wdt * esz)
        q["end"] = base
        return q

    o_wo = 0
    o_rt = aln(o_wo + ET * D * 2)
    o_gld = aln(o_rt + D * 4)
    o_q3 = aln(o_gld + 2 * ET * CW * 2)
    q3 = qblock(o_q3, D)
    o_x2b = aln(q3["end"])
    o_x2q = aln(o_x2b + D * 4)
    end_p3 = o_x2q + D * 2
    o_wgq = 0
    o_wuq = o_wgq + DT * I * 2
    o_hq = aln(o_wuq + DT * I * 2)
    o_hqT = aln(o_hq + I * 2)
    o_qf = aln(o_hqT + IT * P * 2)
    QFW = min(512, I)
    qf = qblock(o_qf, QFW)
    end_pfa = qf["end"]
    qb = qblock(0, I)
    o_outt = 0
    o_rt2 = aln(o_outt + 2 * D * 4)
    o_x2r = aln(o_rt2 + D * 4)
    o_wdq = aln(max(qb["end"], o_x2r + 2 * D * 4))
    o_hqr = aln(o_wdq + IT * D * 2)
    end_pfb = o_hqr + 2 * I * 2
    assert max(end_p1, end_p2, end_p3, end_pfa, end_pfb) <= AR_BYTES, (
        end_p1, end_p2, end_p3, end_pfa, end_pfb)

    ident = sb("ident_s", (P, P), BF16)
    nwbc = sb("nwbc", (P, D), F32)
    cwc = sb("cwc", (P, 4 * ET), F32)
    A_t = sb("A_t", (P, ET * N), F32)
    A_s = sb("A_s", (P, ET * N), F32)
    D_s = sb("D_s", (P, ET), F32)
    Wx_s = sb("Wx_s", (P, ET * 96), BF16)
    Wdt_s = sb("Wdt_s", (R + 1, E), BF16)
    dtlr_s = sb("dtlr_s", (R + 1, T), BF16)
    bcsb = sb("bcsb", (96, T), BF16)
    cols = sb("cols", (P, 32), F32)
    cst = sb("cst", (P, 8), F32)
    WSW = max(D, I, 1024) + 512
    wst = [sb(f"wst{i}", (P, WSW), F32) for i in range(2)]
    wbf = [sb(f"wbf{i}", (P, P), BF16) for i in range(4)]
    xt = [sb(f"xt{i}", (P, D), F32) for i in range(2)]
    scr = sb("scr", (P, max(D, I)), F32)
    evt = [sb(f"evt{i}", (P, P), BF16) for i in range(2)]
    xqt = [sb(f"xqt{i}", (P, DT * P), BF16) for i in range(2)]
    qo = sb("qo", (P, 2 * max(D, I, T, 1024)), BF16)

    ps = [es.enter_context(nc.psum_tensor(f"ps{i}", (P, 512), F32)) for i in range(8)]
    def psv(i, w=512):
        return ps[i][:, 0:w]
    def psb(i):
        return ps[i][:, 0:64].bitcast(BF16)

    s = {k: pr.sem(k) for k in (
        "bar", "sW", "sWc", "sXf", "sRst", "sXnw", "sTr", "sEvD", "sWbf",
        "sWuse", "sPsX", "sEvX", "sConv", "sSig", "sU", "sWxU", "sSZp",
        "sRows", "sRowsD", "sDpre", "sSPa", "sSPr", "sDel", "sDelV", "sDu",
        "sDA", "sScan", "sYm", "sGate", "sWo", "sMMo", "sStA", "sStB",
        "sX2", "sX2sp", "sQd", "sQo", "sQs", "sTr3", "sEv3", "sWq",
        "sWqh", "sWqT", "sMMg", "sSigF", "sHq", "sTrH", "sEvH", "sHsp",
        "sMMd", "sRes", "sLs1", "sLs2", "sLs3", "sDbg", "sCol")}

    MMKW = dict(skip_group_check=True)

    rX = DmaSlots(pr, "rX", 2)
    rW = DmaSlots(pr, "rW", 2)
    rUg = DmaSlots(pr, "rUg", GE)
    rBB = DmaSlots(pr, "rBB", 2)
    rCC = DmaSlots(pr, "rCC", 2)
    rSZ = DmaSlots(pr, "rSZ", 2)
    rGld = DmaSlots(pr, "rGld", 2)
    rXq = DmaSlots(pr, "rXq", 2)
    rHr = DmaSlots(pr, "rHr", 2)
    rX2r = DmaSlots(pr, "rX2r", 2)
    rUsp = DmaSlots(pr, "rUsp", 2)
    rSZsp = DmaSlots(pr, "rSZsp", 2)
    rGsp = DmaSlots(pr, "rGsp", GE)
    rXsp = DmaSlots(pr, "rXsp", 2)
    rOut = DmaSlots(pr, "rOut", 2)
    rWQ = DmaSlots(pr, "rWQ", 2)
    rWQsp = DmaSlots(pr, "rWQsp", 2)
    rWQ2 = DmaSlots(pr, "rWQ2", 2)

    def p0():
        loads = [
            (nwbc[:], nw_d.broadcast_to((P, D))),
            (cwc[:], convw),
            (A_t[:], alog),
            (D_s[:], dpar),
            (ident[:], idn),
            (wst[0][:, :ET * 96], w_x),
            (wst[1][:R + 1, :E], wdt_e),
        ]
        for dst, src in loads:
            pr.emit("sync", lambda e, d=dst, sr=src: e.dma_start(out=d, in_=sr),
                    incs=((s["sW"], 16),))
        w0 = s["sW"].n
        pr.emit("scalar", lambda e: e.activation(A_s[:], A_t[:], AF.Exp),
                waits=((s["sW"], w0),), incs=((s["sWc"], 1),))
        pr.emit("vector", lambda e: e.tensor_scalar_mul(A_t[:], A_s[:], -1.0),
                waits=((s["sWc"], s["sWc"].n),), incs=((s["sWc"], 1),))
        src = wst[0][:, :ET * 96]
        pr.emit("vector", lambda e, sr=src: e.tensor_copy(Wx_s[:], sr),
                waits=((s["sW"], w0),), incs=((s["sWc"], 1),))
        src = wst[1][:R + 1, :E]
        pr.emit("vector", lambda e, sr=src: e.tensor_copy(Wdt_s[:], sr),
                waits=((s["sW"], w0),), incs=((s["sWc"], 1),))
        pr.emit("vector", lambda e: e.memset(dtlr_s[R:R + 1, :], 1.0),
                incs=((s["sWc"], 1),))
        pr.emit("vector", lambda e, a=arf(o_xin, 4): e.memset(a, 0.0),
                incs=((s["sWc"], 1),))
        for ci, val in ((0, EPS_RMS), (1, EPS_LN), (2, 1.0 / D), (3, -0.5), (4, -1.0), (5, 1.0), (6, 0.5), (7, QTH[3])):
            pr.emit("vector", lambda e, a=cst[:, ci:ci + 1], v=val: e.memset(a, v),
                    incs=((s["sWc"], 1),))

    def xnT_tile(dt):
        return arb(o_xnT + dt * T * 2, T)

    def p1a():
        for ts in range(TS):
            sl = ts % 2
            src = x_in[ts * P:(ts + 1) * P, :]
            smx, vx = rX.issue(sl)
            pr.emit("sync", lambda e, d=xt[sl][:], sr=src: e.dma_start(out=d, in_=sr),
                    waits=((s["sXf"], ts - 1),), incs=((smx, 16),))
            c_ssq = cols[:, sl:sl + 1]
            pr.emit("scalar", lambda e, xi=xt[sl][:], c=c_ssq: e.activation(
                scr[:, :D], xi, AF.Square, accum_out=c),
                    waits=((smx, vx),))
            c_l = cols[:, 4 + sl:5 + sl]
            c_r = cols[:, 6 + sl:7 + sl]
            pr.emit("scalar", lambda e, a=c_l, b=c_ssq: e.activation(
                a, b, AF.Ln, bias=cst[:, 0:1], scale=cst[:, 2:3]),
                    waits=((s["sXf"], ts - 1),))
            pr.emit("scalar", lambda e, a=c_r, b=c_l: e.activation(a, b, AF.Exp, scale=cst[:, 3:4]),
                    incs=((s["sRst"], 1),))
            tmpf = arf(o_acc1, D)
            pr.emit("vector", lambda e, xi=xt[sl][:], c=c_r, t_=tmpf: e.tensor_scalar_mul(t_, xi, c),
                    waits=((s["sRst"], ts + 1),), incs=((s["sXf"], 1),))
            xnw = arb(o_xnw + sl * D * 2, D)
            pr.emit("vector", lambda e, t_=tmpf, o=xnw: e.tensor_mul(o, t_, nwbc[:]),
                    waits=((s["sEvD"], (ts - 1) * DT),), incs=((s["sXnw"], 1),))
            for dt in range(DT):
                pp = dt % 2
                src_t = arb(o_xnw + sl * D * 2 + dt * P * 2, P)
                pr.emit("tensor", lambda e, o=psb(4 + pp), i=src_t: e.transpose(o, i, ident[:]),
                        waits=((s["sXnw"], ts + 1), (s["sEvD"], s["sTr"].n - 1)),
                        incs=((s["sTr"], 1),))
                dst_t = xnT_tile(dt)[:, ts * P:(ts + 1) * P]
                pr.emit("vector", lambda e, o=dst_t, i=psb(4 + pp): e.tensor_copy(o, i),
                        waits=((s["sTr"], s["sTr"].n),), incs=((s["sEvD"], 1),))

    def p1b_half(half):
        def kloop(et):
            ib = half * ET + et
            for k in range(DT):
                li = ib * DT + k
                wsl = li % 4
                src = w_in[(k * 2 * ET + ib) * P:(k * 2 * ET + ib + 1) * P, :]
                smw, vw = rW.issue(li % 2)
                pr.emit("sync", lambda e, d=wst[li % 2][:, :P], sr=src: e.dma_start(out=d, in_=sr),
                        waits=((s["sWbf"], li - 1),), incs=((smw, 16),))
                pr.emit("vector", lambda e, o=wbf[wsl][:], i=wst[li % 2][:, :P]: e.tensor_copy(o, i),
                        waits=((smw, vw), (s["sWuse"], li - 3)),
                        incs=((s["sWbf"], 1),))
                for tc in range(NTC):
                    incs = []
                    if k == DT - 1:
                        incs.append((s["sPsX"], 1))
                    rhs = xnT_tile(k)[:, tc * TCH:(tc + 1) * TCH]
                    pr.emit("tensor", lambda e, o=psv(tc, TCH), l_=wbf[wsl][:], r_=rhs,
                            st=(k == 0), sp=(k == DT - 1): e.matmul(o, l_, r_, start=st, stop=sp, **MMKW),
                            waits=((s["sWbf"], s["sWbf"].n),
                                   (s["sEvX"], (ib - 1) * NTC + tc + 1)),
                            incs=tuple(incs))
                pr.emit("tensor", lambda e: e.nop(), incs=((s["sWuse"], 1),))

        def tail(et):
            ib = half * ET + et
            for tc in range(NTC):
                if half == 0:
                    dst = arf(o_xin + 16 + tc * TCH * 4, TCH)
                else:
                    dst = arf(o_acc1 + tc * TCH * 4, TCH)
                waits = [(s["sPsX"], ib * NTC + tc + 1)]
                if half == 1 and et == 0 and tc == 0:
                    waits.append((s["sU"], ET))
                if half == 1 and et > 0 and tc == 0:
                    waits.append((s["sSZp"], et))
                if half == 0 and et > 0 and tc == 0:
                    waits.append((s["sConv"], et))
                pr.emit("scalar", lambda e, d=dst, i=psv(tc, TCH): e.copy(d, i),
                        waits=tuple(waits), incs=((s["sEvX"], 1),))
            if half == 0:
                xin = lambda j: arf(o_xin + j * 4, T)
                wc = lambda j: cwc[:, et * 4 + j: et * 4 + j + 1]
                pr.emit("vector", lambda e, o=arf(o_acc1, T), i=xin(1), w=wc(0): e.tensor_scalar_mul(o, i, w),
                        waits=((s["sEvX"], (ib + 1) * NTC),))
                pr.emit("vector", lambda e, o=arf(o_acc2, T), i=xin(2), w=wc(1),
                        a=arf(o_acc1, T): e.scalar_tensor_tensor(o, i, w, a, ALU.mult, ALU.add))
                pr.emit("vector", lambda e, o=arf(o_acc1, T), i=xin(3), w=wc(2),
                        a=arf(o_acc2, T): e.scalar_tensor_tensor(o, i, w, a, ALU.mult, ALU.add))
                pr.emit("vector", lambda e, o=arf(o_acc2, T), i=xin(4), w=wc(3),
                        a=arf(o_acc1, T): e.scalar_tensor_tensor(o, i, w, a, ALU.mult, ALU.add),
                        incs=((s["sConv"], 1),))
                pr.emit("scalar", lambda e, o=arf(o_et, T), i=arf(o_acc2, T): e.activation(
                    o, i, AF.Tanh, scale=cst[:, 6:7]),
                        waits=((s["sConv"], s["sConv"].n),), incs=((s["sSig"], 1),))
                pr.emit("vector", lambda e, o=arf(o_acc1, T), i=arf(o_et, T): e.tensor_scalar(
                    o, i, 0.5, 0.5, ALU.mult, ALU.add),
                        waits=((s["sSig"], s["sSig"].n),))
                ub = arb(o_ubuf + (et % 2) * T * 2, T)
                pr.emit("vector", lambda e, o=ub, a=arf(o_acc2, T), b=arf(o_acc1, T): e.tensor_mul(o, a, b),
                        waits=((s["sWxU"], et - 1), (rUsp.s[et % 2], 16 * rUsp.u[et % 2])),
                        incs=((s["sU"], 1),))
                for tc in range(NTC):
                    rhs = ub[:, tc * TCH:(tc + 1) * TCH]
                    pr.emit("tensor", lambda e, o=ps[4 + tc][0:96, 0:TCH],
                            l_=Wx_s[:, et * 96:(et + 1) * 96], r_=rhs,
                            st=(et == 0), sp=(et == ET - 1): e.matmul(o, l_, r_, start=st, stop=sp, **MMKW),
                            waits=((s["sU"], et + 1),),
                            incs=((s["sWxU"], 1),) if tc == NTC - 1 else ())
                smu, _ = rUsp.issue(et % 2)
                pr.emit("gpsimd", lambda e, o=sc_u[et * P:(et + 1) * P, :], i=ub: e.dma_start(out=o, in_=i),
                        waits=((s["sU"], et + 1),), incs=((smu, 16),))
                emit_wq_tasks(2)
            elif True:
                pr.emit("scalar", lambda e, o=arf(o_et, T), i=arf(o_acc1, T): e.activation(
                    o, i, AF.Tanh, scale=cst[:, 6:7]),
                        waits=((s["sEvX"], (ib + 1) * NTC),), incs=((s["sSig"], 1),))
                pr.emit("vector", lambda e, o=arf(o_acc2, T), i=arf(o_et, T): e.tensor_scalar(
                    o, i, 0.5, 0.5, ALU.mult, ALU.add),
                        waits=((s["sSig"], s["sSig"].n),))
                szb = arb(o_szb1 + (et % 2) * T * 2, T)
                pr.emit("vector", lambda e, o=szb, a=arf(o_acc1, T), b=arf(o_acc2, T): e.tensor_mul(o, a, b),
                        waits=((rSZsp.s[et % 2], 16 * rSZsp.u[et % 2]),), incs=((s["sSZp"], 1),))
                smz, _ = rSZsp.issue(et % 2)
                pr.emit("gpsimd", lambda e, o=sc_sz[et * P:(et + 1) * P, :], i=szb: e.dma_start(out=o, in_=i),
                        waits=((s["sSZp"], s["sSZp"].n),), incs=((smz, 16),))
            emit_wq_tasks(2)

        for et in range(ET):
            kloop(et)
            if et > 0:
                tail(et - 1)
        tail(ET - 1)

    def p1c():
        for tc in range(NTC):
            pr.emit("scalar", lambda e, o=dtlr_s[0:R, tc * TCH:(tc + 1) * TCH],
                    i=ps[4 + tc][0:R, 0:TCH]: e.copy(o, i),
                    waits=((s["sWxU"], ET),))
            pr.emit("scalar", lambda e, o=bcsb[64:96, tc * TCH:(tc + 1) * TCH],
                    i=ps[4 + tc][64:96, 0:TCH]: e.copy(o, i),
                    incs=((s["sRows"], 1),))
        pr.emit("gpsimd", lambda e: e.dma_start(out=sc_rows[:], in_=bcsb[64:96, :]),
                waits=((s["sRows"], s["sRows"].n),), incs=((s["sRowsD"], 16),))
        emit_wq_collectives()
        if debug:
            pr.emit("gpsimd", lambda e: e.dma_start(out=dr["dbg_dtlr"][:], in_=dtlr_s[:]),
                    waits=((s["sRows"], NTC),), incs=((s["sDbg"], 16),))

    def p2(until=None):
        ROWS_RDY = NTC

        def dl_t(eh): return arb(o_dl + eh * T * 2, T)
        def du_t(eh): return arb(o_du + eh * T * 2, T)
        def y_e(eh): return arb(o_y + eh * T * 4, T)
        def y_o(eh): return arb(o_y + eh * T * 4 + T * 2, T)
        def ug_t(eh): return arb(o_ug + eh * T * 2, T)
        def dA_t(r2): return arb(o_dA + r2 * T * 2, T)
        def b_t(r2): return arb(o_b + r2 * T * 2, T)
        def h_t(r2): return arb(o_h + r2 * T * 2, T)
        def bcB(r2): return arb(o_bcB + r2 * T * 2, T)
        def bcC(r2): return arb(o_bcC + r2 * T * 2, T)

        for g in range(NG):
            if until == "p2a0":
                return
            for eh in range(GE):
                et = g * GE + eh
                smug, _ = rUg.issue(eh)
                pr.emit("sync", lambda e, o=ug_t(eh), i=sc_u[et * P:(et + 1) * P, :]: e.dma_start(out=o, in_=i),
                        waits=((rGsp.s[eh], 16 * g),
                               (rUsp.s[0], 16 * ((ET + 1) // 2)), (rUsp.s[1], 16 * (ET // 2))),
                        incs=((smug, 16),))
            if until == "p2a1":
                return
            if until == "p2b0":
                for tc in range(NTC):
                    rhs = dtlr_s[:, tc * TCH:(tc + 1) * TCH]
                    pr.emit("tensor", lambda e, o=psv(tc, TCH),
                            l_=Wdt_s[:, 0:P], r_=rhs: e.matmul(
                        o, l_, r_, start=True, stop=True, **MMKW),
                            waits=((s["sRows"], ROWS_RDY), (s["sRowsD"], 16)),
                            incs=((s["sDpre"], 1),))
                return
            for eh in range(GE):
                et = g * GE + eh
                j = g * GE + eh
                for tc in range(NTC):
                    rhs = dtlr_s[:, tc * TCH:(tc + 1) * TCH]
                    free_v = (j - 1) * NTC + tc + 1
                    pr.emit("tensor", lambda e, o=psv(tc, TCH),
                            l_=Wdt_s[:, et * P:(et + 1) * P], r_=rhs: e.matmul(
                        o, l_, r_, start=True, stop=True, **MMKW),
                            waits=((s["sRows"], ROWS_RDY), (s["sRowsD"], 16),
                                   (s["sSPa"], free_v), (s["sSPr"], free_v)),
                            incs=((s["sDpre"], 1),))
                for tc in range(NTC):
                    pw = j * NTC + tc + 1
                    spA = wst[0][:, tc * TCH:(tc + 1) * TCH]
                    spB = wst[1][:, tc * TCH:(tc + 1) * TCH]
                    spP = scr[:, tc * TCH:(tc + 1) * TCH]
                    rlb = qo[:, tc * TCH:(tc + 1) * TCH]
                    wq_guard = ((rWQsp.s[0], 16 * rWQsp.u[0]),
                                (rWQsp.s[1], 16 * rWQsp.u[1])) if (g == 0 and j == 0 and tc == 0) else ()
                    pr.emit("scalar", lambda e, o=spP, i=psv(tc, TCH): e.copy(o, i),
                            waits=((s["sDpre"], pw), (s["sSPr"], (j - 1) * NTC + tc + 1),
                                   (s["sDelV"], (j - 1) * NTC + tc + 1)) + wq_guard,
                            incs=((s["sSPa"], 1),))
                    pr.emit("scalar", lambda e, o=spA, i=spP: e.activation(o, i, AF.Abs))
                    pr.emit("scalar", lambda e, o=spB, i=spA: e.activation(o, i, AF.Exp, scale=cst[:, 4:5]))
                    pr.emit("scalar", lambda e, o=spA, i=spB: e.activation(o, i, AF.Ln, bias=cst[:, 5:6]),
                            incs=((s["sDel"], 1),))
                    pr.emit("vector", lambda e, o=rlb, i=spP: e.tensor_scalar_max(o, i, 0.0),
                            waits=((s["sDel"], s["sDel"].n),), incs=((s["sSPr"], 1),))
                    pr.emit("vector", lambda e, o=dl_t(eh)[:, tc * TCH:(tc + 1) * TCH],
                            a=rlb, b=spA: e.tensor_tensor(o, a, b, ALU.add),
                            incs=((s["sDelV"], 1),))
                pr.emit("vector", lambda e, o=du_t(eh), a=dl_t(eh), b=ug_t(eh): e.tensor_mul(o, a, b),
                        waits=((rUg.s[eh], 16 * (g + 1)),), incs=((s["sDu"], 1),))
                if debug:
                    pr.emit("gpsimd", lambda e, o=dr["dbg_delta"][et * P:(et + 1) * P, :],
                            i=dl_t(eh): e.dma_start(out=o, in_=i),
                            waits=((s["sDu"], s["sDu"].n),), incs=((s["sDbg"], 16),))
            if until in ("p2b", "p2b1", "p2b2", "p2b3", "p2b4", "p2b0", "p2n1", "p2n2", "p2n3", "p2n4", "p2n5"):
                return
            dgrp = (g + 1) * GE * NTC
            for n in range(N if until != "p2c" else 1):
                nsl = n % 2
                gi = g * N + n
                smb, vbB = rBB.issue(nsl)
                pr.emit("sync", lambda e, o=bcB(nsl),
                        i=sc_rows[n:n + 1, :].broadcast_to((P, T)): e.dma_start(out=o, in_=i),
                        waits=((s["sRows"], ROWS_RDY), (s["sRowsD"], 16),
                               (s["sYm"], (gi - 1) * GE)),
                        incs=((smb, 16),))
                smc, vbC = rCC.issue(nsl)
                pr.emit("sync", lambda e, o=bcC(nsl),
                        i=sc_rows[16 + n:16 + n + 1, :].broadcast_to((P, T)): e.dma_start(out=o, in_=i),
                        incs=((smc, 16),))
                for eh in range(GE):
                    et = g * GE + eh
                    it = gi * GE + eh
                    r2 = it % 2
                    pr.emit("scalar", lambda e, o=dA_t(r2), i=dl_t(eh),
                            a=A_t[:, et * N + n: et * N + n + 1]: e.activation(o, i, AF.Exp, scale=a),
                            waits=((s["sDelV"], dgrp), (s["sScan"], it - 1)),
                            incs=((s["sDA"], 1),))
                    pr.emit("vector", lambda e, o=b_t(r2), a=du_t(eh), b=bcB(nsl): e.tensor_mul(o, a, b),
                            waits=((s["sDu"], (g + 1) * GE), (smb, vbB),
                                   (s["sScan"], it - 1)))
                    pr.emit("vector", lambda e, o=h_t(r2), a=dA_t(r2), b=b_t(r2): e.tensor_tensor_scan(
                        o, a, b, 0.0, ALU.mult, ALU.add),
                            waits=((s["sDA"], it + 1), (s["sYm"], it - 1)),
                            incs=((s["sScan"], 1),))
                    yacc = y_e(eh) if n % 2 == 0 else y_o(eh)
                    if n < 2:
                        pr.emit("vector", lambda e, o=yacc, a=h_t(r2), b=bcC(nsl): e.tensor_mul(o, a, b),
                                waits=((smc, vbC),), incs=((s["sYm"], 1),))
                    else:
                        pr.emit("vector", lambda e, o=h_t(r2), a=h_t(r2), b=bcC(nsl): e.tensor_mul(o, a, b),
                                waits=((smc, vbC),))
                        pr.emit("vector", lambda e, o=yacc, a=yacc, b=h_t(r2): e.tensor_tensor(
                            o, a, b, ALU.add), incs=((s["sYm"], 1),))
            szv = {}
            for eh in range(GE):
                et = g * GE + eh
                szv[eh] = rSZ.issue(eh % 2)
                pr.emit("sync", lambda e, o=arb(o_szb2 + (eh % 2) * T * 2, T),
                        i=sc_sz[et * P:(et + 1) * P, :]: e.dma_start(out=o, in_=i),
                        waits=((rSZsp.s[0], 16 * ((ET + 1) // 2)), (rSZsp.s[1], 16 * (ET // 2)),
                               (s["sGate"], g * GE + eh - 1),
                               (s["sScan"], (g + 1) * N * GE)),
                        incs=((szv[eh][0], 16),))
            ym_grp = (g * N + N - 1) * GE
            for eh in range(GE):
                et = g * GE + eh
                if debug:
                    pr.emit("gpsimd", lambda e, o=dr["dbg_y"][et * P:(et + 1) * P, :],
                            i=y_e(eh): e.dma_start(out=o, in_=i),
                            waits=((s["sYm"], ym_grp + eh + 1),), incs=((s["sDbg"], 16),))
                yf = arf(o_h, T)
                pr.emit("vector", lambda e, o=yf, i=ug_t(eh), d=D_s[:, et:et + 1],
                        y=y_e(eh): e.scalar_tensor_tensor(o, i, d, y, ALU.mult, ALU.add),
                        waits=((s["sYm"], ym_grp + eh + 1),))
                pr.emit("vector", lambda e, o=yf, a=yf, b=y_o(eh): e.tensor_tensor(o, a, b, ALU.add))
                pr.emit("vector", lambda e, o=ug_t(eh), a=yf,
                        b=arb(o_szb2 + (eh % 2) * T * 2, T): e.tensor_mul(o, a, b),
                        waits=(szv[eh],),
                        incs=((s["sGate"], 1),))
                smg, _ = rGsp.issue(eh)
                pr.emit("gpsimd", lambda e, o=sc_g[et * P:(et + 1) * P, :], i=ug_t(eh): e.dma_start(out=o, in_=i),
                        waits=((s["sGate"], g * GE + eh + 1),), incs=((smg, 16),))

    import os
    QDBG = {"n": 0, "tgt": int(os.environ.get("QDBG_N", "0"))}

    def quant(src_ap, out_ap, W, q, waits_first=(), waits_dve=(), incs_last=()):
        B = W // 16
        ab = arf(*q["absx"])[:, :W]
        vv = arf(*q["v"])[:, :W]
        qa = arb(*q["qa"])[:, :W]
        qb_ = arb(*q["qb"])[:, :W]
        sg = arb(*q["sgn"])[:, :W]
        mx = arf(*q["mx"])[:, :B]
        mxc = arf(*q["mxc"])[:, :B]
        rs = arf(*q["rs"])[:, :B]
        scq = arf(*q["sc"])[:, :B]
        prev = s["sQd"].n
        dbgw = ((s["sDbg"], s["sDbg"].n),) if debug else ()
        pr.emit("scalar", lambda e, o=ab, i=src_ap: e.activation(o, i, AF.Abs),
                waits=tuple(waits_first) + ((s["sQd"], prev),) + dbgw, incs=((s["sQd"], 1),))
        pr.emit("scalar", lambda e, o=sg, i=src_ap: e.activation(o, i, AF.Sign))
        pr.emit("scalar", lambda e, o=cols[:, 24:25], i=ab[:, W - 1:W]: e.copy(o, i))
        pr.emit("scalar", lambda e, o=cols[:, 25:26], i=sg[:, W - 1:W]: e.copy(o, i),
                incs=((s["sQd"], 1),))
        w0 = s["sQd"].n
        pr.emit("vector", lambda e, o=mx.unsqueeze(2),
                i=ab.rearrange("p (b i) -> p b i", i=16): e.tensor_reduce(
            o, i, mybir.AxisListType.X, ALU.max),
                waits=((s["sQd"], w0),), incs=((s["sQs"], 1),))
        pr.emit("vector", lambda e, o=mxc, i=mx: e.tensor_scalar_max(o, i, 6e-12),
                waits=((s["sQs"], s["sQs"].n),), incs=((s["sQs"], 1),))
        pr.emit("vector", lambda e, o=rs, i=mxc: e.reciprocal(o, i),
                waits=((s["sQs"], s["sQs"].n),))
        pr.emit("vector", lambda e, o=scq, i=mxc: e.tensor_scalar_mul(o, i, 1.0 / 6.0),
                incs=((s["sQs"], 1),))
        bv = lambda t_: t_.rearrange("p (b i) -> p b i", i=16)
        rsb = rs.unsqueeze(2).broadcast_to((P, B, 16))
        scb = scq.unsqueeze(2).broadcast_to((P, B, 16))
        pr.emit("vector", lambda e, o=bv(vv), a=bv(ab), b=rsb: e.tensor_tensor(o, a, b, ALU.mult),
                waits=((s["sQs"], s["sQs"].n),))
        pr.emit("vector", lambda e, o=qa, i=vv: e.tensor_scalar(o, i, QTH[0], QW[0], ALU.is_gt, ALU.mult))
        for j in range(1, 7):
            pr.emit("vector", lambda e, o=qb_, i=vv, t_=QTH[j], w=QW[j]: e.tensor_scalar(
                o, i, t_, w, ALU.is_gt, ALU.mult))
            pr.emit("vector", lambda e, o=qa, b=qb_: e.tensor_tensor(o, o, b, ALU.add))
        take = debug and W == D and QDBG["n"] == QDBG["tgt"]
        QDBG["n"] += 1
        if take:
            pr.emit("gpsimd", lambda e, i=ab: e.dma_start(out=dr["dbg_q_ab"], in_=i),
                    waits=((s["sQd"], s["sQd"].n),), incs=((s["sDbg"], 16),))
            pr.emit("gpsimd", lambda e, i=mx: e.dma_start(out=dr["dbg_q_mx"], in_=i),
                    incs=((s["sDbg"], 16),))
            pr.emit("gpsimd", lambda e, i=vv: e.dma_start(out=dr["dbg_q_v"], in_=i),
                    incs=((s["sDbg"], 16),))
            pr.emit("gpsimd", lambda e, i=qa: e.dma_start(out=dr["dbg_q_qa"], in_=i),
                    incs=((s["sDbg"], 16),))
        pr.emit("vector", lambda e, o=qa, b=sg: e.tensor_mul(o, o, b),
                incs=((s["sQd"], 1),))
        assert len(incs_last) <= 1
        pr.emit("vector", lambda e, o=bv(out_ap), a=bv(qa), b=scb: e.tensor_tensor(o, a, b, ALU.mult),
                waits=tuple(waits_dve), incs=tuple(incs_last))

    QHW = 1408 if I % 1408 == 0 or I >= 1408 else I
    qh = qblock(aln(end_p1), min(1408, I))
    assert qh["end"] <= AR_BYTES
    QOS = max(D, I, T, 1024)

    def _row_tiles(rows):
        off, out = 0, []
        while off < rows:
            out.append((off, min(P, rows - off)))
            off += P
        return out

    wq_tasks = []
    for off, h in _row_tiles(MYI if WQ_SHARD else I):
        wq_tasks.append(("wg", off, h))
    for off, h in _row_tiles(MYI if WQ_SHARD else I):
        wq_tasks.append(("wu", off, h))
    for off, h in _row_tiles(MYD if WQ_SHARD else D):
        wq_tasks.append(("wd", off, h))
    wq_state = {"i": 0}

    def emit_wq_tasks(k):
        for _ in range(k):
            qi = wq_state["i"]
            if qi >= len(wq_tasks):
                return
            wq_state["i"] += 1
            kind, roff, h = wq_tasks[qi]
            sl = qi % 2
            if kind == "wd":
                srcd, W = wd_d[roff:roff + h, :], I
            else:
                srcd = (wg_d if kind == "wg" else wu_d)[roff:roff + h, :]
                W = D
            stg = wst[sl][:, P:P + W]
            smq, vq = rWQ.issue(sl)
            pr.emit("sync", lambda e, o=stg[0:h, :], i=srcd: e.dma_start(out=o, in_=i),
                    waits=((s["sQd"], s["sQd"].n - 4),), incs=((smq, 16),))
            oslot = qo[:, sl * QOS: sl * QOS + W]
            halves = [(0, W)] if W <= 1408 else [(0, W // 2), (W // 2, W // 2)]
            for hoff, hw in halves:
                quant(stg[:, hoff:hoff + hw], oslot[:, hoff:hoff + hw], hw, qh,
                      waits_first=((smq, vq),),
                      waits_dve=((rWQsp.s[sl], 16 * rWQsp.u[sl]),),
                      incs_last=((s["sWqh"], 1),))
            if WQ_SHARD:
                dstd = {"wg": sc_wgq_my, "wu": sc_wuq_my, "wd": sc_wdq_my}[kind]
            else:
                dstd = {"wg": sc_wgq, "wu": sc_wuq, "wd": sc_wdq}[kind]
            smsp, _ = rWQsp.issue(sl)
            pr.emit("gpsimd", lambda e, o=dstd[roff:roff + h, :], i=oslot[0:h, :]: e.dma_start(out=o, in_=i),
                    waits=((s["sWqh"], s["sWqh"].n),), incs=((smsp, 16),))

    def emit_wq_collectives():
        emit_wq_tasks(len(wq_tasks))
        if not WQ_SHARD:
            return
        rgs = [list(range(n_cores))]
        wqw = ((rWQsp.s[0], 16 * rWQsp.u[0]), (rWQsp.s[1], 16 * rWQsp.u[1]))
        for my_t, full_t in ((sc_wgq_my, sc_wgq), (sc_wuq_my, sc_wuq), (sc_wdq_my, sc_wdq)):
            pr.emit("gpsimd", lambda e, i=my_t, o=full_t: e.collective_compute(
                "AllGather", ALU.bypass, rgs, ins=[i], outs=[o]),
                    waits=wqw, incs=((s["sCol"], 1),))

    def p3():
        def wo_t(et): return arb(o_wo + et * D * 2, D)
        for et in range(ET):
            sl = et % 2
            smw, vw = rW.issue(sl)
            pr.emit("sync", lambda e, o=wst[sl][:, :D],
                    i=wout_d[et * P:(et + 1) * P, :]: e.dma_start(out=o, in_=i),
                    waits=((s["sWo"], et - 1),), incs=((smw, 16),))
            pr.emit("vector", lambda e, o=wo_t(et), i=wst[sl][:, :D]: e.tensor_copy(o, i),
                    waits=((smw, vw),), incs=((s["sWo"], 1),))
        for ch in range(NCH):
            cs = ch % 2
            def gld(et, cs=cs):
                return arb(o_gld + (cs * ET + et) * CW * 2, CW)
            gw = tuple((rGsp.s[eh], 16 * NG) for eh in range(GE))
            vgld = 0
            for et in range(ET):
                smgl, vgld = rGld.issue(cs)
                pr.emit("sync", lambda e, o=gld(et),
                        i=sc_g[et * P:(et + 1) * P, ch * CW:(ch + 1) * CW]: e.dma_start(out=o, in_=i),
                        waits=gw + ((s["sMMo"], (ch - 1) * TS2 * NDH),),
                        incs=((smgl, 16),))
            for t2 in range(TS2):
                ts = ch * TS2 + t2
                for dh in range(NDH):
                    bank = t2 * NDH + dh
                    for et in range(ET):
                        lhs = gld(et)[:, t2 * P:(t2 + 1) * P]
                        rhs = wo_t(et)[:, dh * 512:dh * 512 + DW]
                        pr.emit("tensor", lambda e, o=psv(bank, DW), l_=lhs, r_=rhs,
                                st=(et == 0), sp=(et == ET - 1): e.matmul(o, l_, r_, start=st, stop=sp, **MMKW),
                                waits=((rGld.s[cs], vgld), (s["sWo"], ET),
                                       (s["sX2"], ts - TS2 + 1)),
                                incs=((s["sMMo"], 1),) if et == ET - 1 else ())
                smx3, vx3 = rX.issue(ts % 2)
                pr.emit("sync", lambda e, o=xt[ts % 2][:],
                        i=x_in[ts * P:(ts + 1) * P, :]: e.dma_start(out=o, in_=i),
                        waits=((s["sX2"], ts - 1),), incs=((smx3, 16),))
                mmw = s["sMMo"].n
                for dh in range(NDH):
                    bank = t2 * NDH + dh
                    rt_h = arf(o_rt + dh * 512 * 4, DW)
                    xr_h = xt[ts % 2][:, dh * 512:dh * 512 + DW]
                    pr.emit("vector", lambda e, o=rt_h, i=xr_h, p_=psv(bank, DW): e.scalar_tensor_tensor(
                        o, i, 2.0, p_, ALU.mult, ALU.add),
                            waits=((s["sMMo"], mmw), (smx3, vx3)),
                            incs=((s["sStA"], 1),) if dh == NDH - 1 else ())
                rt = arf(o_rt, D)
                pr.emit("scalar", lambda e, i=rt: e.activation(scr[:, :D], i, AF.Identity,
                                                               accum_out=cols[:, 8:9]),
                        waits=((s["sStA"], s["sStA"].n),))
                pr.emit("scalar", lambda e, i=rt: e.activation(scr[:, :D], i, AF.Square,
                                                               accum_out=cols[:, 9:10]))
                pr.emit("scalar", lambda e: e.mul(cols[:, 10:11], cols[:, 8:9], cst[:, 2:3]))
                pr.emit("scalar", lambda e: e.activation(cols[:, 11:12], cols[:, 10:11], AF.Square))
                pr.emit("scalar", lambda e: e.mul(cols[:, 12:13], cols[:, 9:10], cst[:, 2:3]),
                        incs=((s["sLs1"], 1),))
                pr.emit("vector", lambda e: e.tensor_tensor(cols[:, 13:14], cols[:, 12:13],
                                                            cols[:, 11:12], ALU.subtract),
                        waits=((s["sLs1"], s["sLs1"].n),), incs=((s["sLs2"], 1),))
                pr.emit("scalar", lambda e: e.activation(cols[:, 14:15], cols[:, 13:14], AF.Ln, bias=cst[:, 1:2]),
                        waits=((s["sLs2"], s["sLs2"].n),))
                pr.emit("scalar", lambda e: e.activation(cols[:, 15:16], cols[:, 14:15], AF.Exp, scale=cst[:, 3:4]),
                        incs=((s["sLs3"], 1),))
                x2b = arf(o_x2b, D)
                x2q = arb(o_x2q, D)
                pr.emit("vector", lambda e, o=x2b, i=rt: e.tensor_scalar(
                    o, i, cols[:, 10:11], cols[:, 15:16], ALU.subtract, ALU.mult),
                        waits=((s["sLs3"], s["sLs3"].n), (s["sQd"], s["sQd"].n),
                               (s["sX2sp"], s["sX2sp"].n)), incs=((s["sStB"], 1),))
                pr.emit("gpsimd", lambda e, o=sc_x2[ts * P:(ts + 1) * P, :], i=x2b: e.dma_start(out=o, in_=i),
                        waits=((s["sStB"], s["sStB"].n),), incs=((s["sX2sp"], 16),))
                quant(x2b, x2q, D, q3,
                      waits_first=((s["sStB"], s["sStB"].n),),
                      waits_dve=((s["sEv3"], s["sEv3"].n),),
                      incs_last=((s["sQo"], 1),))
                qd0 = s["sQo"].n
                for dt in range(DT):
                    pp = dt % 2
                    pr.emit("tensor", lambda e, o=psb(4 + pp), i=x2q[:, dt * P:(dt + 1) * P]: e.transpose(
                        o, i, ident[:]),
                            waits=((s["sQo"], qd0), (s["sEv3"], s["sTr3"].n - 1)),
                            incs=((s["sTr3"], 1),))
                    pr.emit("vector", lambda e, o=evt[pp][:], i=psb(4 + pp): e.tensor_copy(o, i),
                            waits=((s["sTr3"], s["sTr3"].n), (rXsp.s[pp], 16 * rXsp.u[pp])),
                            incs=((s["sEv3"], 1),))
                    smxs, _ = rXsp.issue(pp)
                    pr.emit("gpsimd", lambda e, o=sc_xqT[dt * P:(dt + 1) * P, ts * P:(ts + 1) * P],
                            i=evt[pp][:]: e.dma_start(out=o, in_=i),
                            waits=((s["sEv3"], s["sEv3"].n),), incs=((smxs, 16),))
                pr.emit("vector", lambda e: e.nop(), incs=((s["sX2"], 1),))

    def pf():
        def wgq_t(dt): return arb(o_wgq + dt * I * 2, I)
        def wuq_t(dt): return arb(o_wuq + dt * I * 2, I)
        def wdq_t(kt): return arb(o_wdq + kt * D * 2, D)
        qpf0 = qblock(o_hq, D)
        wq_rdy = (((s["sCol"], s["sCol"].n),) if WQ_SHARD else
                  ((rWQsp.s[0], 16 * rWQsp.u[0]), (rWQsp.s[1], 16 * rWQsp.u[1])))
        for wi, (wspl, dstf) in enumerate(((sc_wgq, wgq_t), (sc_wuq, wuq_t))):
            for it in range(IT):
                li = wi * IT + it
                sl = li % 2
                qslot = qo[:, sl * QOS: sl * QOS + D]
                smw, vw = rWQ2.issue(sl)
                pr.emit("sync", lambda e, o=qslot,
                        i=wspl[it * P:(it + 1) * P, :]: e.dma_start(out=o, in_=i),
                        waits=wq_rdy + ((s["sEvH"], (li - 1) * DT),),
                        incs=((smw, 16),))
                for dt in range(DT):
                    pp = dt % 2
                    pr.emit("tensor", lambda e, o=psb(4 + pp), i=qslot[:, dt * P:(dt + 1) * P]: e.transpose(
                        o, i, ident[:]),
                            waits=((smw, vw), (s["sEvH"], s["sTrH"].n - 1)),
                            incs=((s["sTrH"], 1),))
                    pr.emit("vector", lambda e, o=dstf(dt)[:, it * P:(it + 1) * P],
                            i=psb(4 + pp): e.tensor_copy(o, i),
                            waits=((s["sTrH"], s["sTrH"].n),), incs=((s["sEvH"], 1),))
        pr.barrier(s["bar"])
        if debug:
            pr.emit("gpsimd", lambda e, i=wgq_t(0): e.dma_start(out=dr["dbg_wgq"], in_=i),
                    incs=((s["sDbg"], 16),))
        for ts in range(TS):
            sl = ts % 2
            src = sc_xqT[:, ts * P:(ts + 1) * P].rearrange("(dt p) c -> dt p c", p=P).transpose([1, 0, 2])
            dst3 = xqt[sl][:].rearrange("p (dt c) -> p dt c", c=P)
            smxq, vxq = rXq.issue(sl)
            pr.emit("sync", lambda e, o=dst3, i=src: e.dma_start(out=o, in_=i),
                    waits=((rXsp.s[0], 16 * TS * DT // 2), (rXsp.s[1], 16 * TS * DT // 2),
                           (s["sMMg"], (ts - 1) * 2 * NNC)),
                    incs=((smxq, 16),))
            for ni, (noff, nwd) in enumerate(ncols):
                nig = ts * NNC + ni
                rot3 = nig % 3
                bg = (0, 1, 6)[rot3]
                bu = (2, 3, 7)[rot3]
                for wq_t, bank in ((wgq_t, bg), (wuq_t, bu)):
                    for k in range(DT):
                        lhs = xqt[sl][:, k * P:(k + 1) * P]
                        rhs = wq_t(k)[:, noff:noff + nwd]
                        pr.emit("tensor", lambda e, o=psv(bank, nwd), l_=lhs, r_=rhs,
                                st=(k == 0), sp=(k == DT - 1): e.matmul(o, l_, r_, start=st, stop=sp, **MMKW),
                                waits=((smxq, vxq),
                                       (s["sSigF"], 3 * (nig - 2))),
                                incs=((s["sMMg"], 1),) if k == DT - 1 else ())
                mmw = s["sMMg"].n
                spE = wst[0][:, 0:nwd]
                spD_ = wst[0][:, 512:512 + nwd].bitcast(BF16)[:, 0:nwd]
                g_s = wst[1][:, 0:nwd].bitcast(BF16)[:, 0:nwd]
                u_s = wst[1][:, 512:512 + nwd].bitcast(BF16)[:, 0:nwd]
                hhb = wst[0][:, 1024:1024 + nwd]
                pr.emit("scalar", lambda e, o=g_s, i=psv(bg, nwd): e.copy(o, i),
                        waits=((s["sMMg"], mmw), (s["sSigF"], 3 * nig)))
                pr.emit("scalar", lambda e, o=u_s, i=psv(bu, nwd): e.copy(o, i),
                        incs=((s["sSigF"], 1),))
                pr.emit("scalar", lambda e, o=spE, i=g_s: e.activation(o, i, AF.Tanh, scale=cst[:, 6:7]),
                        incs=((s["sSigF"], 1),))
                pr.emit("vector", lambda e, o=spD_, i=spE: e.tensor_scalar(
                    o, i, 0.5, 0.5, ALU.mult, ALU.add),
                        waits=((s["sSigF"], 3 * nig + 2),))
                pr.emit("vector", lambda e, o=spD_, a=g_s, b=spD_: e.tensor_mul(o, a, b))
                pr.emit("vector", lambda e, o=hhb, a=spD_, b=u_s: e.tensor_mul(o, a, b),
                        incs=((s["sSigF"], 1),))
                quant(hhb, arb(o_hq + noff * 2, nwd), nwd, qf,
                      waits_first=((s["sSigF"], s["sSigF"].n),),
                      waits_dve=((s["sEvH"], s["sEvH"].n),),
                      incs_last=((s["sHq"], 1),))
            hq0 = s["sHq"].n
            if debug and ts == 0:
                pr.emit("gpsimd", lambda e, i=arb(o_hq, I): e.dma_start(out=dr["dbg_hq"], in_=i),
                        waits=((s["sHq"], hq0),), incs=((s["sDbg"], 16),))
            for kt in range(IT):
                pp = kt % 2
                pr.emit("tensor", lambda e, o=psb(4 + pp), i=arb(o_hq + kt * P * 2, P): e.transpose(
                    o, i, ident[:]),
                        waits=((s["sHq"], hq0), (s["sEvH"], s["sTrH"].n - 1)),
                        incs=((s["sTrH"], 1),))
                pr.emit("vector", lambda e, o=arb(o_hqT + kt * P * 2, P), i=psb(4 + pp): e.tensor_copy(o, i),
                        waits=((s["sTrH"], s["sTrH"].n), (s["sHsp"], 16 * ts)),
                        incs=((s["sEvH"], 1),))
            dst = sc_hqT[:, ts * P:(ts + 1) * P].rearrange("(kt p) c -> kt p c", p=P).transpose([1, 0, 2])
            src_h = arb(o_hqT, IT * P).rearrange("p (kt c) -> p kt c", c=P)
            pr.emit("gpsimd", lambda e, o=dst, i=src_h: e.dma_start(out=o, in_=i),
                    waits=((s["sEvH"], s["sEvH"].n),), incs=((s["sHsp"], 16),))
        pr.barrier(s["bar"])
        for dt in range(DT):
            sl = dt % 2
            qslot = qo[:, sl * QOS: sl * QOS + I]
            smw, vw = rWQ2.issue(sl)
            evb = s["sEvH"].n - IT if dt >= 2 else 0
            pr.emit("sync", lambda e, o=qslot,
                    i=sc_wdq[dt * P:(dt + 1) * P, :]: e.dma_start(out=o, in_=i),
                    waits=((s["sEvH"], evb),), incs=((smw, 16),))
            for kt in range(IT):
                pp = kt % 2
                pr.emit("tensor", lambda e, o=psb(4 + pp), i=qslot[:, kt * P:(kt + 1) * P]: e.transpose(
                    o, i, ident[:]),
                        waits=((smw, vw), (s["sEvH"], s["sTrH"].n - 1)),
                        incs=((s["sTrH"], 1),))
                pr.emit("vector", lambda e, o=wdq_t(kt)[:, dt * P:(dt + 1) * P], i=psb(4 + pp): e.tensor_copy(o, i),
                        waits=((s["sTrH"], s["sTrH"].n),), incs=((s["sEvH"], 1),))
        pr.emit("vector", lambda e: e.nop(), incs=((s["sWqT"], 1),))
        wq_all = s["sWq"].n
        trh_all = s["sTrH"].n
        for ts in range(TS):
            sl = ts % 2
            src = sc_hqT[:, ts * P:(ts + 1) * P].rearrange("(kt p) c -> kt p c", p=P).transpose([1, 0, 2])
            dsth = arb(o_hqr + sl * I * 2, I).rearrange("p (kt c) -> p kt c", c=P)
            smhr, vhr = rHr.issue(sl)
            pr.emit("sync", lambda e, o=dsth, i=src: e.dma_start(out=o, in_=i),
                    waits=((s["sHsp"], 16 * TS), (s["sMMd"], (ts - 1) * NDH)),
                    incs=((smhr, 16),))
            smx2, vx2 = rX2r.issue(sl)
            pr.emit("sync", lambda e, o=arf(o_x2r + sl * D * 4, D),
                    i=sc_x2[ts * P:(ts + 1) * P, :]: e.dma_start(out=o, in_=i),
                    waits=((s["sX2sp"], 16 * TS), (s["sRes"], (ts - 1) * NDH),
                           (s["sWq"], wq_all), (s["sTrH"], trh_all)),
                    incs=((smx2, 16),))
            for dh in range(NDH):
                bank = (ts % 2) * 2 + dh
                for kt in range(IT):
                    lhs = arb(o_hqr + sl * I * 2 + kt * P * 2, P)
                    rhs = wdq_t(kt)[:, dh * 512:dh * 512 + DW]
                    pr.emit("tensor", lambda e, o=psv(bank, DW), l_=lhs, r_=rhs,
                            st=(kt == 0), sp=(kt == IT - 1): e.matmul(o, l_, r_, start=st, stop=sp, **MMKW),
                            waits=((smhr, vhr), (s["sWqT"], 1),
                                   (s["sRes"], (ts - 2) * NDH + dh + 1)),
                            incs=((s["sMMd"], 1),) if kt == IT - 1 else ())
            mmw = s["sMMd"].n
            rt2 = arf(o_rt2, D)
            for dh in range(NDH):
                bank = (ts % 2) * 2 + dh
                rt2h = arf(o_rt2 + dh * 512 * 4, DW)
                x2h = arf(o_x2r + sl * D * 4 + dh * 512 * 4, DW)
                pr.emit("vector", lambda e, o=rt2h, a=x2h, p_=psv(bank, DW): e.tensor_tensor(
                    o, a, p_, ALU.add),
                        waits=((s["sMMd"], mmw), (smx2, vx2)),
                        incs=((s["sRes"], 1),))
            pr.emit("scalar", lambda e, i=rt2: e.activation(scr[:, :D], i, AF.Identity,
                                                            accum_out=cols[:, 16:17]),
                    waits=((s["sRes"], s["sRes"].n),))
            pr.emit("scalar", lambda e, i=rt2: e.activation(scr[:, :D], i, AF.Square,
                                                            accum_out=cols[:, 17:18]))
            pr.emit("scalar", lambda e: e.mul(cols[:, 18:19], cols[:, 16:17], cst[:, 2:3]))
            pr.emit("scalar", lambda e: e.activation(cols[:, 19:20], cols[:, 18:19], AF.Square))
            pr.emit("scalar", lambda e: e.mul(cols[:, 20:21], cols[:, 17:18], cst[:, 2:3]),
                    incs=((s["sLs1"], 1),))
            pr.emit("vector", lambda e: e.tensor_tensor(cols[:, 21:22], cols[:, 20:21],
                                                        cols[:, 19:20], ALU.subtract),
                    waits=((s["sLs1"], s["sLs1"].n),), incs=((s["sLs2"], 1),))
            pr.emit("scalar", lambda e: e.activation(cols[:, 22:23], cols[:, 21:22], AF.Ln, bias=cst[:, 1:2]),
                    waits=((s["sLs2"], s["sLs2"].n),))
            pr.emit("scalar", lambda e: e.activation(cols[:, 23:24], cols[:, 22:23], AF.Exp, scale=cst[:, 3:4]),
                    incs=((s["sLs3"], 1),))
            ot = arf(o_outt + (ts % 2) * D * 4, D)
            pr.emit("vector", lambda e, o=ot, i=rt2: e.tensor_scalar(
                o, i, cols[:, 18:19], cols[:, 23:24], ALU.subtract, ALU.mult),
                    waits=((s["sLs3"], s["sLs3"].n), (rOut.s[ts % 2], 16 * rOut.u[ts % 2])),
                    incs=((s["sStA"], 1),))
            smo, _ = rOut.issue(ts % 2)
            pr.emit("gpsimd", lambda e, o=out_d[ts * P:(ts + 1) * P, :], i=ot: e.dma_start(out=o, in_=i),
                    waits=((s["sStA"], s["sStA"].n),), incs=((smo, 16),))
        pr.emit("gpsimd", None, waits=((rOut.s[0], 16 * ((TS + 1) // 2)), (rOut.s[1], 16 * (TS // 2))))

    def _terminate():
        zt = arf(0, D)
        pr.emit("vector", lambda e, o=zt: e.memset(o, 0.0), incs=((s["sStB"], 1),))
        for ts in range(TS):
            smo, _ = rOut.issue(ts % 2)
            pr.emit("gpsimd", lambda e, o=out_d[ts * P:(ts + 1) * P, :], i=zt: e.dma_start(out=o, in_=i),
                    waits=((s["sStB"], s["sStB"].n),), incs=((smo, 16),))
        pr.emit("gpsimd", None, waits=((rOut.s[0], 16 * rOut.u[0]), (rOut.s[1], 16 * rOut.u[1])))

    phases = [("p0", p0), ("p1a", p1a), ("p1bx", lambda: p1b_half(0)),
              ("p1bz", lambda: p1b_half(1)), ("p1c", p1c)]
    if stop_after in ("p2b", "p2c", "p2a0", "p2a1", "p2b1", "p2b2", "p2b3", "p2b4", "p2b0", "p2n1", "p2n2", "p2n3", "p2n4", "p2n5"):
        phases.append((stop_after, lambda: p2(until=stop_after)))
    else:
        phases += [("p2", p2), ("p3", p3), ("pf", pf)]
    for nm, fn in phases:
        fn()
        pr.barrier(s["bar"])
        if stop_after == nm:
            _terminate()
            break

    with nc.Block() as block:
        def runner(lst):
            def f(e):
                for fn, waits, incs in lst:
                    for sm, v in waits:
                        e.wait_ge(sm.h, v)
                    if fn is None:
                        continue
                    ins = fn(e)
                    for sm, k in incs:
                        if k > 0:
                            ins.then_inc(sm.h, k)
            return f
        block.sync(runner(pr.ops["sync"]))
        block.gpsimd(runner(pr.ops["gpsimd"]))
        block.tensor(runner(pr.ops["tensor"]))
        block.scalar(runner(pr.ops["scalar"]))
        block.vector(runner(pr.ops["vector"]))
    es.close()
    return nc




CFG = dict(T=2048, D=1024, E=2048, N=16, R=64, I=2816)
N_CORES = 8
_NC_CACHE = {}


def _get_nc():
    if "nc" not in _NC_CACHE:
        _NC_CACHE["nc"] = build(CFG, debug=False, n_cores=N_CORES)
    return _NC_CACHE["nc"]


def kernel(L0_attn_hidden_states, L0_attn_norm_w, L0_attn_W_in, L0_attn_conv_w,
           L0_attn_W_x, L0_attn_W_dt, L0_attn_b_dt, L0_attn_A_log,
           L0_attn_D_param, L0_attn_W_out, L0_ffn_w_gate, L0_ffn_w_up,
           L0_ffn_w_down):
    import ml_dtypes
    from concourse.bass_utils import run_bass_kernel_spmd
    E, Pn = CFG["E"], P
    ET = E // Pn
    c = np.ascontiguousarray

    def featmajor(a, w):
        return c(np.asarray(a, np.float32).reshape(ET, Pn, w).transpose(1, 0, 2)
                 .reshape(Pn, ET * w))

    DT_, ET2 = CFG["D"] // Pn, 2 * CFG["E"] // Pn
    common = {
        "W_in": c(np.asarray(L0_attn_W_in, np.float32)
                  .reshape(DT_, Pn, ET2, Pn).transpose(0, 2, 1, 3)
                  .reshape(DT_ * ET2 * Pn, Pn)),
        "convw": featmajor(np.asarray(L0_attn_conv_w, np.float32).reshape(E, 4), 4),
        "W_x": featmajor(L0_attn_W_x, 96),
        "Wdt_ext": c(np.concatenate(
            [np.asarray(L0_attn_W_dt, np.float32),
             np.asarray(L0_attn_b_dt, np.float32)[None, :]], axis=0)),
        "A_log": featmajor(L0_attn_A_log, CFG["N"]),
        "D_param": c(np.asarray(L0_attn_D_param, np.float32).reshape(ET, Pn).T),
        "norm_w": c(np.asarray(L0_attn_norm_w, np.float32).reshape(1, CFG["D"])),
        "ident": np.eye(Pn, dtype=ml_dtypes.bfloat16),
        "W_out": c(np.asarray(L0_attn_W_out, np.float32)),
    }
    wg_full = np.asarray(L0_ffn_w_gate, np.float32)
    wu_full = np.asarray(L0_ffn_w_up, np.float32)
    wd_full = np.asarray(L0_ffn_w_down, np.float32)
    myi = CFG["I"] // N_CORES
    myd = CFG["D"] // N_CORES
    xs = np.asarray(L0_attn_hidden_states, np.float32)
    in_maps = [dict(common, x=c(xs[b]),
                    wg=c(wg_full[b * myi:(b + 1) * myi]),
                    wu=c(wu_full[b * myi:(b + 1) * myi]),
                    wd=c(wd_full[b * myd:(b + 1) * myd]))
               for b in range(xs.shape[0])]
    nc = _get_nc()
    res = run_bass_kernel_spmd(nc, in_maps, core_ids=list(range(len(in_maps))))
    return np.stack([r["out"] for r in res.results], axis=0)


# version 20 (final)
# speedup vs baseline: 1.1399x; 1.0125x over previous
import numpy as np
from contextlib import ExitStack
import concourse.bass as bass
import concourse.mybir as mybir
from concourse.dve_op import DveOp
from concourse.dve_spec import Spec, Src0, Src1, C0, C1, C2, Latch, lower as _dve_lower
import concourse.dve_ops as _dve_ops_mod

F32 = mybir.dt.float32
BF16 = mybir.dt.bfloat16
AF = mybir.ActivationFunctionType
ALU = mybir.AluOpType
P = 128

ENGINES = ("sync", "gpsimd", "tensor", "scalar", "vector")

QTH = (0.25 / 6, 0.75 / 6, 1.25 / 6, 1.75 / 6, 2.5 / 6, 3.5 / 6, 5.0 / 6)
QW = (0.5, 0.5, 0.5, 0.5, 1.0, 1.0, 2.0)

_FP4OPS = {}


def _register_fp4_ops():
    if _FP4OPS:
        return _FP4OPS
    b1 = (Src0 > C0) + (Src0 > C1) + (Src0 > C2) + (Src0 > Latch(Src1))
    _c5 = Src0 > C0
    _c6 = Src0 > C1
    _c7 = Src0 > C2
    _e = _c5 + _c6 + _c7 + _c7
    b2 = Src1 + (_e + _e)

    def ref1(in0, in1, s0, s1, imm2):
        f = np.result_type(in0.dtype, np.float32)
        return ((in0 > s0).astype(f) + (in0 > s1).astype(f)
                + (in0 > imm2).astype(f) + (in0 > in1).astype(f))

    def ref2(in0, in1, s0, s1, imm2):
        f = np.result_type(in0.dtype, np.float32)
        return in1 + 2 * ((in0 > s0).astype(f) + (in0 > s1).astype(f)) + 4 * (in0 > imm2).astype(f)

    for nm, body, ref in (("FP4Q1", b1, ref1), ("FP4Q2", b2, ref2)):
        shas = {}
        for ver in ("v3", "v4"):
            try:
                shas[ver] = _dve_lower(Spec(body=body), ver=ver, perf_en=False, name=nm).sha(ver)
            except Exception:
                pass
        op = DveOp(nm, Spec(body=body), subdim=False, uops_sha=shas, reference=ref)
        _dve_ops_mod.OPS.append(op)
        _dve_ops_mod._SUB_OPCODE_FOR_NAME[nm] = max(_dve_ops_mod._SUB_OPCODE_FOR_NAME.values()) + 1
        _dve_ops_mod.CUSTOM_DVE_REFS[nm] = op.ref
        _FP4OPS[nm] = op
    assert max(_dve_ops_mod._SUB_OPCODE_FOR_NAME.values()) < 0x20
    return _FP4OPS


class SemC:
    def __init__(self, h):
        self.h = h
        self.n = 0


class Prog:
    def __init__(self, nc, es):
        self.nc = nc
        self.es = es
        self.ops = {e: [] for e in ENGINES}

    def sem(self, name):
        return SemC(self.es.enter_context(self.nc.semaphore(name)))

    def emit(self, eng, fn, waits=(), incs=()):
        waits = tuple((sm, v) for sm, v in waits if v > 0)
        for sm, k in incs:
            sm.n += k
        self.ops[eng].append((fn, waits, tuple(incs)))

    def barrier(self, bar):
        tgt = bar.n + len(ENGINES)
        for e in ENGINES:
            self.emit(e, lambda en: en.nop(), incs=((bar, 1),))
        for e in ENGINES:
            self.emit(e, None, waits=((bar, tgt),))


class DmaSlots:
    def __init__(self, pr, name, nslots):
        self.s = [pr.sem(f"{name}_{i}") for i in range(nslots)]
        self.u = [0] * nslots

    def issue(self, slot):
        self.u[slot] += 1
        return self.s[slot], 16 * self.u[slot]


def build(cfg, debug=False, stop_after=None, n_cores=1):
    T = cfg["T"]; D = cfg["D"]; E = cfg["E"]; N = cfg["N"]; R = cfg["R"]; I = cfg["I"]
    WQ_SHARD = n_cores > 1
    if WQ_SHARD:
        assert I % n_cores == 0 and D % n_cores == 0
        MYI = I // n_cores
        MYD = D // n_cores
    DT = D // P
    ET = E // P
    IT = I // P
    TS = T // P
    TCH = min(512, T)
    NTC = T // TCH
    GE = min(2, ET)
    NG = ET // GE
    NDH = 2 if D > 512 else 1
    DW = min(512, D)
    CW = min(256, T)
    TS2 = CW // P
    NCH = T // CW
    EPS_RMS = 1e-6
    EPS_LN = 1e-5
    ncols = []
    off = 0
    while off < I:
        w = min(512, I - off)
        ncols.append((off, w))
        off += w
    NNC = len(ncols)

    nc = bass.Bass("TRN2", target_bir_lowering=False, debug=False,
                   detect_race_conditions=False)
    es = ExitStack()
    pr = Prog(nc, es)
    io = "ExternalOutput" if debug else "Internal"

    dr = {}
    def dram(name, shape, dt, kind, addr_space="Local"):
        dr[name] = nc.dram_tensor(name, list(shape), dt, kind=kind,
                                  addr_space=addr_space).ap()
        return dr[name]

    x_in = dram("x", (T, D), F32, "ExternalInput")
    w_in = dram("W_in", (DT * 2 * ET * P, P), F32, "ExternalInput")
    convw = dram("convw", (P, 4 * (E // P)), F32, "ExternalInput")
    w_x = dram("W_x", (P, (E // P) * 96), F32, "ExternalInput")
    wdt_e = dram("Wdt_ext", (R + 1, E), F32, "ExternalInput")
    alog = dram("A_log", (P, (E // P) * N), F32, "ExternalInput")
    dpar = dram("D_param", (P, E // P), F32, "ExternalInput")
    nw_d = dram("norm_w", (1, D), F32, "ExternalInput")
    wg_d = dram("wg", (MYI if WQ_SHARD else I, D), F32, "ExternalInput")
    wu_d = dram("wu", (MYI if WQ_SHARD else I, D), F32, "ExternalInput")
    wd_d = dram("wd", (MYD if WQ_SHARD else D, I), F32, "ExternalInput")
    idn = dram("ident", (P, P), BF16, "ExternalInput")
    wout_d = dram("W_out", (E, D), F32, "ExternalInput")
    out_d = dram("out", (T, D), F32, "ExternalOutput")

    sc_rows = dram("sc_rows", (32, T), BF16, io)
    sc_u = dram("sc_u", (E, T), BF16, io)
    sc_sz = dram("sc_sz", (E, T), BF16, io)
    sc_g = dram("sc_g", (E, T), BF16, io)
    sc_x2 = dram("sc_x2", (T, D), F32, io)
    sc_xqT = dram("sc_xqT", (D, T), BF16, io)
    sc_hqT = dram("sc_hqT", (I, T), BF16, io)
    wq_space = "Shared" if WQ_SHARD else "Local"
    sc_wgq = dram("sc_wgq", (I, D), BF16, io, addr_space=wq_space)
    sc_wuq = dram("sc_wuq", (I, D), BF16, io, addr_space=wq_space)
    sc_wdq = dram("sc_wdq", (D, I), BF16, io, addr_space=wq_space)
    if WQ_SHARD:
        sc_wgq_my = dram("sc_wgq_my", (MYI, D), BF16, "Internal")
        sc_wuq_my = dram("sc_wuq_my", (MYI, D), BF16, "Internal")
        sc_wdq_my = dram("sc_wdq_my", (MYD, I), BF16, "Internal")
    if debug:
        dram("dbg_q_ab", (P, D), F32, "ExternalOutput")
        dram("dbg_q_m1", (P, D // 2), F32, "ExternalOutput")
        dram("dbg_q_m2", (P, D // 4), F32, "ExternalOutput")
        dram("dbg_q_m3", (P, D // 8), F32, "ExternalOutput")
        dram("dbg_wgq", (P, I), BF16, "ExternalOutput")
        dram("dbg_qo", (P, D), BF16, "ExternalOutput")
        dram("dbg_hq", (P, I), BF16, "ExternalOutput")
        dram("dbg_q_mx", (P, D // 16), F32, "ExternalOutput")
        dram("dbg_q_v", (P, D), F32, "ExternalOutput")
        dram("dbg_q_qa", (P, D), BF16, "ExternalOutput")
        dram("dbg_delta", (E, T), BF16, "ExternalOutput")
        dram("dbg_y", (E, T), F32, "ExternalOutput")
        dram("dbg_dtlr", (R + 1, T), BF16, "ExternalOutput")

    def sb(name, shape, dt):
        return es.enter_context(nc.sbuf_tensor(name, list(shape), dt))

    AR_BYTES = 114688
    ar = sb("ar", (P, AR_BYTES // 2), BF16)

    def arb(off, n):
        assert off % 2 == 0 and off + 2 * n <= AR_BYTES, (off, n)
        return ar[:, off // 2: off // 2 + n]

    def arf(off, n):
        assert off % 4 == 0 and off + 4 * n <= AR_BYTES, (off, n)
        return ar[:, off // 2: off // 2 + 2 * n].bitcast(F32)

    def aln(v):
        return (v + 15) // 16 * 16

    o_xnT = 0
    o_acc1 = aln(o_xnT + DT * T * 2)
    o_acc2 = aln(o_acc1 + T * 4)
    o_et = aln(o_acc2 + T * 4)
    o_xin = aln(o_et + T * 4)
    o_ubuf = aln(o_xin + (T + 4) * 4)
    o_szb1 = aln(o_ubuf + 2 * T * 2)
    o_xnw = aln(o_szb1 + 2 * T * 2)
    end_p1 = o_xnw + 2 * D * 2
    o_dl = 0
    o_du = o_dl + GE * T * 2
    o_y = o_du + GE * T * 2
    o_ug = o_y + GE * T * 4
    o_dA = o_ug + GE * T * 2
    o_b = o_dA + 2 * T * 4
    o_h = o_b + 2 * T * 2
    o_bcB = o_h + 2 * T * 2
    o_bcC = o_bcB + 2 * T * 2
    o_szb2 = o_b
    end_p2 = o_bcC + 2 * T * 2

    def qblock(base, W):
        q = {}
        for nm, wdt, esz in (("absx", W, 4), ("v", W, 4), ("qa", W, 2), ("qb", W, 2),
                             ("sgn", W, 2), ("mx", W // 16, 4), ("mxc", W // 16, 4),
                             ("rs", W // 16, 4), ("sc", W // 16, 4)):
            q[nm] = (base, wdt)
            base = aln(base + wdt * esz)
        q["end"] = base
        return q

    o_wo = 0
    o_rt = aln(o_wo + ET * D * 2)
    o_gld = aln(o_rt + D * 4)
    o_q3 = aln(o_gld + 2 * ET * CW * 2)
    q3 = qblock(o_q3, D)
    o_x2b = aln(q3["end"])
    o_x2q = aln(o_x2b + D * 4)
    end_p3 = o_x2q + D * 2
    o_wgq = 0
    o_wuq = o_wgq + DT * I * 2
    o_hq = aln(o_wuq + DT * I * 2)
    o_hqT = aln(o_hq + I * 2)
    o_qf = aln(o_hqT + IT * P * 2)
    QFW = min(512, I)
    qf = qblock(o_qf, QFW)
    end_pfa = qf["end"]
    qb = qblock(0, I)
    o_outt = 0
    o_rt2 = aln(o_outt + 2 * D * 4)
    o_x2r = aln(o_rt2 + D * 4)
    o_wdq = aln(max(qb["end"], o_x2r + 2 * D * 4))
    o_hqr = aln(o_wdq + IT * D * 2)
    end_pfb = o_hqr + 2 * I * 2
    assert max(end_p1, end_p2, end_p3, end_pfa, end_pfb) <= AR_BYTES, (
        end_p1, end_p2, end_p3, end_pfa, end_pfb)

    ident = sb("ident_s", (P, P), BF16)
    nwbc = sb("nwbc", (P, D), F32)
    cwc = sb("cwc", (P, 4 * ET), F32)
    A_t = sb("A_t", (P, ET * N), F32)
    A_s = sb("A_s", (P, ET * N), F32)
    D_s = sb("D_s", (P, ET), F32)
    Wx_s = sb("Wx_s", (P, ET * 96), BF16)
    Wdt_s = sb("Wdt_s", (R + 1, E), BF16)
    dtlr_s = sb("dtlr_s", (R + 1, T), BF16)
    bcsb = sb("bcsb", (96, T), BF16)
    cols = sb("cols", (P, 32), F32)
    cst = sb("cst", (P, 8), F32)
    WSW = max(D, I, 1024) + 512
    wst = [sb(f"wst{i}", (P, WSW), F32) for i in range(2)]
    wbf = [sb(f"wbf{i}", (P, P), BF16) for i in range(4)]
    xt = [sb(f"xt{i}", (P, D), F32) for i in range(2)]
    scr = sb("scr", (P, max(D, I)), F32)
    evt = [sb(f"evt{i}", (P, P), BF16) for i in range(2)]
    xqt = [sb(f"xqt{i}", (P, DT * P), BF16) for i in range(2)]
    qo = sb("qo", (P, 2 * max(D, I, T, 1024)), BF16)

    ps = [es.enter_context(nc.psum_tensor(f"ps{i}", (P, 512), F32)) for i in range(8)]
    def psv(i, w=512):
        return ps[i][:, 0:w]
    def psb(i):
        return ps[i][:, 0:64].bitcast(BF16)

    s = {k: pr.sem(k) for k in (
        "bar", "sW", "sWc", "sXf", "sRst", "sXnw", "sTr", "sEvD", "sWbf",
        "sWuse", "sPsX", "sEvX", "sConv", "sSig", "sU", "sWxU", "sSZp",
        "sRows", "sRowsD", "sDpre", "sSPa", "sSPr", "sDel", "sDelV", "sDu",
        "sDA", "sScan", "sYm", "sGate", "sWo", "sMMo", "sStA", "sStB",
        "sX2", "sX2sp", "sQd", "sQo", "sQs", "sTr3", "sEv3", "sWq",
        "sWqh", "sWqT", "sMMg", "sSigF", "sHq", "sTrH", "sEvH", "sHsp",
        "sMMd", "sRes", "sLs1", "sLs2", "sLs3", "sDbg", "sCol")}

    MMKW = dict(skip_group_check=True)

    rX = DmaSlots(pr, "rX", 2)
    rW = DmaSlots(pr, "rW", 2)
    rUg = DmaSlots(pr, "rUg", GE)
    rBB = DmaSlots(pr, "rBB", 2)
    rCC = DmaSlots(pr, "rCC", 2)
    rSZ = DmaSlots(pr, "rSZ", 2)
    rGld = DmaSlots(pr, "rGld", 2)
    rXq = DmaSlots(pr, "rXq", 2)
    rHr = DmaSlots(pr, "rHr", 2)
    rX2r = DmaSlots(pr, "rX2r", 2)
    rUsp = DmaSlots(pr, "rUsp", 2)
    rSZsp = DmaSlots(pr, "rSZsp", 2)
    rGsp = DmaSlots(pr, "rGsp", GE)
    rXsp = DmaSlots(pr, "rXsp", 2)
    rOut = DmaSlots(pr, "rOut", 2)
    rWQ = DmaSlots(pr, "rWQ", 2)
    rWQsp = DmaSlots(pr, "rWQsp", 2)
    rWQ2 = DmaSlots(pr, "rWQ2", 2)

    def p0():
        loads = [
            (nwbc[:], nw_d.broadcast_to((P, D))),
            (cwc[:], convw),
            (A_t[:], alog),
            (D_s[:], dpar),
            (ident[:], idn),
            (wst[0][:, :ET * 96], w_x),
            (wst[1][:R + 1, :E], wdt_e),
        ]
        for dst, src in loads:
            pr.emit("sync", lambda e, d=dst, sr=src: e.dma_start(out=d, in_=sr),
                    incs=((s["sW"], 16),))
        w0 = s["sW"].n
        pr.emit("scalar", lambda e: e.activation(A_s[:], A_t[:], AF.Exp),
                waits=((s["sW"], w0),), incs=((s["sWc"], 1),))
        pr.emit("vector", lambda e: e.tensor_scalar_mul(A_t[:], A_s[:], -1.0),
                waits=((s["sWc"], s["sWc"].n),), incs=((s["sWc"], 1),))
        src = wst[0][:, :ET * 96]
        pr.emit("vector", lambda e, sr=src: e.tensor_copy(Wx_s[:], sr),
                waits=((s["sW"], w0),), incs=((s["sWc"], 1),))
        src = wst[1][:R + 1, :E]
        pr.emit("vector", lambda e, sr=src: e.tensor_copy(Wdt_s[:], sr),
                waits=((s["sW"], w0),), incs=((s["sWc"], 1),))
        pr.emit("vector", lambda e: e.memset(dtlr_s[R:R + 1, :], 1.0),
                incs=((s["sWc"], 1),))
        pr.emit("vector", lambda e, a=arf(o_xin, 4): e.memset(a, 0.0),
                incs=((s["sWc"], 1),))
        for ci, val in ((0, EPS_RMS), (1, EPS_LN), (2, 1.0 / D), (3, -0.5), (4, -1.0), (5, 1.0), (6, 0.5), (7, QTH[3])):
            pr.emit("vector", lambda e, a=cst[:, ci:ci + 1], v=val: e.memset(a, v),
                    incs=((s["sWc"], 1),))

    def xnT_tile(dt):
        return arb(o_xnT + dt * T * 2, T)

    def p1a():
        for ts in range(TS):
            sl = ts % 2
            src = x_in[ts * P:(ts + 1) * P, :]
            smx, vx = rX.issue(sl)
            pr.emit("sync", lambda e, d=xt[sl][:], sr=src: e.dma_start(out=d, in_=sr),
                    waits=((s["sXf"], ts - 1),), incs=((smx, 16),))
            c_ssq = cols[:, sl:sl + 1]
            pr.emit("scalar", lambda e, xi=xt[sl][:], c=c_ssq: e.activation(
                scr[:, :D], xi, AF.Square, accum_out=c),
                    waits=((smx, vx),))
            c_l = cols[:, 4 + sl:5 + sl]
            c_r = cols[:, 6 + sl:7 + sl]
            pr.emit("scalar", lambda e, a=c_l, b=c_ssq: e.activation(
                a, b, AF.Ln, bias=cst[:, 0:1], scale=cst[:, 2:3]),
                    waits=((s["sXf"], ts - 1),))
            pr.emit("scalar", lambda e, a=c_r, b=c_l: e.activation(a, b, AF.Exp, scale=cst[:, 3:4]),
                    incs=((s["sRst"], 1),))
            tmpf = arf(o_acc1, D)
            pr.emit("vector", lambda e, xi=xt[sl][:], c=c_r, t_=tmpf: e.tensor_scalar_mul(t_, xi, c),
                    waits=((s["sRst"], ts + 1),), incs=((s["sXf"], 1),))
            xnw = arb(o_xnw + sl * D * 2, D)
            pr.emit("vector", lambda e, t_=tmpf, o=xnw: e.tensor_mul(o, t_, nwbc[:]),
                    waits=((s["sEvD"], (ts - 1) * DT),), incs=((s["sXnw"], 1),))
            for dt in range(DT):
                pp = dt % 2
                src_t = arb(o_xnw + sl * D * 2 + dt * P * 2, P)
                pr.emit("tensor", lambda e, o=psb(4 + pp), i=src_t: e.transpose(o, i, ident[:]),
                        waits=((s["sXnw"], ts + 1), (s["sEvD"], s["sTr"].n - 1)),
                        incs=((s["sTr"], 1),))
                dst_t = xnT_tile(dt)[:, ts * P:(ts + 1) * P]
                pr.emit("vector", lambda e, o=dst_t, i=psb(4 + pp): e.tensor_copy(o, i),
                        waits=((s["sTr"], s["sTr"].n),), incs=((s["sEvD"], 1),))

    def p1b_half(half):
        def kloop(et):
            ib = half * ET + et
            for k in range(DT):
                li = ib * DT + k
                wsl = li % 4
                src = w_in[(k * 2 * ET + ib) * P:(k * 2 * ET + ib + 1) * P, :]
                smw, vw = rW.issue(li % 2)
                pr.emit("sync", lambda e, d=wst[li % 2][:, :P], sr=src: e.dma_start(out=d, in_=sr),
                        waits=((s["sWbf"], li - 1),), incs=((smw, 16),))
                pr.emit("vector", lambda e, o=wbf[wsl][:], i=wst[li % 2][:, :P]: e.tensor_copy(o, i),
                        waits=((smw, vw), (s["sWuse"], li - 3)),
                        incs=((s["sWbf"], 1),))
                for tc in range(NTC):
                    incs = []
                    if k == DT - 1:
                        incs.append((s["sPsX"], 1))
                    rhs = xnT_tile(k)[:, tc * TCH:(tc + 1) * TCH]
                    pr.emit("tensor", lambda e, o=psv(tc, TCH), l_=wbf[wsl][:], r_=rhs,
                            st=(k == 0), sp=(k == DT - 1): e.matmul(o, l_, r_, start=st, stop=sp, **MMKW),
                            waits=((s["sWbf"], s["sWbf"].n),
                                   (s["sEvX"], (ib - 1) * NTC + tc + 1)),
                            incs=tuple(incs))
                pr.emit("tensor", lambda e: e.nop(), incs=((s["sWuse"], 1),))

        def tail(et):
            ib = half * ET + et
            for tc in range(NTC):
                if half == 0:
                    dst = arf(o_xin + 16 + tc * TCH * 4, TCH)
                else:
                    dst = arf(o_acc1 + tc * TCH * 4, TCH)
                waits = [(s["sPsX"], ib * NTC + tc + 1)]
                if half == 1 and et == 0 and tc == 0:
                    waits.append((s["sU"], ET))
                if half == 1 and et > 0 and tc == 0:
                    waits.append((s["sSZp"], et))
                if half == 0 and et > 0 and tc == 0:
                    waits.append((s["sConv"], et))
                pr.emit("scalar", lambda e, d=dst, i=psv(tc, TCH): e.copy(d, i),
                        waits=tuple(waits), incs=((s["sEvX"], 1),))
            if half == 0:
                xin = lambda j: arf(o_xin + j * 4, T)
                wc = lambda j: cwc[:, et * 4 + j: et * 4 + j + 1]
                pr.emit("vector", lambda e, o=arf(o_acc1, T), i=xin(1), w=wc(0): e.tensor_scalar_mul(o, i, w),
                        waits=((s["sEvX"], (ib + 1) * NTC),))
                pr.emit("vector", lambda e, o=arf(o_acc2, T), i=xin(2), w=wc(1),
                        a=arf(o_acc1, T): e.scalar_tensor_tensor(o, i, w, a, ALU.mult, ALU.add))
                pr.emit("vector", lambda e, o=arf(o_acc1, T), i=xin(3), w=wc(2),
                        a=arf(o_acc2, T): e.scalar_tensor_tensor(o, i, w, a, ALU.mult, ALU.add))
                pr.emit("vector", lambda e, o=arf(o_acc2, T), i=xin(4), w=wc(3),
                        a=arf(o_acc1, T): e.scalar_tensor_tensor(o, i, w, a, ALU.mult, ALU.add),
                        incs=((s["sConv"], 1),))
                pr.emit("scalar", lambda e, o=arf(o_et, T), i=arf(o_acc2, T): e.activation(
                    o, i, AF.Tanh, scale=cst[:, 6:7]),
                        waits=((s["sConv"], s["sConv"].n),), incs=((s["sSig"], 1),))
                pr.emit("vector", lambda e, o=arf(o_acc1, T), i=arf(o_et, T): e.tensor_scalar(
                    o, i, 0.5, 0.5, ALU.mult, ALU.add),
                        waits=((s["sSig"], s["sSig"].n),))
                ub = arb(o_ubuf + (et % 2) * T * 2, T)
                pr.emit("vector", lambda e, o=ub, a=arf(o_acc2, T), b=arf(o_acc1, T): e.tensor_mul(o, a, b),
                        waits=((s["sWxU"], et - 1), (rUsp.s[et % 2], 16 * rUsp.u[et % 2])),
                        incs=((s["sU"], 1),))
                for tc in range(NTC):
                    rhs = ub[:, tc * TCH:(tc + 1) * TCH]
                    pr.emit("tensor", lambda e, o=ps[4 + tc][0:96, 0:TCH],
                            l_=Wx_s[:, et * 96:(et + 1) * 96], r_=rhs,
                            st=(et == 0), sp=(et == ET - 1): e.matmul(o, l_, r_, start=st, stop=sp, **MMKW),
                            waits=((s["sU"], et + 1),),
                            incs=((s["sWxU"], 1),) if tc == NTC - 1 else ())
                smu, _ = rUsp.issue(et % 2)
                pr.emit("gpsimd", lambda e, o=sc_u[et * P:(et + 1) * P, :], i=ub: e.dma_start(out=o, in_=i),
                        waits=((s["sU"], et + 1),), incs=((smu, 16),))
                emit_wq_tasks(2)
            elif True:
                pr.emit("scalar", lambda e, o=arf(o_et, T), i=arf(o_acc1, T): e.activation(
                    o, i, AF.Tanh, scale=cst[:, 6:7]),
                        waits=((s["sEvX"], (ib + 1) * NTC),), incs=((s["sSig"], 1),))
                pr.emit("vector", lambda e, o=arf(o_acc2, T), i=arf(o_et, T): e.tensor_scalar(
                    o, i, 0.5, 0.5, ALU.mult, ALU.add),
                        waits=((s["sSig"], s["sSig"].n),))
                szb = arb(o_szb1 + (et % 2) * T * 2, T)
                pr.emit("vector", lambda e, o=szb, a=arf(o_acc1, T), b=arf(o_acc2, T): e.tensor_mul(o, a, b),
                        waits=((rSZsp.s[et % 2], 16 * rSZsp.u[et % 2]),), incs=((s["sSZp"], 1),))
                smz, _ = rSZsp.issue(et % 2)
                pr.emit("gpsimd", lambda e, o=sc_sz[et * P:(et + 1) * P, :], i=szb: e.dma_start(out=o, in_=i),
                        waits=((s["sSZp"], s["sSZp"].n),), incs=((smz, 16),))
            emit_wq_tasks(2)

        for et in range(ET):
            kloop(et)
            if et > 0:
                tail(et - 1)
        tail(ET - 1)

    def p1c():
        for tc in range(NTC):
            pr.emit("scalar", lambda e, o=dtlr_s[0:R, tc * TCH:(tc + 1) * TCH],
                    i=ps[4 + tc][0:R, 0:TCH]: e.copy(o, i),
                    waits=((s["sWxU"], ET),))
            pr.emit("scalar", lambda e, o=bcsb[64:96, tc * TCH:(tc + 1) * TCH],
                    i=ps[4 + tc][64:96, 0:TCH]: e.copy(o, i),
                    incs=((s["sRows"], 1),))
        pr.emit("gpsimd", lambda e: e.dma_start(out=sc_rows[:], in_=bcsb[64:96, :]),
                waits=((s["sRows"], s["sRows"].n),), incs=((s["sRowsD"], 16),))
        emit_wq_collectives()
        if debug:
            pr.emit("gpsimd", lambda e: e.dma_start(out=dr["dbg_dtlr"][:], in_=dtlr_s[:]),
                    waits=((s["sRows"], NTC),), incs=((s["sDbg"], 16),))

    def p2(until=None):
        ROWS_RDY = NTC

        def dl_t(eh): return arb(o_dl + eh * T * 2, T)
        def du_t(eh): return arb(o_du + eh * T * 2, T)
        def y_e(eh): return arb(o_y + eh * T * 4, T)
        def y_o(eh): return arb(o_y + eh * T * 4 + T * 2, T)
        def ug_t(eh): return arb(o_ug + eh * T * 2, T)
        def dA_t(r2): return arf(o_dA + r2 * T * 4, T)
        def b_t(r2): return arb(o_b + r2 * T * 2, T)
        def h_t(r2): return arb(o_h + r2 * T * 2, T)
        def bcB(r2): return arb(o_bcB + r2 * T * 2, T)
        def bcC(r2): return arb(o_bcC + r2 * T * 2, T)

        for g in range(NG):
            if until == "p2a0":
                return
            for eh in range(GE):
                et = g * GE + eh
                smug, _ = rUg.issue(eh)
                pr.emit("sync", lambda e, o=ug_t(eh), i=sc_u[et * P:(et + 1) * P, :]: e.dma_start(out=o, in_=i),
                        waits=((rGsp.s[eh], 16 * g),
                               (rUsp.s[0], 16 * ((ET + 1) // 2)), (rUsp.s[1], 16 * (ET // 2))),
                        incs=((smug, 16),))
            if until == "p2a1":
                return
            if until == "p2b0":
                for tc in range(NTC):
                    rhs = dtlr_s[:, tc * TCH:(tc + 1) * TCH]
                    pr.emit("tensor", lambda e, o=psv(tc, TCH),
                            l_=Wdt_s[:, 0:P], r_=rhs: e.matmul(
                        o, l_, r_, start=True, stop=True, **MMKW),
                            waits=((s["sRows"], ROWS_RDY), (s["sRowsD"], 16)),
                            incs=((s["sDpre"], 1),))
                return
            for eh in range(GE):
                et = g * GE + eh
                j = g * GE + eh
                for tc in range(NTC):
                    rhs = dtlr_s[:, tc * TCH:(tc + 1) * TCH]
                    free_v = (j - 1) * NTC + tc + 1
                    pr.emit("tensor", lambda e, o=psv(tc, TCH),
                            l_=Wdt_s[:, et * P:(et + 1) * P], r_=rhs: e.matmul(
                        o, l_, r_, start=True, stop=True, **MMKW),
                            waits=((s["sRows"], ROWS_RDY), (s["sRowsD"], 16),
                                   (s["sSPa"], free_v), (s["sSPr"], free_v)),
                            incs=((s["sDpre"], 1),))
                for tc in range(NTC):
                    pw = j * NTC + tc + 1
                    spA = wst[0][:, tc * TCH:(tc + 1) * TCH]
                    spB = wst[1][:, tc * TCH:(tc + 1) * TCH]
                    spP = scr[:, tc * TCH:(tc + 1) * TCH]
                    rlb = qo[:, tc * TCH:(tc + 1) * TCH]
                    wq_guard = ((rWQsp.s[0], 16 * rWQsp.u[0]),
                                (rWQsp.s[1], 16 * rWQsp.u[1])) if (g == 0 and j == 0 and tc == 0) else ()
                    pr.emit("scalar", lambda e, o=spP, i=psv(tc, TCH): e.copy(o, i),
                            waits=((s["sDpre"], pw), (s["sSPr"], (j - 1) * NTC + tc + 1),
                                   (s["sDelV"], (j - 1) * NTC + tc + 1)) + wq_guard,
                            incs=((s["sSPa"], 1),))
                    pr.emit("scalar", lambda e, o=spA, i=spP: e.activation(o, i, AF.Abs))
                    pr.emit("scalar", lambda e, o=spB, i=spA: e.activation(o, i, AF.Exp, scale=cst[:, 4:5]))
                    pr.emit("scalar", lambda e, o=spA, i=spB: e.activation(o, i, AF.Ln, bias=cst[:, 5:6]),
                            incs=((s["sDel"], 1),))
                    pr.emit("vector", lambda e, o=rlb, i=spP: e.tensor_scalar_max(o, i, 0.0),
                            waits=((s["sDel"], s["sDel"].n),), incs=((s["sSPr"], 1),))
                    pr.emit("vector", lambda e, o=dl_t(eh)[:, tc * TCH:(tc + 1) * TCH],
                            a=rlb, b=spA: e.tensor_tensor(o, a, b, ALU.add),
                            incs=((s["sDelV"], 1),))
                pr.emit("vector", lambda e, o=du_t(eh), a=dl_t(eh), b=ug_t(eh): e.tensor_mul(o, a, b),
                        waits=((rUg.s[eh], 16 * (g + 1)),), incs=((s["sDu"], 1),))
                if debug:
                    pr.emit("gpsimd", lambda e, o=dr["dbg_delta"][et * P:(et + 1) * P, :],
                            i=dl_t(eh): e.dma_start(out=o, in_=i),
                            waits=((s["sDu"], s["sDu"].n),), incs=((s["sDbg"], 16),))
            if until in ("p2b", "p2b1", "p2b2", "p2b3", "p2b4", "p2b0", "p2n1", "p2n2", "p2n3", "p2n4", "p2n5"):
                return
            dgrp = (g + 1) * GE * NTC
            for n in range(N if until != "p2c" else 1):
                nsl = n % 2
                gi = g * N + n
                smb, vbB = rBB.issue(nsl)
                pr.emit("sync", lambda e, o=bcB(nsl),
                        i=sc_rows[n:n + 1, :].broadcast_to((P, T)): e.dma_start(out=o, in_=i),
                        waits=((s["sRows"], ROWS_RDY), (s["sRowsD"], 16),
                               (s["sYm"], (gi - 1) * GE)),
                        incs=((smb, 16),))
                smc, vbC = rCC.issue(nsl)
                pr.emit("sync", lambda e, o=bcC(nsl),
                        i=sc_rows[16 + n:16 + n + 1, :].broadcast_to((P, T)): e.dma_start(out=o, in_=i),
                        incs=((smc, 16),))
                for eh in range(GE):
                    et = g * GE + eh
                    it = gi * GE + eh
                    r2 = it % 2
                    pr.emit("scalar", lambda e, o=dA_t(r2), i=dl_t(eh),
                            a=A_t[:, et * N + n: et * N + n + 1]: e.activation(o, i, AF.Exp, scale=a),
                            waits=((s["sDelV"], dgrp), (s["sScan"], it - 1)),
                            incs=((s["sDA"], 1),))
                    pr.emit("vector", lambda e, o=b_t(r2), a=du_t(eh), b=bcB(nsl): e.tensor_mul(o, a, b),
                            waits=((s["sDu"], (g + 1) * GE), (smb, vbB),
                                   (s["sScan"], it - 1)))
                    pr.emit("vector", lambda e, o=h_t(r2), a=dA_t(r2), b=b_t(r2): e.tensor_tensor_scan(
                        o, a, b, 0.0, ALU.mult, ALU.add),
                            waits=((s["sDA"], it + 1), (s["sYm"], it - 1)),
                            incs=((s["sScan"], 1),))
                    yacc = y_e(eh) if n % 2 == 0 else y_o(eh)
                    if n < 2:
                        pr.emit("vector", lambda e, o=yacc, a=h_t(r2), b=bcC(nsl): e.tensor_mul(o, a, b),
                                waits=((smc, vbC),), incs=((s["sYm"], 1),))
                    else:
                        pr.emit("vector", lambda e, o=h_t(r2), a=h_t(r2), b=bcC(nsl): e.tensor_mul(o, a, b),
                                waits=((smc, vbC),))
                        pr.emit("vector", lambda e, o=yacc, a=yacc, b=h_t(r2): e.tensor_tensor(
                            o, a, b, ALU.add), incs=((s["sYm"], 1),))
            szv = {}
            for eh in range(GE):
                et = g * GE + eh
                szv[eh] = rSZ.issue(eh % 2)
                pr.emit("sync", lambda e, o=arb(o_szb2 + (eh % 2) * T * 2, T),
                        i=sc_sz[et * P:(et + 1) * P, :]: e.dma_start(out=o, in_=i),
                        waits=((rSZsp.s[0], 16 * ((ET + 1) // 2)), (rSZsp.s[1], 16 * (ET // 2)),
                               (s["sGate"], g * GE + eh - 1),
                               (s["sScan"], (g + 1) * N * GE)),
                        incs=((szv[eh][0], 16),))
            ym_grp = (g * N + N - 1) * GE
            for eh in range(GE):
                et = g * GE + eh
                if debug:
                    pr.emit("gpsimd", lambda e, o=dr["dbg_y"][et * P:(et + 1) * P, :],
                            i=y_e(eh): e.dma_start(out=o, in_=i),
                            waits=((s["sYm"], ym_grp + eh + 1),), incs=((s["sDbg"], 16),))
                yf = arf(o_h, T)
                pr.emit("vector", lambda e, o=yf, i=ug_t(eh), d=D_s[:, et:et + 1],
                        y=y_e(eh): e.scalar_tensor_tensor(o, i, d, y, ALU.mult, ALU.add),
                        waits=((s["sYm"], ym_grp + eh + 1),))
                pr.emit("vector", lambda e, o=yf, a=yf, b=y_o(eh): e.tensor_tensor(o, a, b, ALU.add))
                pr.emit("vector", lambda e, o=ug_t(eh), a=yf,
                        b=arb(o_szb2 + (eh % 2) * T * 2, T): e.tensor_mul(o, a, b),
                        waits=(szv[eh],),
                        incs=((s["sGate"], 1),))
                smg, _ = rGsp.issue(eh)
                pr.emit("gpsimd", lambda e, o=sc_g[et * P:(et + 1) * P, :], i=ug_t(eh): e.dma_start(out=o, in_=i),
                        waits=((s["sGate"], g * GE + eh + 1),), incs=((smg, 16),))

    import os
    QDBG = {"n": 0, "tgt": int(os.environ.get("QDBG_N", "0"))}

    def quant(src_ap, out_ap, W, q, waits_first=(), waits_dve=(), incs_last=()):
        B = W // 16
        ab = arf(*q["absx"])[:, :W]
        vv = arf(*q["v"])[:, :W]
        qa = arb(*q["qa"])[:, :W]
        qb_ = arb(*q["qb"])[:, :W]
        sg = arb(*q["sgn"])[:, :W]
        mx = arf(*q["mx"])[:, :B]
        mxc = arf(*q["mxc"])[:, :B]
        rs = arf(*q["rs"])[:, :B]
        scq = arf(*q["sc"])[:, :B]
        prev = s["sQd"].n
        dbgw = ((s["sDbg"], s["sDbg"].n),) if debug else ()
        pr.emit("scalar", lambda e, o=ab, i=src_ap: e.activation(o, i, AF.Abs),
                waits=tuple(waits_first) + ((s["sQd"], prev),) + dbgw, incs=((s["sQd"], 1),))
        pr.emit("scalar", lambda e, o=sg, i=src_ap: e.activation(o, i, AF.Sign))
        pr.emit("scalar", lambda e, o=cols[:, 24:25], i=ab[:, W - 1:W]: e.copy(o, i))
        pr.emit("scalar", lambda e, o=cols[:, 25:26], i=sg[:, W - 1:W]: e.copy(o, i),
                incs=((s["sQd"], 1),))
        w0 = s["sQd"].n
        pr.emit("vector", lambda e, o=mx.unsqueeze(2),
                i=ab.rearrange("p (b i) -> p b i", i=16): e.tensor_reduce(
            o, i, mybir.AxisListType.X, ALU.max),
                waits=((s["sQd"], w0),), incs=((s["sQs"], 1),))
        pr.emit("vector", lambda e, o=mxc, i=mx: e.tensor_scalar_max(o, i, 6e-12),
                waits=((s["sQs"], s["sQs"].n),), incs=((s["sQs"], 1),))
        pr.emit("vector", lambda e, o=rs, i=mxc: e.reciprocal(o, i),
                waits=((s["sQs"], s["sQs"].n),))
        pr.emit("vector", lambda e, o=scq, i=mxc: e.tensor_scalar_mul(o, i, 1.0 / 6.0),
                incs=((s["sQs"], 1),))
        bv = lambda t_: t_.rearrange("p (b i) -> p b i", i=16)
        rsb = rs.unsqueeze(2).broadcast_to((P, B, 16))
        scb = scq.unsqueeze(2).broadcast_to((P, B, 16))
        pr.emit("vector", lambda e, o=bv(vv), a=bv(ab), b=rsb: e.tensor_tensor(o, a, b, ALU.mult),
                waits=((s["sQs"], s["sQs"].n),))
        pr.emit("vector", lambda e, o=qa, i=vv: e.tensor_scalar(o, i, QTH[0], QW[0], ALU.is_gt, ALU.mult))
        for j in range(1, 7):
            pr.emit("vector", lambda e, o=qb_, i=vv, t_=QTH[j], w=QW[j]: e.tensor_scalar(
                o, i, t_, w, ALU.is_gt, ALU.mult))
            pr.emit("vector", lambda e, o=qa, b=qb_: e.tensor_tensor(o, o, b, ALU.add))
        take = debug and W == D and QDBG["n"] == QDBG["tgt"]
        QDBG["n"] += 1
        if take:
            pr.emit("gpsimd", lambda e, i=ab: e.dma_start(out=dr["dbg_q_ab"], in_=i),
                    waits=((s["sQd"], s["sQd"].n),), incs=((s["sDbg"], 16),))
            pr.emit("gpsimd", lambda e, i=mx: e.dma_start(out=dr["dbg_q_mx"], in_=i),
                    incs=((s["sDbg"], 16),))
            pr.emit("gpsimd", lambda e, i=vv: e.dma_start(out=dr["dbg_q_v"], in_=i),
                    incs=((s["sDbg"], 16),))
            pr.emit("gpsimd", lambda e, i=qa: e.dma_start(out=dr["dbg_q_qa"], in_=i),
                    incs=((s["sDbg"], 16),))
        pr.emit("vector", lambda e, o=qa, b=sg: e.tensor_mul(o, o, b),
                incs=((s["sQd"], 1),))
        assert len(incs_last) <= 1
        pr.emit("vector", lambda e, o=bv(out_ap), a=bv(qa), b=scb: e.tensor_tensor(o, a, b, ALU.mult),
                waits=tuple(waits_dve), incs=tuple(incs_last))

    QHW = 1408 if I % 1408 == 0 or I >= 1408 else I
    qh = qblock(aln(end_p1), min(1408, I))
    assert qh["end"] <= AR_BYTES
    QOS = max(D, I, T, 1024)

    def _row_tiles(rows):
        off, out = 0, []
        while off < rows:
            out.append((off, min(P, rows - off)))
            off += P
        return out

    wq_tasks = []
    for off, h in _row_tiles(MYI if WQ_SHARD else I):
        wq_tasks.append(("wg", off, h))
    for off, h in _row_tiles(MYI if WQ_SHARD else I):
        wq_tasks.append(("wu", off, h))
    for off, h in _row_tiles(MYD if WQ_SHARD else D):
        wq_tasks.append(("wd", off, h))
    wq_state = {"i": 0}

    def emit_wq_tasks(k):
        for _ in range(k):
            qi = wq_state["i"]
            if qi >= len(wq_tasks):
                return
            wq_state["i"] += 1
            kind, roff, h = wq_tasks[qi]
            sl = qi % 2
            if kind == "wd":
                srcd, W = wd_d[roff:roff + h, :], I
            else:
                srcd = (wg_d if kind == "wg" else wu_d)[roff:roff + h, :]
                W = D
            stg = wst[sl][:, P:P + W]
            smq, vq = rWQ.issue(sl)
            pr.emit("sync", lambda e, o=stg[0:h, :], i=srcd: e.dma_start(out=o, in_=i),
                    waits=((s["sQd"], s["sQd"].n - 4),), incs=((smq, 16),))
            oslot = qo[:, sl * QOS: sl * QOS + W]
            halves = [(0, W)] if W <= 1408 else [(0, W // 2), (W // 2, W // 2)]
            for hoff, hw in halves:
                quant(stg[:, hoff:hoff + hw], oslot[:, hoff:hoff + hw], hw, qh,
                      waits_first=((smq, vq),),
                      waits_dve=((rWQsp.s[sl], 16 * rWQsp.u[sl]),),
                      incs_last=((s["sWqh"], 1),))
            if WQ_SHARD:
                dstd = {"wg": sc_wgq_my, "wu": sc_wuq_my, "wd": sc_wdq_my}[kind]
            else:
                dstd = {"wg": sc_wgq, "wu": sc_wuq, "wd": sc_wdq}[kind]
            smsp, _ = rWQsp.issue(sl)
            pr.emit("gpsimd", lambda e, o=dstd[roff:roff + h, :], i=oslot[0:h, :]: e.dma_start(out=o, in_=i),
                    waits=((s["sWqh"], s["sWqh"].n),), incs=((smsp, 16),))

    def emit_wq_collectives():
        emit_wq_tasks(len(wq_tasks))
        if not WQ_SHARD:
            return
        rgs = [list(range(n_cores))]
        wqw = ((rWQsp.s[0], 16 * rWQsp.u[0]), (rWQsp.s[1], 16 * rWQsp.u[1]))
        for my_t, full_t in ((sc_wgq_my, sc_wgq), (sc_wuq_my, sc_wuq), (sc_wdq_my, sc_wdq)):
            pr.emit("gpsimd", lambda e, i=my_t, o=full_t: e.collective_compute(
                "AllGather", ALU.bypass, rgs, ins=[i], outs=[o]),
                    waits=wqw, incs=((s["sCol"], 1),))

    def p3():
        def wo_t(et): return arb(o_wo + et * D * 2, D)
        for et in range(ET):
            sl = et % 2
            smw, vw = rW.issue(sl)
            pr.emit("sync", lambda e, o=wst[sl][:, :D],
                    i=wout_d[et * P:(et + 1) * P, :]: e.dma_start(out=o, in_=i),
                    waits=((s["sWo"], et - 1),), incs=((smw, 16),))
            pr.emit("vector", lambda e, o=wo_t(et), i=wst[sl][:, :D]: e.tensor_copy(o, i),
                    waits=((smw, vw),), incs=((s["sWo"], 1),))
        for ch in range(NCH):
            cs = ch % 2
            def gld(et, cs=cs):
                return arb(o_gld + (cs * ET + et) * CW * 2, CW)
            gw = tuple((rGsp.s[eh], 16 * NG) for eh in range(GE))
            vgld = 0
            for et in range(ET):
                smgl, vgld = rGld.issue(cs)
                pr.emit("sync", lambda e, o=gld(et),
                        i=sc_g[et * P:(et + 1) * P, ch * CW:(ch + 1) * CW]: e.dma_start(out=o, in_=i),
                        waits=gw + ((s["sMMo"], (ch - 1) * TS2 * NDH),),
                        incs=((smgl, 16),))
            for t2 in range(TS2):
                ts = ch * TS2 + t2
                for dh in range(NDH):
                    bank = t2 * NDH + dh
                    for et in range(ET):
                        lhs = gld(et)[:, t2 * P:(t2 + 1) * P]
                        rhs = wo_t(et)[:, dh * 512:dh * 512 + DW]
                        pr.emit("tensor", lambda e, o=psv(bank, DW), l_=lhs, r_=rhs,
                                st=(et == 0), sp=(et == ET - 1): e.matmul(o, l_, r_, start=st, stop=sp, **MMKW),
                                waits=((rGld.s[cs], vgld), (s["sWo"], ET),
                                       (s["sX2"], ts - TS2 + 1)),
                                incs=((s["sMMo"], 1),) if et == ET - 1 else ())
                smx3, vx3 = rX.issue(ts % 2)
                pr.emit("sync", lambda e, o=xt[ts % 2][:],
                        i=x_in[ts * P:(ts + 1) * P, :]: e.dma_start(out=o, in_=i),
                        waits=((s["sX2"], ts - 1),), incs=((smx3, 16),))
                mmw = s["sMMo"].n
                for dh in range(NDH):
                    bank = t2 * NDH + dh
                    rt_h = arf(o_rt + dh * 512 * 4, DW)
                    xr_h = xt[ts % 2][:, dh * 512:dh * 512 + DW]
                    pr.emit("vector", lambda e, o=rt_h, i=xr_h, p_=psv(bank, DW): e.scalar_tensor_tensor(
                        o, i, 2.0, p_, ALU.mult, ALU.add),
                            waits=((s["sMMo"], mmw), (smx3, vx3)),
                            incs=((s["sStA"], 1),) if dh == NDH - 1 else ())
                rt = arf(o_rt, D)
                pr.emit("scalar", lambda e, i=rt: e.activation(scr[:, :D], i, AF.Identity,
                                                               accum_out=cols[:, 8:9]),
                        waits=((s["sStA"], s["sStA"].n),))
                pr.emit("scalar", lambda e, i=rt: e.activation(scr[:, :D], i, AF.Square,
                                                               accum_out=cols[:, 9:10]))
                pr.emit("scalar", lambda e: e.mul(cols[:, 10:11], cols[:, 8:9], cst[:, 2:3]))
                pr.emit("scalar", lambda e: e.activation(cols[:, 11:12], cols[:, 10:11], AF.Square))
                pr.emit("scalar", lambda e: e.mul(cols[:, 12:13], cols[:, 9:10], cst[:, 2:3]),
                        incs=((s["sLs1"], 1),))
                pr.emit("vector", lambda e: e.tensor_tensor(cols[:, 13:14], cols[:, 12:13],
                                                            cols[:, 11:12], ALU.subtract),
                        waits=((s["sLs1"], s["sLs1"].n),), incs=((s["sLs2"], 1),))
                pr.emit("scalar", lambda e: e.activation(cols[:, 14:15], cols[:, 13:14], AF.Ln, bias=cst[:, 1:2]),
                        waits=((s["sLs2"], s["sLs2"].n),))
                pr.emit("scalar", lambda e: e.activation(cols[:, 15:16], cols[:, 14:15], AF.Exp, scale=cst[:, 3:4]),
                        incs=((s["sLs3"], 1),))
                x2b = arf(o_x2b, D)
                x2q = arb(o_x2q, D)
                pr.emit("vector", lambda e, o=x2b, i=rt: e.tensor_scalar(
                    o, i, cols[:, 10:11], cols[:, 15:16], ALU.subtract, ALU.mult),
                        waits=((s["sLs3"], s["sLs3"].n), (s["sQd"], s["sQd"].n),
                               (s["sX2sp"], s["sX2sp"].n)), incs=((s["sStB"], 1),))
                pr.emit("gpsimd", lambda e, o=sc_x2[ts * P:(ts + 1) * P, :], i=x2b: e.dma_start(out=o, in_=i),
                        waits=((s["sStB"], s["sStB"].n),), incs=((s["sX2sp"], 16),))
                quant(x2b, x2q, D, q3,
                      waits_first=((s["sStB"], s["sStB"].n),),
                      waits_dve=((s["sEv3"], s["sEv3"].n),),
                      incs_last=((s["sQo"], 1),))
                qd0 = s["sQo"].n
                for dt in range(DT):
                    pp = dt % 2
                    pr.emit("tensor", lambda e, o=psb(4 + pp), i=x2q[:, dt * P:(dt + 1) * P]: e.transpose(
                        o, i, ident[:]),
                            waits=((s["sQo"], qd0), (s["sEv3"], s["sTr3"].n - 1)),
                            incs=((s["sTr3"], 1),))
                    pr.emit("vector", lambda e, o=evt[pp][:], i=psb(4 + pp): e.tensor_copy(o, i),
                            waits=((s["sTr3"], s["sTr3"].n), (rXsp.s[pp], 16 * rXsp.u[pp])),
                            incs=((s["sEv3"], 1),))
                    smxs, _ = rXsp.issue(pp)
                    pr.emit("gpsimd", lambda e, o=sc_xqT[dt * P:(dt + 1) * P, ts * P:(ts + 1) * P],
                            i=evt[pp][:]: e.dma_start(out=o, in_=i),
                            waits=((s["sEv3"], s["sEv3"].n),), incs=((smxs, 16),))
                pr.emit("vector", lambda e: e.nop(), incs=((s["sX2"], 1),))

    def pf():
        def wgq_t(dt): return arb(o_wgq + dt * I * 2, I)
        def wuq_t(dt): return arb(o_wuq + dt * I * 2, I)
        def wdq_t(kt): return arb(o_wdq + kt * D * 2, D)
        qpf0 = qblock(o_hq, D)
        wq_rdy = (((s["sCol"], s["sCol"].n),) if WQ_SHARD else
                  ((rWQsp.s[0], 16 * rWQsp.u[0]), (rWQsp.s[1], 16 * rWQsp.u[1])))
        for wi, (wspl, dstf) in enumerate(((sc_wgq, wgq_t), (sc_wuq, wuq_t))):
            for it in range(IT):
                li = wi * IT + it
                sl = li % 2
                qslot = qo[:, sl * QOS: sl * QOS + D]
                smw, vw = rWQ2.issue(sl)
                pr.emit("sync", lambda e, o=qslot,
                        i=wspl[it * P:(it + 1) * P, :]: e.dma_start(out=o, in_=i),
                        waits=wq_rdy + ((s["sEvH"], (li - 1) * DT),),
                        incs=((smw, 16),))
                for dt in range(DT):
                    pp = dt % 2
                    pr.emit("tensor", lambda e, o=psb(4 + pp), i=qslot[:, dt * P:(dt + 1) * P]: e.transpose(
                        o, i, ident[:]),
                            waits=((smw, vw), (s["sEvH"], s["sTrH"].n - 1)),
                            incs=((s["sTrH"], 1),))
                    pr.emit("vector", lambda e, o=dstf(dt)[:, it * P:(it + 1) * P],
                            i=psb(4 + pp): e.tensor_copy(o, i),
                            waits=((s["sTrH"], s["sTrH"].n),), incs=((s["sEvH"], 1),))
        pr.barrier(s["bar"])
        if debug:
            pr.emit("gpsimd", lambda e, i=wgq_t(0): e.dma_start(out=dr["dbg_wgq"], in_=i),
                    incs=((s["sDbg"], 16),))
        for ts in range(TS):
            sl = ts % 2
            src = sc_xqT[:, ts * P:(ts + 1) * P].rearrange("(dt p) c -> dt p c", p=P).transpose([1, 0, 2])
            dst3 = xqt[sl][:].rearrange("p (dt c) -> p dt c", c=P)
            smxq, vxq = rXq.issue(sl)
            pr.emit("sync", lambda e, o=dst3, i=src: e.dma_start(out=o, in_=i),
                    waits=((rXsp.s[0], 16 * TS * DT // 2), (rXsp.s[1], 16 * TS * DT // 2),
                           (s["sMMg"], (ts - 1) * 2 * NNC)),
                    incs=((smxq, 16),))
            for ni, (noff, nwd) in enumerate(ncols):
                nig = ts * NNC + ni
                rot3 = nig % 3
                bg = (0, 1, 6)[rot3]
                bu = (2, 3, 7)[rot3]
                for wq_t, bank in ((wgq_t, bg), (wuq_t, bu)):
                    for k in range(DT):
                        lhs = xqt[sl][:, k * P:(k + 1) * P]
                        rhs = wq_t(k)[:, noff:noff + nwd]
                        mmwts = (((smxq, vxq), (s["sSigF"], 3 * (nig - 2)))
                                 if (bank == bg and k == 0) else ())
                        pr.emit("tensor", lambda e, o=psv(bank, nwd), l_=lhs, r_=rhs,
                                st=(k == 0), sp=(k == DT - 1): e.matmul(o, l_, r_, start=st, stop=sp, **MMKW),
                                waits=mmwts,
                                incs=((s["sMMg"], 1),) if k == DT - 1 else ())
                mmw = s["sMMg"].n
                spE = wst[0][:, 0:nwd]
                spD_ = wst[0][:, 512:512 + nwd].bitcast(BF16)[:, 0:nwd]
                g_s = wst[1][:, 0:nwd].bitcast(BF16)[:, 0:nwd]
                u_s = wst[1][:, 512:512 + nwd].bitcast(BF16)[:, 0:nwd]
                hhb = wst[0][:, 1024:1024 + nwd]
                pr.emit("scalar", lambda e, o=g_s, i=psv(bg, nwd): e.copy(o, i),
                        waits=((s["sMMg"], mmw), (s["sSigF"], 3 * nig)))
                pr.emit("scalar", lambda e, o=u_s, i=psv(bu, nwd): e.copy(o, i),
                        incs=((s["sSigF"], 1),))
                pr.emit("scalar", lambda e, o=spE, i=g_s: e.activation(o, i, AF.Tanh, scale=cst[:, 6:7]),
                        incs=((s["sSigF"], 1),))
                pr.emit("vector", lambda e, o=spD_, i=spE: e.tensor_scalar(
                    o, i, 0.5, 0.5, ALU.mult, ALU.add),
                        waits=((s["sSigF"], 3 * nig + 2),))
                pr.emit("vector", lambda e, o=spD_, a=g_s, b=spD_: e.tensor_mul(o, a, b))
                pr.emit("vector", lambda e, o=hhb, a=spD_, b=u_s: e.tensor_mul(o, a, b),
                        incs=((s["sSigF"], 1),))
                quant(hhb, arb(o_hq + noff * 2, nwd), nwd, qf,
                      waits_first=((s["sSigF"], s["sSigF"].n),),
                      waits_dve=((s["sEvH"], s["sEvH"].n),),
                      incs_last=((s["sHq"], 1),))
            hq0 = s["sHq"].n
            if debug and ts == 0:
                pr.emit("gpsimd", lambda e, i=arb(o_hq, I): e.dma_start(out=dr["dbg_hq"], in_=i),
                        waits=((s["sHq"], hq0),), incs=((s["sDbg"], 16),))
            for kt in range(IT):
                pp = kt % 2
                pr.emit("tensor", lambda e, o=psb(4 + pp), i=arb(o_hq + kt * P * 2, P): e.transpose(
                    o, i, ident[:]),
                        waits=((s["sHq"], hq0), (s["sEvH"], s["sTrH"].n - 1)),
                        incs=((s["sTrH"], 1),))
                pr.emit("vector", lambda e, o=arb(o_hqT + kt * P * 2, P), i=psb(4 + pp): e.tensor_copy(o, i),
                        waits=((s["sTrH"], s["sTrH"].n), (s["sHsp"], 16 * ts)),
                        incs=((s["sEvH"], 1),))
            dst = sc_hqT[:, ts * P:(ts + 1) * P].rearrange("(kt p) c -> kt p c", p=P).transpose([1, 0, 2])
            src_h = arb(o_hqT, IT * P).rearrange("p (kt c) -> p kt c", c=P)
            pr.emit("gpsimd", lambda e, o=dst, i=src_h: e.dma_start(out=o, in_=i),
                    waits=((s["sEvH"], s["sEvH"].n),), incs=((s["sHsp"], 16),))
        pr.barrier(s["bar"])
        for dt in range(DT):
            sl = dt % 2
            qslot = qo[:, sl * QOS: sl * QOS + I]
            smw, vw = rWQ2.issue(sl)
            evb = s["sEvH"].n - IT if dt >= 2 else 0
            pr.emit("sync", lambda e, o=qslot,
                    i=sc_wdq[dt * P:(dt + 1) * P, :]: e.dma_start(out=o, in_=i),
                    waits=((s["sEvH"], evb),), incs=((smw, 16),))
            for kt in range(IT):
                pp = kt % 2
                pr.emit("tensor", lambda e, o=psb(4 + pp), i=qslot[:, kt * P:(kt + 1) * P]: e.transpose(
                    o, i, ident[:]),
                        waits=((smw, vw), (s["sEvH"], s["sTrH"].n - 1)),
                        incs=((s["sTrH"], 1),))
                pr.emit("vector", lambda e, o=wdq_t(kt)[:, dt * P:(dt + 1) * P], i=psb(4 + pp): e.tensor_copy(o, i),
                        waits=((s["sTrH"], s["sTrH"].n),), incs=((s["sEvH"], 1),))
        pr.emit("vector", lambda e: e.nop(), incs=((s["sWqT"], 1),))
        wq_all = s["sWq"].n
        trh_all = s["sTrH"].n
        for ts in range(TS):
            sl = ts % 2
            src = sc_hqT[:, ts * P:(ts + 1) * P].rearrange("(kt p) c -> kt p c", p=P).transpose([1, 0, 2])
            dsth = arb(o_hqr + sl * I * 2, I).rearrange("p (kt c) -> p kt c", c=P)
            smhr, vhr = rHr.issue(sl)
            pr.emit("sync", lambda e, o=dsth, i=src: e.dma_start(out=o, in_=i),
                    waits=((s["sHsp"], 16 * TS), (s["sMMd"], (ts - 1) * NDH)),
                    incs=((smhr, 16),))
            smx2, vx2 = rX2r.issue(sl)
            pr.emit("sync", lambda e, o=arf(o_x2r + sl * D * 4, D),
                    i=sc_x2[ts * P:(ts + 1) * P, :]: e.dma_start(out=o, in_=i),
                    waits=((s["sX2sp"], 16 * TS), (s["sRes"], (ts - 1) * NDH),
                           (s["sWq"], wq_all), (s["sTrH"], trh_all)),
                    incs=((smx2, 16),))
            for dh in range(NDH):
                bank = (ts % 2) * 2 + dh
                for kt in range(IT):
                    lhs = arb(o_hqr + sl * I * 2 + kt * P * 2, P)
                    rhs = wdq_t(kt)[:, dh * 512:dh * 512 + DW]
                    mmwts = ()
                    if kt == 0:
                        mmwts = ((s["sRes"], (ts - 2) * NDH + dh + 1),)
                        if dh == 0:
                            mmwts += ((smhr, vhr), (s["sWqT"], 1))
                    pr.emit("tensor", lambda e, o=psv(bank, DW), l_=lhs, r_=rhs,
                            st=(kt == 0), sp=(kt == IT - 1): e.matmul(o, l_, r_, start=st, stop=sp, **MMKW),
                            waits=mmwts,
                            incs=((s["sMMd"], 1),) if kt == IT - 1 else ())
            mmw = s["sMMd"].n
            rt2 = arf(o_rt2, D)
            for dh in range(NDH):
                bank = (ts % 2) * 2 + dh
                rt2h = arf(o_rt2 + dh * 512 * 4, DW)
                x2h = arf(o_x2r + sl * D * 4 + dh * 512 * 4, DW)
                pr.emit("vector", lambda e, o=rt2h, a=x2h, p_=psv(bank, DW): e.tensor_tensor(
                    o, a, p_, ALU.add),
                        waits=((s["sMMd"], mmw), (smx2, vx2)),
                        incs=((s["sRes"], 1),))
            pr.emit("scalar", lambda e, i=rt2: e.activation(scr[:, :D], i, AF.Identity,
                                                            accum_out=cols[:, 16:17]),
                    waits=((s["sRes"], s["sRes"].n),))
            pr.emit("scalar", lambda e, i=rt2: e.activation(scr[:, :D], i, AF.Square,
                                                            accum_out=cols[:, 17:18]))
            pr.emit("scalar", lambda e: e.mul(cols[:, 18:19], cols[:, 16:17], cst[:, 2:3]))
            pr.emit("scalar", lambda e: e.activation(cols[:, 19:20], cols[:, 18:19], AF.Square))
            pr.emit("scalar", lambda e: e.mul(cols[:, 20:21], cols[:, 17:18], cst[:, 2:3]),
                    incs=((s["sLs1"], 1),))
            pr.emit("vector", lambda e: e.tensor_tensor(cols[:, 21:22], cols[:, 20:21],
                                                        cols[:, 19:20], ALU.subtract),
                    waits=((s["sLs1"], s["sLs1"].n),), incs=((s["sLs2"], 1),))
            pr.emit("scalar", lambda e: e.activation(cols[:, 22:23], cols[:, 21:22], AF.Ln, bias=cst[:, 1:2]),
                    waits=((s["sLs2"], s["sLs2"].n),))
            pr.emit("scalar", lambda e: e.activation(cols[:, 23:24], cols[:, 22:23], AF.Exp, scale=cst[:, 3:4]),
                    incs=((s["sLs3"], 1),))
            ot = arf(o_outt + (ts % 2) * D * 4, D)
            pr.emit("vector", lambda e, o=ot, i=rt2: e.tensor_scalar(
                o, i, cols[:, 18:19], cols[:, 23:24], ALU.subtract, ALU.mult),
                    waits=((s["sLs3"], s["sLs3"].n), (rOut.s[ts % 2], 16 * rOut.u[ts % 2])),
                    incs=((s["sStA"], 1),))
            smo, _ = rOut.issue(ts % 2)
            pr.emit("gpsimd", lambda e, o=out_d[ts * P:(ts + 1) * P, :], i=ot: e.dma_start(out=o, in_=i),
                    waits=((s["sStA"], s["sStA"].n),), incs=((smo, 16),))
        pr.emit("gpsimd", None, waits=((rOut.s[0], 16 * ((TS + 1) // 2)), (rOut.s[1], 16 * (TS // 2))))

    def _terminate():
        zt = arf(0, D)
        pr.emit("vector", lambda e, o=zt: e.memset(o, 0.0), incs=((s["sStB"], 1),))
        for ts in range(TS):
            smo, _ = rOut.issue(ts % 2)
            pr.emit("gpsimd", lambda e, o=out_d[ts * P:(ts + 1) * P, :], i=zt: e.dma_start(out=o, in_=i),
                    waits=((s["sStB"], s["sStB"].n),), incs=((smo, 16),))
        pr.emit("gpsimd", None, waits=((rOut.s[0], 16 * rOut.u[0]), (rOut.s[1], 16 * rOut.u[1])))

    phases = [("p0", p0), ("p1a", p1a), ("p1bx", lambda: p1b_half(0)),
              ("p1bz", lambda: p1b_half(1)), ("p1c", p1c)]
    if stop_after in ("p2b", "p2c", "p2a0", "p2a1", "p2b1", "p2b2", "p2b3", "p2b4", "p2b0", "p2n1", "p2n2", "p2n3", "p2n4", "p2n5"):
        phases.append((stop_after, lambda: p2(until=stop_after)))
    else:
        phases += [("p2", p2), ("p3", p3), ("pf", pf)]
    for nm, fn in phases:
        fn()
        pr.barrier(s["bar"])
        if stop_after == nm:
            _terminate()
            break

    with nc.Block() as block:
        def runner(lst):
            def f(e):
                for fn, waits, incs in lst:
                    for sm, v in waits:
                        e.wait_ge(sm.h, v)
                    if fn is None:
                        continue
                    ins = fn(e)
                    for sm, k in incs:
                        if k > 0:
                            ins.then_inc(sm.h, k)
            return f
        block.sync(runner(pr.ops["sync"]))
        block.gpsimd(runner(pr.ops["gpsimd"]))
        block.tensor(runner(pr.ops["tensor"]))
        block.scalar(runner(pr.ops["scalar"]))
        block.vector(runner(pr.ops["vector"]))
    es.close()
    return nc




CFG = dict(T=2048, D=1024, E=2048, N=16, R=64, I=2816)
N_CORES = 8
_NC_CACHE = {}


def _get_nc():
    if "nc" not in _NC_CACHE:
        _NC_CACHE["nc"] = build(CFG, debug=False, n_cores=N_CORES)
    return _NC_CACHE["nc"]


def kernel(L0_attn_hidden_states, L0_attn_norm_w, L0_attn_W_in, L0_attn_conv_w,
           L0_attn_W_x, L0_attn_W_dt, L0_attn_b_dt, L0_attn_A_log,
           L0_attn_D_param, L0_attn_W_out, L0_ffn_w_gate, L0_ffn_w_up,
           L0_ffn_w_down):
    import ml_dtypes
    from concourse.bass_utils import run_bass_kernel_spmd
    E, Pn = CFG["E"], P
    ET = E // Pn
    c = np.ascontiguousarray

    def featmajor(a, w):
        return c(np.asarray(a, np.float32).reshape(ET, Pn, w).transpose(1, 0, 2)
                 .reshape(Pn, ET * w))

    DT_, ET2 = CFG["D"] // Pn, 2 * CFG["E"] // Pn
    common = {
        "W_in": c(np.asarray(L0_attn_W_in, np.float32)
                  .reshape(DT_, Pn, ET2, Pn).transpose(0, 2, 1, 3)
                  .reshape(DT_ * ET2 * Pn, Pn)),
        "convw": featmajor(np.asarray(L0_attn_conv_w, np.float32).reshape(E, 4), 4),
        "W_x": featmajor(L0_attn_W_x, 96),
        "Wdt_ext": c(np.concatenate(
            [np.asarray(L0_attn_W_dt, np.float32),
             np.asarray(L0_attn_b_dt, np.float32)[None, :]], axis=0)),
        "A_log": featmajor(L0_attn_A_log, CFG["N"]),
        "D_param": c(np.asarray(L0_attn_D_param, np.float32).reshape(ET, Pn).T),
        "norm_w": c(np.asarray(L0_attn_norm_w, np.float32).reshape(1, CFG["D"])),
        "ident": np.eye(Pn, dtype=ml_dtypes.bfloat16),
        "W_out": c(np.asarray(L0_attn_W_out, np.float32)),
    }
    wg_full = np.asarray(L0_ffn_w_gate, np.float32)
    wu_full = np.asarray(L0_ffn_w_up, np.float32)
    wd_full = np.asarray(L0_ffn_w_down, np.float32)
    myi = CFG["I"] // N_CORES
    myd = CFG["D"] // N_CORES
    xs = np.asarray(L0_attn_hidden_states, np.float32)
    in_maps = [dict(common, x=c(xs[b]),
                    wg=c(wg_full[b * myi:(b + 1) * myi]),
                    wu=c(wu_full[b * myi:(b + 1) * myi]),
                    wd=c(wd_full[b * myd:(b + 1) * myd]))
               for b in range(xs.shape[0])]
    nc = _get_nc()
    res = run_bass_kernel_spmd(nc, in_maps, core_ids=list(range(len(in_maps))))
    return np.stack([r["out"] for r in res.results], axis=0)
